# version 17
# baseline (speedup 1.0000x reference)
import contextlib
import math
from collections import defaultdict

import numpy as np
import ml_dtypes

import concourse.bass as bass
import concourse.mybir as mybir
from concourse.bass_utils import run_bass_kernel_spmd

F32 = mybir.dt.float32
BF16 = mybir.dt.bfloat16
I32 = mybir.dt.int32
AF = mybir.ActivationFunctionType
ALU = mybir.AluOpType
AX = mybir.AxisListType

B = 8
T = 4096
E = 2048
HD = 16
NOPE, RD, VD = 128, 64, 128
QH = NOPE + RD
LORA = 512
INTER = 5504
NIT = INTER // 128
NET = E // 128
SCALE = QH ** -0.5
RMS_EPS = 1e-6
LN_EPS = 1e-5
ROPE_THETA = 10000.0
NEGBIG = -30000.0

ENGINES = ("tensor", "vector", "scalar", "gpsimd", "sync")

DEBUG = False
PHASES = ("p0a", "p0c", "p0b", "p1", "p2", "fw", "f")


class Prog:

    def __init__(self, nc):
        self.nc = nc
        self.ctx = contextlib.ExitStack()
        self.sems = {}
        self.count = defaultdict(int)
        self.ops = defaultdict(list)
        self.last_wait = defaultdict(dict)

    def sem(self, name):
        if name not in self.sems:
            self.sems[name] = self.ctx.enter_context(self.nc.semaphore(name))
        return self.sems[name]

    def ev(self, sem_name):
        return (sem_name, self.count[sem_name])

    def emit(self, eng, fn, waits=(), inc=None, amt=1):
        after = None
        if inc is not None:
            self.sem(inc)
            self.count[inc] += amt
            after = (inc, self.count[inc])
        w = []
        for x in waits:
            if x is None:
                continue
            (s, v) = x
            if v <= 0:
                continue
            self.sem(s)
            assert v <= self.count[s], f"wait on future event {s}={v} (@{self.count[s]})"
            if self.last_wait[eng].get(s, 0) < v:
                self.last_wait[eng][s] = v
                w.append((s, v))
        self.ops[eng].append((fn, w, (inc, amt) if inc else None))
        return after

    def dma(self, queue, out, in_, waits=(), sem=None):
        sem = sem or ("d_" + queue)
        return self.emit(queue, lambda e, o=out, i=in_: e.dma_start(o, i),
                         waits=waits, inc=sem, amt=16)

    def flush(self):
        with self.nc.Block() as block:
            for eng in ENGINES:
                ops = self.ops[eng]
                if not ops:
                    continue

                def body(e, ops=list(ops)):
                    for (fn, w, inc) in ops:
                        for (s, v) in w:
                            e.wait_ge(self.sems[s], v)
                        inst = fn(e)
                        if inc is not None:
                            inst.then_inc(self.sems[inc[0]], inc[1])

                getattr(block, eng)(body)
                self.ops[eng] = []

    def barrier(self):
        evs = [(s, c) for s, c in self.count.items() if c > 0]
        for eng in ENGINES:
            w = []
            for (s, v) in evs:
                if self.last_wait[eng].get(s, 0) < v:
                    self.last_wait[eng][s] = v
                    w.append((s, v))
            if w:
                self.ops[eng].append(((lambda e: None), w, None))



def make_consts():
    c = {}
    ident = np.eye(128, dtype=np.float32)
    c["identb"] = ident
    c["identf"] = ident
    c["negI"] = NEGBIG * ident
    c["onesb"] = np.ones((128, 1), dtype=np.float32)
    c["onesf_col"] = np.ones((128, 1), dtype=np.float32)
    c["onesf_row"] = np.ones((1, 128), dtype=np.float32)
    inv = ROPE_THETA ** (-np.arange(0, RD, 2, dtype=np.float32) / RD)
    ang = np.arange(T, dtype=np.float32)[:, None] * inv[None, :]
    cosT = np.cos(ang).T.astype(np.float32)
    sinT = np.sin(ang).T.astype(np.float32)
    c["cos2"] = np.concatenate([cosT, cosT], 0)
    c["sin2"] = np.concatenate([sinT, sinT], 0)
    r = np.arange(128)[:, None]
    col = np.arange(512)[None, :]
    s1 = np.stack([(col > 128 * v + r) for v in range(4)], 0)
    s2 = np.stack([(col < 128 * d + r) for d in range(4)], 0)
    c["step1"] = s1.astype(np.float32).reshape(4 * 128, 512)
    c["step2"] = s2.astype(np.float32).reshape(4 * 128, 512)
    return c



def build_program(debug=False, phases=PHASES):
    nc = bass.Bass("TRN2", target_bir_lowering=False)

    H = {}
    def din(name, shape, dt):
        H[name] = nc.dram_tensor(name, shape, dt, kind="ExternalInput").ap()
        return H[name]

    x_h = din("x", [T, E], F32)
    wq_h = din("wq", [E, HD * QH], F32)
    wkva_h = din("wkva", [E, LORA + RD], F32)
    wkvb_h = din("wkvb", [LORA, HD * (NOPE + VD)], F32)
    wo_h = din("wo", [HD * VD, E], F32)
    gamma_h = din("gamma", [128, 4], F32)
    lut_h = din("lut", [1, 16], F32)
    gidx_h = din("gidx", [INTER, E], I32)
    uidx_h = din("uidx", [INTER, E], I32)
    didx_h = din("didx", [E, INTER], I32)
    identb_h = din("identb", [128, 128], F32)
    identf_h = din("identf", [128, 128], F32)
    negI_h = din("negI", [128, 128], F32)
    onesb_h = din("onesb", [128, 1], F32)
    onesf_col_h = din("onesf_col", [128, 1], F32)
    onesf_row_h = din("onesf_row", [1, 128], F32)
    cos2_h = din("cos2", [64, T], F32)
    sin2_h = din("sin2", [64, T], F32)
    step1_h = din("step1", [4 * 128, 512], F32)
    step2_h = din("step2", [4 * 128, 512], F32)

    out_h = nc.dram_tensor("out", [T, E], F32, kind="ExternalOutput").ap()

    q_flat = nc.dram_tensor("q_flat", [HD * QH, T], BF16, kind="Internal").ap()
    ks1_d = nc.dram_tensor("ks1_d", [HD * 128, T], BF16, kind="Internal").ap()
    vtok_d = nc.dram_tensor("vtok_d", [HD * 128, T], BF16, kind="Internal").ap()
    o_d = nc.dram_tensor("o_d", [HD * VD, T], BF16, kind="Internal").ap()
    y_d = nc.dram_tensor("y_d", [T, E], F32, kind="Internal").ap()
    yT_d = nc.dram_tensor("yT_d", [8, NET, 128, 512], BF16, kind="Internal").ap()
    wgT_d = nc.dram_tensor("wgT_d", [NIT, 128, E], BF16, kind="Internal").ap()
    wuT_d = nc.dram_tensor("wuT_d", [NIT, 128, E], BF16, kind="Internal").ap()
    wdT_d = nc.dram_tensor("wdT_d", [NET, 128, INTER], BF16, kind="Internal").ap()

    dbg = {}
    if debug:
        def dout(name, shape, dt):
            dbg[name] = nc.dram_tensor("dbg_" + name, shape, dt, kind="ExternalOutput").ap()
        dout("ckv", [LORA, T], F32)
        dout("kpe", [66, T], F32)
        dout("q", [HD * QH, T], F32)
        dout("ks1", [HD * 128, T], F32)
        dout("vtok", [HD * 128, T], F32)
        dout("o", [HD * VD, T], F32)
        dout("y", [T, E], F32)
        dout("wgT", [NIT, 128, E], F32)
        dout("wdT", [NET, 128, INTER], F32)
        dout("z", [T, E], F32)

    P = Prog(nc)
    stk = contextlib.ExitStack()

    identb = stk.enter_context(nc.sbuf_tensor("s_identb", [128, 128], BF16))
    identf = stk.enter_context(nc.sbuf_tensor("s_identf", [128, 128], F32))
    negI = stk.enter_context(nc.sbuf_tensor("s_negI", [128, 128], BF16))
    onesb = stk.enter_context(nc.sbuf_tensor("s_onesb", [128, 1], BF16))
    onesf_col = stk.enter_context(nc.sbuf_tensor("s_onesf_col", [128, 1], F32))
    onesf_row = stk.enter_context(nc.sbuf_tensor("s_onesf_row", [1, 128], F32))
    step1 = stk.enter_context(nc.sbuf_tensor("s_step1", [128, 4 * 512], BF16))
    step2 = stk.enter_context(nc.sbuf_tensor("s_step2", [128, 4 * 512], BF16))
    ks2 = stk.enter_context(nc.sbuf_tensor("ks2", [66, T], BF16))
    gamma = stk.enter_context(nc.sbuf_tensor("s_gamma", [128, 4], F32))
    lut_col = stk.enter_context(nc.sbuf_tensor("s_lut_col", [128, 16], F32))

    for (t, h) in [(identf, identf_h), (onesf_col, onesf_col_h),
                   (onesf_row, onesf_row_h), (gamma, gamma_h)]:
        P.dma("sync", t[:, :], h)
    for (t, h) in [(identb, identb_h), (negI, negI_h), (onesb, onesb_h)]:
        P.dma("gpsimd", t[:, :], h)
    P.dma("gpsimd", step1[:, :].rearrange("p (v c) -> p v c", c=512),
          step1_h.rearrange("(v p) c -> p v c", p=128))
    P.dma("gpsimd", step2[:, :].rearrange("p (v c) -> p v c", c=512),
          step2_h.rearrange("(v p) c -> p v c", p=128))
    P.dma("gpsimd", lut_col[:, :], lut_h.partition_broadcast(128))
    P.emit("vector", lambda e: e.memset(ks2[64:66, :], 1.0), inc="ve")
    const_pool = stk.enter_context(nc.sbuf_tensor("s_constpool", [128, 8], F32))
    for j, val in enumerate([RMS_EPS, LN_EPS, 1e-20]):
        P.emit("vector", lambda e, j=j, val=val: e.memset(const_pool[:, j:j + 1], val),
               inc="ve")
        nc.const_aps.register(F32, val, const_pool[:, j:j + 1])
    P.barrier()

    ckv_stk = contextlib.ExitStack()
    ckv_n = [ckv_stk.enter_context(nc.sbuf_tensor(f"ckvn{i}", [128, T], BF16))
             for i in range(4)]
    cs_stk = contextlib.ExitStack()
    cos2 = cs_stk.enter_context(nc.sbuf_tensor("s_cos2", [64, T], BF16))
    sin2 = cs_stk.enter_context(nc.sbuf_tensor("s_sin2", [64, T], BF16))
    P.dma("gpsimd", cos2[:, :], cos2_h)
    P.dma("gpsimd", sin2[:, :], sin2_h)
    cs_ev = P.ev("d_gpsimd")
    wqb = [cs_stk.enter_context(nc.sbuf_tensor(f"wqb{i}", [128, HD * QH], BF16))
           for i in range(14)]
    wq_ev = None
    for i in range(14):
        wq_ev = P.dma("gpsimd", wqb[i][:, :], wq_h[i * 128:(i + 1) * 128, :])

    def emit_x_chunk_transposes(tc, x_in, xTb, pst, pst_free, xin_free, xT_free):
        t0 = tc * 512
        x_ev = [None] * 4
        xT_ready = None

        def loadx(ts):
            x_ev[ts] = P.dma("gpsimd", x_in[ts % 2][:, :],
                             x_h[t0 + ts * 128: t0 + (ts + 1) * 128, :],
                             waits=[xin_free[ts % 2]], sem=f"dxi{ts % 2}")

        loadx(0)
        loadx(1)
        for bat in range(8):
            ts, half = divmod(bat, 2)
            pb = pst[bat % 2]
            ev_t = None
            for j in range(8):
                et = half * 8 + j
                w = [pst_free[bat % 2], x_ev[ts], xT_free] if j == 0 else []
                ev_t = P.emit("tensor",
                              lambda e, pb=pb, xx=x_in[ts % 2], et=et, j=j:
                              e.transpose(pb[:, j * 128:(j + 1) * 128],
                                          xx[:, et * 128:(et + 1) * 128],
                                          identb[:, :]),
                              waits=w, inc="pe" if j == 7 else None)
            out_ap = xTb[:, :].rearrange("p (et f) -> p et f", f=512)[
                :, half * 8:(half + 1) * 8, ts * 128:(ts + 1) * 128]
            in_ap = pb[:, :].rearrange("p (j f) -> p j f", f=128)
            ev_e = P.emit("scalar",
                          lambda e, oa=out_ap, ia=in_ap: e.copy(oa, ia),
                          waits=[ev_t], inc="se")
            pst_free[bat % 2] = ev_e
            xT_ready = ev_e
            if half == 1:
                xin_free[ts % 2] = P.ev("pe")
                if ts + 2 < 4:
                    loadx(ts + 2)
        return xT_ready, x_ev[3]

    if "p0a" in phases:
        with contextlib.ExitStack() as ph:
            wkva = [ph.enter_context(nc.sbuf_tensor(f"wkva{i}", [128, LORA + RD], BF16))
                    for i in range(NET)]
            x_in = [ph.enter_context(nc.sbuf_tensor(f"xin{i}", [128, E], BF16))
                    for i in range(2)]
            xTb = ph.enter_context(nc.sbuf_tensor("xTb", [128, 8192], BF16))
            sq = ph.enter_context(nc.sbuf_tensor("sq", [128, 512], BF16))
            rstdb_s = ph.enter_context(nc.sbuf_tensor("rstdb_s", [128, 512], BF16))
            ropescr_a = ph.enter_context(nc.sbuf_tensor("ropescr_a", [32, 512], BF16))
            ropescr_b = ph.enter_context(nc.sbuf_tensor("ropescr_b", [32, 512], BF16))
            rmsrow = ph.enter_context(nc.sbuf_tensor("rmsrow", [1, 512], F32))

            pst = [ph.enter_context(nc.psum_tensor(f"pst{i}", [128, 1024], BF16))
                   for i in range(2)]
            psK4 = [ph.enter_context(nc.psum_tensor(f"psK4_{i}", [128, 512], F32))
                    for i in range(4)]
            psR = ph.enter_context(nc.psum_tensor("psR", [128, 512], F32))
            psBC = ph.enter_context(nc.psum_tensor("psBC", [128, 512], F32))

            wk_ev = None
            for i in range(NET):
                wk_ev = P.dma("gpsimd", wkva[i][:, :],
                              wkva_h[i * 128:(i + 1) * 128, :])

            xin_free = [None] * 2
            xT_free = None
            pst_free = [None, None]
            psK_free = [None] * 4
            psR_free = None
            psBC_free = None
            sq_free = None
            for tc in range(8):
                xT_ready, x_ev = emit_x_chunk_transposes(
                    tc, x_in, xTb, pst, pst_free, xin_free, xT_free)
                ev_r = None
                kv_mm = [None] * 4
                for mc in range(4):
                    ps = psK4[mc]
                    ev_mm = None
                    for et in range(NET):
                        w = [psK_free[mc], xT_ready] if et == 0 else []
                        ev_mm = P.emit("tensor",
                                       lambda e, ps=ps, et=et, mc=mc:
                                       e.matmul(ps[:, 0:512],
                                                wkva[et][:, mc * 128: mc * 128 + 128],
                                                xTb[:, et * 512:(et + 1) * 512],
                                                start=(et == 0), stop=(et == NET - 1)),
                                       waits=w, inc="pe" if et == NET - 1 else None)
                    kv_mm[mc] = ev_mm
                    ev_sq = P.emit("scalar",
                                   lambda e, ps=ps: e.activation(sq[:, :], ps[:, 0:512],
                                                                 AF.Square),
                                   waits=[ev_mm, sq_free], inc="se")
                    ev_r = P.emit("tensor",
                                  lambda e, mc=mc:
                                  e.matmul(psR[0:1, 0:512], onesb[:, :],
                                           sq[:, :], start=(mc == 0), stop=(mc == 3)),
                                  waits=[ev_sq] + ([psR_free] if mc == 0 else []),
                                  inc="pe")
                    sq_free = ev_r
                ev_ln = P.emit("scalar",
                               lambda e: e.activation(rmsrow[:, :], psR[0:1, 0:512],
                                                      AF.Ln, bias=RMS_EPS, scale=1.0 / LORA),
                               waits=[ev_r], inc="se")
                psR_free = ev_ln
                ev_rstd = P.emit("scalar",
                                 lambda e: e.activation(rmsrow[:, :], rmsrow[:, :],
                                                        AF.Exp, scale=-0.5),
                                 waits=[], inc="se")
                ev_bc = P.emit("tensor",
                               lambda e: e.matmul(psBC[:, 0:512], onesf_row[:, :],
                                                  rmsrow[:, :], start=True, stop=True),
                               waits=[ev_rstd, psBC_free], inc="pe")
                ev_rb = P.emit("scalar", lambda e: e.copy(rstdb_s[:, :], psBC[:, 0:512]),
                               waits=[ev_bc, psBC_free], inc="se")
                psBC_free = ev_rb
                ev_cn = None
                for mc in range(4):
                    ev_cn = P.emit("vector",
                                   lambda e, mc=mc, tc=tc:
                                   e.scalar_tensor_tensor(
                                       ckv_n[mc][:, tc * 512:(tc + 1) * 512],
                                       psK4[mc][:, 0:512], gamma[:, mc:mc + 1],
                                       rstdb_s[:, :],
                                       ALU.mult, ALU.mult),
                                   waits=[ev_rb] if mc == 0 else [],
                                   inc="ve")
                    psK_free[mc] = ev_cn
                ps = psK4[0]
                ev_mm = None
                for et in range(NET):
                    w = [psK_free[0], xT_ready] if et == 0 else []
                    ev_mm = P.emit("tensor",
                                   lambda e, ps=ps, et=et:
                                   e.matmul(ps[0:64, 0:512],
                                            wkva[et][:, 512:576],
                                            xTb[:, et * 512:(et + 1) * 512],
                                            start=(et == 0), stop=(et == NET - 1)),
                                   waits=w, inc="pe" if et == NET - 1 else None)
                w = [ev_mm, cs_ev]
                t0 = tc * 512
                x1 = ps[0:32, 0:512]
                x2 = ps[32:64, 0:512]
                cS = cos2[0:32, t0:t0 + 512]
                sS = sin2[0:32, t0:t0 + 512]
                a = ropescr_a[:, :]
                b = ropescr_b[:, :]
                P.emit("vector", lambda e, x1=x1, cS=cS, a=a:
                       e.tensor_tensor(a, x1, cS, ALU.mult), waits=w)
                P.emit("vector", lambda e, x2=x2, sS=sS, b=b:
                       e.tensor_tensor(b, x2, sS, ALU.mult))
                P.emit("vector", lambda e, a=a, b=b, t0=t0:
                       e.tensor_tensor(ks2[0:32, t0:t0 + 512], a, b, ALU.subtract))
                P.emit("vector", lambda e, x2=x2, cS=cS, a=a:
                       e.tensor_tensor(a, x2, cS, ALU.mult))
                P.emit("vector", lambda e, x1=x1, sS=sS, b=b:
                       e.tensor_tensor(b, x1, sS, ALU.mult))
                ev_kpe = P.emit("vector", lambda e, a=a, b=b, t0=t0:
                                e.tensor_tensor(ks2[32:64, t0:t0 + 512], a, b, ALU.add),
                                inc="ve")
                psK_free[0] = ev_kpe
                xT_free = P.ev("pe")
            if debug:
                wv = [P.ev("ve"), P.ev("se"), P.ev("pe")]
                for mc in range(4):
                    P.dma("gpsimd", dbg["ckv"][mc * 128:(mc + 1) * 128, :],
                          ckv_n[mc][:, :], waits=wv)
                P.dma("gpsimd", dbg["kpe"], ks2[:, :], waits=wv)
            P.barrier()
            P.flush()

    if "p0b" in phases:
        with contextlib.ExitStack() as ph:
            x_in = [ph.enter_context(nc.sbuf_tensor(f"xinb{i}", [128, E], BF16))
                    for i in range(2)]
            for i in range(14, NET):
                wqb.append(ph.enter_context(
                    nc.sbuf_tensor(f"wqb{i}", [128, HD * QH], BF16)))
            for i in range(14, NET):
                wq_ev = P.dma("gpsimd", wqb[i][:, :], wq_h[i * 128:(i + 1) * 128, :])
            xTb = ph.enter_context(nc.sbuf_tensor("xTbb", [128, 8192], BF16))
            qev = [ph.enter_context(nc.sbuf_tensor(f"qev{i}", [128, 512], BF16))
                   for i in range(2)]
            rs_a = ph.enter_context(nc.sbuf_tensor("rs_a", [64, 512], F32))
            rs_b = ph.enter_context(nc.sbuf_tensor("rs_b", [64, 512], F32))
            ropeo = [ph.enter_context(nc.sbuf_tensor(f"ropeo{i}", [128, 512], BF16))
                     for i in range(2)]
            pst = [ph.enter_context(nc.psum_tensor(f"pstb{i}", [128, 1024], BF16))
                   for i in range(2)]
            psQ = [ph.enter_context(nc.psum_tensor(f"psQ{i}", [128, 512], F32))
                   for i in range(2)]

            xin_free = [None] * 2
            xT_free = None
            pst_free = [None, None]
            psQ_free = [None, None]
            qev_free = [None, None]
            ropeo_free = [None, None]
            for tc in range(8):
                t0 = tc * 512
                xT_ready, x_ev = emit_x_chunk_transposes(
                    tc, x_in, xTb, pst, pst_free, xin_free, xT_free)
                for m in range(24):
                    ps = psQ[m % 2]
                    ev_mm = None
                    for et in range(NET):
                        w = [psQ_free[m % 2], xT_ready, wq_ev] if et == 0 else []
                        ev_mm = P.emit("tensor",
                                       lambda e, ps=ps, et=et, m=m:
                                       e.matmul(ps[:, 0:512],
                                                wqb[et][:, m * 128:(m + 1) * 128],
                                                xTb[:, et * 512:(et + 1) * 512],
                                                start=(et == 0), stop=(et == NET - 1)),
                                       waits=w, inc="pe" if et == NET - 1 else None)
                    if m < 16:
                        ev_c = P.emit("scalar",
                                      lambda e, ps=ps, m=m, sl=m % 2:
                                      e.copy(qev[sl][:, :], ps[:, 0:512]),
                                      waits=[ev_mm, qev_free[m % 2]], inc="se")
                        psQ_free[m % 2] = ev_c
                        ev_sp = P.dma("sync", q_flat[m * QH: m * QH + 128, t0:t0 + 512],
                                      qev[m % 2][:, :], waits=[ev_c],
                                      sem=f"dspq{m % 2}")
                        qev_free[m % 2] = ev_sp
                        if debug:
                            qev_free[m % 2] = P.dma(
                                "gpsimd", dbg["q"][m * QH: m * QH + 128, t0:t0 + 512],
                                qev[m % 2][:, :], waits=[ev_c], sem=f"dbq{m % 2}")
                    else:
                        j = m - 16
                        x1 = ps[0:64, 0:512]
                        x2 = ps[64:128, 0:512]
                        cS = cos2[:, t0:t0 + 512]
                        sS = sin2[:, t0:t0 + 512]
                        ro = ropeo[j % 2]
                        w0 = [ev_mm, ropeo_free[j % 2]]
                        P.emit("vector", lambda e, x1=x1, cS=cS, a=rs_a:
                               e.tensor_tensor(a[:, :], x1, cS, ALU.mult), waits=w0)
                        P.emit("vector", lambda e, x2=x2, sS=sS, b=rs_b:
                               e.tensor_tensor(b[:, :], x2, sS, ALU.mult))
                        P.emit("vector", lambda e, ro=ro:
                               e.tensor_tensor(ro[0:64, :], rs_a[:, :], rs_b[:, :],
                                               ALU.subtract))
                        P.emit("vector", lambda e, x2=x2, cS=cS, a=rs_a:
                               e.tensor_tensor(a[:, :], x2, cS, ALU.mult))
                        P.emit("vector", lambda e, x1=x1, sS=sS, b=rs_b:
                               e.tensor_tensor(b[:, :], x1, sS, ALU.mult))
                        ev_ro = P.emit("vector", lambda e, ro=ro:
                                       e.tensor_tensor(ro[64:128, :], rs_a[:, :], rs_b[:, :],
                                                       ALU.add), inc="ve")
                        psQ_free[m % 2] = ev_ro
                        for two in range(2):
                            hh = 2 * j + two
                            for half in range(2):
                                in_ap = ro[half * 64 + two * 32: half * 64 + two * 32 + 32, :]
                                r0 = hh * QH + 128 + half * 32
                                ropeo_free[j % 2] = P.dma(
                                    "sync", q_flat[r0:r0 + 32, t0:t0 + 512],
                                    in_ap, waits=[ev_ro], sem=f"dspr{j % 2}")
                                if debug:
                                    ropeo_free[j % 2] = P.dma(
                                        "gpsimd", dbg["q"][r0:r0 + 32, t0:t0 + 512],
                                        in_ap, waits=[ev_ro], sem=f"dbr{j % 2}")
                xin_free = [xT_ready] * 4
                xT_free = P.ev("pe")
            P.barrier()
            P.flush()


    cs_stk.close()


    if "p0c" in phases:
        with contextlib.ExitStack() as ph:
            wkvb = [ph.enter_context(nc.sbuf_tensor(f"wkvb{i}", [128, HD * 256], BF16))
                    for i in range(4)]
            ks1buf = [ph.enter_context(nc.sbuf_tensor(f"ks1b{i}", [128, T], BF16))
                      for i in range(2)]
            vbuf = [ph.enter_context(nc.sbuf_tensor(f"vb{i}", [128, T], BF16))
                    for i in range(2)]
            vtkbuf = [ph.enter_context(nc.sbuf_tensor(f"vtk{i}", [128, T], BF16))
                      for i in range(2)]
            psK = [ph.enter_context(nc.psum_tensor(f"psK{i}", [128, 512], F32))
                   for i in range(2)]
            pst = [ph.enter_context(nc.psum_tensor(f"pstc{i}", [128, 1024], BF16))
                   for i in range(2)]

            wb_ev = None
            for i in range(4):
                wb_ev = P.dma("gpsimd", wkvb[i][:, :],
                              wkvb_h[i * 128:(i + 1) * 128, :])
            psK_free = [None, None]
            pst_free = [None, None]
            buf_free = [[None, None], [None, None], [None, None]]
            for h in range(HD):
                sl = h % 2
                for part in range(2):
                    col0 = h * 256 + part * 128
                    dstb = ks1buf[sl] if part == 0 else vbuf[sl]
                    ev_ev = None
                    for tcc in range(8):
                        ps = psK[tcc % 2]
                        ev_mm = None
                        for lt in range(4):
                            w = []
                            if lt == 0:
                                w = [psK_free[tcc % 2], wb_ev,
                                     buf_free[part][sl] if tcc == 0 else None]
                            ev_mm = P.emit("tensor",
                                           lambda e, ps=ps, lt=lt, col0=col0, tcc=tcc:
                                           e.matmul(ps[:, 0:512],
                                                    wkvb[lt][:, col0:col0 + 128],
                                                    ckv_n[lt][:, tcc * 512:(tcc + 1) * 512],
                                                    start=(lt == 0), stop=(lt == 3)),
                                           waits=w, inc="pe" if lt == 3 else None)
                        ev_ev = P.emit("scalar",
                                       lambda e, ps=ps, dstb=dstb, tcc=tcc:
                                       e.copy(dstb[:, tcc * 512:(tcc + 1) * 512],
                                              ps[:, 0:512]),
                                       waits=[ev_mm], inc="se")
                        psK_free[tcc % 2] = ev_ev
                    if part == 0:
                        ev_sp = P.dma("sync", ks1_d[h * 128:(h + 1) * 128, :],
                                      ks1buf[sl][:, :], waits=[ev_ev],
                                      sem=f"dspk{sl}")
                        buf_free[0][sl] = ev_sp
                        if debug:
                            buf_free[0][sl] = P.dma(
                                "gpsimd", dbg["ks1"][h * 128:(h + 1) * 128, :],
                                ks1buf[sl][:, :], waits=[ev_ev], sem=f"dbk{sl}")
                    else:
                        ev_tk = None
                        for b8 in range(4):
                            pb = pst[b8 % 2]
                            ev_t = None
                            for j in range(8):
                                kt = b8 * 8 + j
                                w = [pst_free[b8 % 2], ev_ev,
                                     buf_free[2][sl] if b8 == 0 else None] if j == 0 else []
                                ev_t = P.emit("tensor",
                                              lambda e, pb=pb, vv=vbuf[sl], kt=kt, j=j:
                                              e.transpose(pb[:, j * 128:(j + 1) * 128],
                                                          vv[:, kt * 128:(kt + 1) * 128],
                                                          identb[:, :]),
                                              waits=w, inc="pe" if j == 7 else None)
                            ev_tk = P.emit("scalar",
                                           lambda e, pb=pb, dst=vtkbuf[sl], b8=b8:
                                           e.copy(dst[:, b8 * 1024:(b8 + 1) * 1024],
                                                  pb[:, :]),
                                           waits=[ev_t], inc="se")
                            pst_free[b8 % 2] = ev_tk
                        ev_sp = P.dma("sync", vtok_d[h * 128:(h + 1) * 128, :],
                                      vtkbuf[sl][:, :], waits=[ev_tk],
                                      sem=f"dspv{sl}")
                        buf_free[2][sl] = ev_sp
                        buf_free[1][sl] = P.ev("pe")
                        if debug:
                            buf_free[2][sl] = P.dma(
                                "gpsimd", dbg["vtok"][h * 128:(h + 1) * 128, :],
                                vtkbuf[sl][:, :], waits=[ev_tk], sem=f"dbv{sl}")
            P.barrier()
            P.flush()
    ckv_stk.close()


    fw_stk = contextlib.ExitStack()
    fw_inline = ("fw" in phases and "p1" in phases)
    fw_state = {}
    if fw_inline:
        fwi32_t = fw_stk.enter_context(nc.sbuf_tensor("fwi32", [128, E], I32))
        fwi32 = [fwi32_t, fwi32_t]
        fwidxb = fw_stk.enter_context(nc.sbuf_tensor("fwidxb", [128, E], BF16))
        fwidxT_ = [fw_stk.enter_context(nc.sbuf_tensor(f"fwidxT{i}", [128, E], F32))
                   for i in range(2)]
        fwmask = fw_stk.enter_context(nc.sbuf_tensor("fwmask", [128, E], F32))
        fwacc = [fw_stk.enter_context(nc.sbuf_tensor(f"fwacc{i}", [128, E], BF16))
                 for i in range(2)]
        psFT = fw_stk.enter_context(nc.psum_tensor("psFT", [128, 1024], BF16))
    if "p1" in phases:
        with contextlib.ExitStack() as ph:
            qs1 = [ph.enter_context(nc.sbuf_tensor(f"qs1_{i}", [128, T], BF16)) for i in range(2)]
            qs2 = [ph.enter_context(nc.sbuf_tensor(f"qs2_{i}", [66, T], BF16)) for i in range(2)]
            ks1 = [ph.enter_context(nc.sbuf_tensor(f"ks1p_{i}", [128, T], BF16)) for i in range(2)]
            vtk = [ph.enter_context(nc.sbuf_tensor(f"vtkp_{i}", [128, T], BF16)) for i in range(2)]
            sqb = ph.enter_context(nc.sbuf_tensor("sqb", [128, T], BF16))
            sq2b = ph.enter_context(nc.sbuf_tensor("sq2b", [64, T], BF16))
            nk2sl = [ph.enter_context(nc.sbuf_tensor(f"nk2_{i}", [1, T], F32))
                     for i in range(2)]
            nk2pe = ph.enter_context(nc.sbuf_tensor("nk2pe", [1, T], BF16))
            pmax = ph.enter_context(nc.sbuf_tensor("pmax", [1, T], F32))
            rs_row_t = ph.enter_context(nc.sbuf_tensor("rs_row", [1, 512], F32))
            rs_row = [rs_row_t, rs_row_t]
            rSb = [ph.enter_context(nc.sbuf_tensor(f"rSb{i}", [128, 512], F32))
                   for i in range(2)]
            pT = [ph.enter_context(nc.sbuf_tensor(f"pT{i}", [128, 512], BF16)) for i in range(2)]
            ob = [ph.enter_context(nc.sbuf_tensor(f"ob{i}", [128, 512], BF16)) for i in range(2)]

            psST = [ph.enter_context(nc.psum_tensor(f"psST_{i}", [128, 512], F32)) for i in range(2)]
            psAV_ = [ph.enter_context(nc.psum_tensor(f"psAV{i}", [128, 512], F32)) for i in range(2)]
            psS_t = ph.enter_context(nc.psum_tensor("psS0", [1, 512], F32))
            psS = [psS_t, psS_t]
            psB = ph.enter_context(nc.psum_tensor("psBn", [128, 512], F32))
            psN = ph.enter_context(nc.psum_tensor("psNn", [1, 512], F32))


            ev0 = P.emit("scalar", lambda e: e.activation(sq2b[:, :], ks2[0:64, :], AF.Square),
                         inc="se")
            psN_free = None
            for c8 in range(8):
                ev_m = P.emit("tensor",
                              lambda e, c8=c8: e.matmul(psN[0:1, 0:512], onesb[0:64, :],
                                                        sq2b[:, c8 * 512:(c8 + 1) * 512],
                                                        start=True, stop=True),
                              waits=[ev0, psN_free], inc="pe")
                psN_free = P.emit("vector",
                                  lambda e, c8=c8: e.tensor_copy(
                                      nk2pe[0:1, c8 * 512:(c8 + 1) * 512], psN[0:1, 0:512]),
                                  waits=[ev_m], inc="ve")

            def load_head(h, slot, waits):
                e1 = P.dma("sync", qs1[slot][:, :], q_flat[h * QH: h * QH + 128, :],
                           waits=waits, sem=f"dldq{slot}")
                P.dma("sync", qs2[slot][0:64, :], q_flat[h * QH + 128: h * QH + QH, :],
                      waits=[], sem=f"dldq{slot}")
                P.dma("sync", ks1[slot][:, :], ks1_d[h * 128:(h + 1) * 128, :],
                      waits=[], sem=f"dldq{slot}")
                ev = P.dma("sync", vtk[slot][:, :], vtok_d[h * 128:(h + 1) * 128, :],
                           waits=[], sem=f"dldq{slot}")
                return ev

            slot_free = [None, None]
            ld_ev = [None, None]
            ld_ev[0] = load_head(0, 0, [])
            ld_ev[1] = load_head(1, 1, [])
            psST_free = [None, None]
            psAV_free = [None, None]
            psS_free = [None, None]
            pend_fin = [None]
            psB_free = None
            pT_free = [None, None]
            ob_free = [None, None]
            rs_free = [None]
            rSb_free = [None, None]
            sq_free = None
            ct_done = [None, None]
            head_ct = [None, None]
            aug_done = [None, None]

            def emit_norms(h, sl, ld):
                nonlocal psN_free, sq_free
                nk2 = nk2sl[sl]
                ev_k1 = P.emit("scalar", lambda e, sl=sl: e.activation(
                    sqb[:, :], ks1[sl][:, :], AF.Square), waits=[ld, sq_free], inc="se")
                for c8 in range(8):
                    P.emit("tensor", lambda e, c8=c8: e.matmul(
                        psN[0:1, 0:512], onesb[:, :], sqb[:, c8 * 512:(c8 + 1) * 512],
                        start=True, stop=False),
                        waits=[ev_k1 if c8 == 0 else None, psN_free], inc=None)
                    ev_m = P.emit("tensor", lambda e, c8=c8: e.matmul(
                        psN[0:1, 0:512], onesb[0:1, :],
                        nk2pe[0:1, c8 * 512:(c8 + 1) * 512],
                        start=False, stop=True), waits=[], inc="pe")
                    psN_free = P.emit("scalar", lambda e, nk2=nk2, c8=c8: e.copy(
                        nk2[0:1, c8 * 512:(c8 + 1) * 512], psN[0:1, 0:512]),
                        waits=[ev_m, ct_done[sl]], inc="se")
                ev_pm = P.emit("vector", lambda e, nk2=nk2: e.tensor_tensor_scan(
                    pmax[0:1, :], nk2[0:1, :], nk2[0:1, :], -1e30, ALU.max, ALU.bypass),
                    waits=[psN_free], inc="ve")
                ev_q1 = P.emit("scalar", lambda e, sl=sl: e.activation(
                    sqb[:, :], qs1[sl][:, :], AF.Square), waits=[P.ev("pe")], inc="se")
                ev_q2 = P.emit("scalar", lambda e, sl=sl: e.activation(
                    sq2b[:, :], qs2[sl][0:64, :], AF.Square), waits=[], inc="se")
                ev_ct = None
                for c8 in range(8):
                    P.emit("tensor", lambda e, c8=c8: e.matmul(
                        psN[0:1, 0:512], onesb[:, :], sqb[:, c8 * 512:(c8 + 1) * 512],
                        start=True, stop=False),
                        waits=[ev_q1 if c8 == 0 else None, psN_free], inc=None)
                    ev_m = P.emit("tensor", lambda e, c8=c8: e.matmul(
                        psN[0:1, 0:512], onesb[0:64, :], sq2b[:, c8 * 512:(c8 + 1) * 512],
                        start=False, stop=True),
                        waits=[ev_q2] if c8 == 0 else [], inc="pe")
                    psN_free = P.emit("vector", lambda e, nk2=nk2, c8=c8: e.tensor_tensor(
                        nk2[0:1, c8 * 512:(c8 + 1) * 512], psN[0:1, 0:512],
                        pmax[0:1, c8 * 512:(c8 + 1) * 512], ALU.add),
                        waits=[ev_m, ev_pm if c8 == 0 else None], inc="ve")
                    ev_ct = psN_free
                sq_free = P.ev("pe")
                ct_done[sl] = ev_ct
                ev_aug = P.emit("scalar", lambda e, sl=sl, nk2=nk2: e.mul(
                    qs2[sl][64:65, :], nk2[0:1, :], -0.5), waits=[ev_ct], inc="se")
                aug_done[sl] = ev_aug
                return ev_ct

            def emit_fin(h, sl, J, uu, ev_av, ev_be):
                nonlocal psAV_free, pT_free, ob_free
                qb5 = J * 512
                ev_ob = P.emit("vector", lambda e, uu=uu, J=J: e.tensor_tensor(
                    ob[uu % 2][:, :], psAV_[uu % 2][:, 0:512], rSb[uu % 2][:, :],
                    ALU.mult),
                    waits=[ev_av, ev_be, ob_free[uu % 2]], inc="ve")
                rSb_free[uu % 2] = ev_ob
                psAV_free[uu % 2] = ev_ob
                ev_sp = P.dma("sync", o_d[h * 128:(h + 1) * 128, qb5:qb5 + 512],
                              ob[uu % 2][:, :], waits=[ev_ob], sem=f"dspo{uu % 2}")
                ob_free[uu % 2] = ev_sp
                if debug:
                    ob_free[uu % 2] = P.dma(
                        "gpsimd", dbg["o"][h * 128:(h + 1) * 128, qb5:qb5 + 512],
                        ob[uu % 2][:, :], waits=[ev_ob], sem=f"dbo{uu % 2}")

            def emit_pass2(h, sl, J, uu):
                nonlocal psST_free, pT_free, psB_free
                qb5 = J * 512
                ktmax = 4 * (J + 1)
                ev_av = None
                ev_s = None
                sS = psS[0]
                psAV = psAV_[uu % 2]

                def emit_av(kt, ev_pt):
                    nonlocal ev_av, ev_s
                    w_av = [ev_pt] if kt > 0 else [ev_pt, psAV_free[uu % 2]]
                    ev_av = P.emit("tensor", lambda e, psAV=psAV, sl=sl, kt=kt, ktmax=ktmax: e.matmul(
                        psAV[:, 0:512], vtk[sl][:, kt * 128:(kt + 1) * 128],
                        pT[kt % 2][:, :], start=(kt == 0), stop=(kt == ktmax - 1)),
                        waits=w_av, inc="pe")
                    ev_s = P.emit("tensor", lambda e, sS=sS, kt=kt, ktmax=ktmax: e.matmul(
                        sS[0:1, 0:512], onesb[:, :], pT[kt % 2][:, :],
                        start=(kt == 0), stop=(kt == ktmax - 1)),
                        waits=[psS_free[0]] if kt == 0 else [],
                        inc="pe" if kt == ktmax - 1 else None)
                    pT_free[kt % 2] = ev_s

                pend_av = None
                for kt in range(ktmax):
                    st = psST[kt % 2]
                    d = kt - 4 * J
                    P.emit("tensor", lambda e, st=st, sl=sl, kt=kt, qb5=qb5: e.matmul(
                        st[:, 0:512], ks1[sl][:, kt * 128:(kt + 1) * 128],
                        qs1[sl][:, qb5:qb5 + 512], start=True, stop=False),
                        waits=[psST_free[kt % 2], aug_done[sl]] if kt == 0
                        else [psST_free[kt % 2]])
                    ev_mm = P.emit("tensor",
                                   lambda e, st=st, sl=sl, kt=kt, qb5=qb5, d=d: e.matmul(
                                       st[:, 0:512], ks2[0:65, kt * 128:(kt + 1) * 128],
                                       qs2[sl][0:65, qb5:qb5 + 512],
                                       start=False, stop=(d < 0)),
                                   inc=None if d >= 0 else "pe")
                    if d >= 0:
                        ev_mm = P.emit("tensor", lambda e, st=st, d=d: e.matmul(
                            st[:, 0:512], negI[:, :], step2[:, d * 512:(d + 1) * 512],
                            start=False, stop=True), inc="pe")
                    ev_pt = P.emit("scalar", lambda e, st=st, kt=kt: e.activation(
                        pT[kt % 2][:, :], st[:, 0:512], AF.Exp, scale=SCALE),
                        waits=[ev_mm, pT_free[kt % 2]], inc="se")
                    psST_free[kt % 2] = ev_pt
                    if pend_av is not None:
                        emit_av(*pend_av)
                    pend_av = (kt, ev_pt)
                emit_av(*pend_av)
                ev_ln = P.emit("scalar", lambda e, sS=sS: e.activation(
                    rs_row[0][0:1, :], sS[0:1, 0:512], AF.Ln),
                    waits=[ev_s, rs_free[0]], inc="se")
                psS_free[0] = ev_ln
                ev_rs = P.emit("scalar", lambda e: e.activation(
                    rs_row[0][0:1, :], rs_row[0][0:1, :], AF.Exp, scale=-1.0),
                    waits=[ev_ln], inc="se")
                ev_bc = P.emit("tensor", lambda e: e.matmul(
                    psB[:, 0:512], onesf_row[:, :], rs_row[0][0:1, :],
                    start=True, stop=True), waits=[ev_rs, psB_free], inc="pe")
                rs_free[0] = ev_bc
                ev_be = P.emit("scalar", lambda e, uu=uu: e.copy(
                    rSb[uu % 2][:, :], psB[:, 0:512]),
                    waits=[ev_bc, rSb_free[uu % 2]], inc="se")
                psB_free = ev_be
                return (ev_av, ev_be)

            def fw_generator():
                i32_free = [None, None]
                idxb_free = None
                idxT_free = [None, None]
                mask_free = None
                accf = [None, None]
                jobs = []
                for it in range(NIT):
                    jobs.append((gidx_h, it * 128, 0, E, wgT_d[it],
                                 dbg.get("wgT")[it] if (debug and dbg.get("wgT") is not None) else None))
                for it in range(NIT):
                    jobs.append((uidx_h, it * 128, 0, E, wuT_d[it], None))
                for et in range(NET):
                    for (c0, w) in ((0, 2048), (2048, 2048), (4096, 1408)):
                        jobs.append((didx_h, et * 128, c0, w,
                                     wdT_d[et][:, c0:c0 + w],
                                     dbg.get("wdT")[et][:, c0:c0 + w]
                                     if (debug and dbg.get("wdT") is not None) else None))
                def prep(si):
                    nonlocal idxb_free
                    (src_h, r0, c0, width, dst_ap, dbg_ap) = jobs[si]
                    sl = si % 2
                    fwidxT = fwidxT_[sl]
                    ev_ld = P.dma("sync", fwi32[sl][:, 0:width],
                                  src_h[r0:r0 + 128, c0:c0 + width],
                                  waits=[i32_free[0], i32_free[1]], sem=f"dfwl{sl}")
                    yield
                    ev_cv = P.emit("scalar", lambda e, sl=sl, width=width: e.copy(
                        fwidxb[:, 0:width], fwi32[sl][:, 0:width]),
                        waits=[ev_ld, idxb_free], inc="se")
                    i32_free[0] = ev_cv
                    nbl = width // 128
                    ev_ev = None
                    for b0 in range(0, nbl, 8):
                        nb = min(8, nbl - b0)
                        ev_t = None
                        for j in range(nb):
                            blk = b0 + j
                            w = ([psFT_free[0], ev_cv] +
                                 ([idxT_free[sl]] if b0 == 0 else [])) if j == 0 else []
                            ev_t = P.emit("tensor", lambda e, fwidxT=fwidxT, blk=blk, j=j:
                                          e.transpose(psFT[:, j * 128:(j + 1) * 128],
                                                      fwidxb[:, blk * 128:(blk + 1) * 128],
                                                      identb[:, :]),
                                          waits=w, inc="pe" if j == nb - 1 else None)
                        ev_ev = P.emit("scalar", lambda e, fwidxT=fwidxT, b0=b0, nb=nb:
                                       e.copy(fwidxT[:, b0 * 128:(b0 + nb) * 128],
                                              psFT[:, 0:nb * 128]),
                                       waits=[ev_t], inc="se")
                        psFT_free[0] = ev_ev
                        yield
                    idxb_free = P.ev("pe")
                    prep_ev[si % 2] = ev_ev

                prep_ev = [None, None]
                g0 = prep(0)
                for _ in g0:
                    pass
                for si in range(len(jobs)):
                    (src_h, r0, c0, width, dst_ap, dbg_ap) = jobs[si]
                    sl = si % 2
                    fwidxT = fwidxT_[sl]
                    ev_ev = prep_ev[sl]
                    pgen = prep(si + 1) if si + 1 < len(jobs) else None
                    ev_k = P.emit("vector", lambda e, sl=sl, width=width: e.memset(
                        fwacc[sl][:, 0:width], 0.0), waits=[accf[sl]], inc="ve")
                    for k in range(16):
                        P.emit("vector", lambda e, fwidxT=fwidxT, k=k, width=width:
                               e.tensor_scalar(fwmask[:, 0:width], fwidxT[:, 0:width],
                                               float(k), None, ALU.is_equal),
                               waits=[ev_ev, mask_free] if k == 0 else [])
                        last = (k == 15)
                        ev_k = P.emit("vector", lambda e, sl=sl, k=k, width=width:
                                      e.scalar_tensor_tensor(
                                          fwacc[sl][:, 0:width], fwmask[:, 0:width],
                                          lut_col[:, k:k + 1], fwacc[sl][:, 0:width],
                                          ALU.mult, ALU.add),
                                      waits=[], inc="ve" if last else None)
                        if pgen is not None and k % 4 == 3:
                            for _ in pgen:
                                break
                        if k % 4 == 3:
                            yield
                    if pgen is not None:
                        for _ in pgen:
                            pass
                    mask_free = ev_k
                    idxT_free[sl] = ev_k
                    ev_sp = P.dma("sync", dst_ap, fwacc[sl][:, 0:width],
                                  waits=[ev_k], sem=f"dfws{sl}")
                    accf[sl] = ev_sp
                    if dbg_ap is not None:
                        accf[sl] = P.dma("gpsimd", dbg_ap, fwacc[sl][:, 0:width],
                                         waits=[ev_k], sem=f"dfwd{sl}")
                    yield

            psFT_free = [None]
            fw_state["it"] = fw_generator() if fw_inline else None
            fw_state["live"] = fw_inline

            def pump(n):
                if not fw_state["live"]:
                    return
                try:
                    for _ in range(n):
                        next(fw_state["it"])
                except StopIteration:
                    fw_state["live"] = False
            fw_state["pump"] = pump

            units = [(h, J) for h in range(HD) for J in range(8)]

            def after_pass2(hh, ssl, JJ):
                if JJ == 7:
                    slot_free[ssl] = P.ev("pe")
                    if hh + 2 < HD:
                        ld_ev[ssl] = load_head(hh + 2, ssl, [slot_free[ssl]])

            head_ct[0] = emit_norms(0, 0, ld_ev[0])
            for uu, (h, J) in enumerate(units):
                sl = h % 2
                pump(4)
                evs = emit_pass2(h, sl, J, uu)
                if pend_fin[0] is not None:
                    emit_fin(*pend_fin[0])
                    after_pass2(pend_fin[0][0], pend_fin[0][1], pend_fin[0][2])
                pend_fin[0] = (h, sl, J, uu, evs[0], evs[1])
                if J == 4 and h + 1 < HD:
                    head_ct[1 - sl] = emit_norms(h + 1, 1 - sl, ld_ev[1 - sl])
                pump(4)
            emit_fin(*pend_fin[0])
            after_pass2(pend_fin[0][0], pend_fin[0][1], pend_fin[0][2])
            P.barrier()
            P.flush()

    if "p2" in phases:
        with contextlib.ExitStack() as ph:
            wo_b = [ph.enter_context(nc.sbuf_tensor(f"wo{i}", [128, E], BF16))
                    for i in range(NET)]
            wo_st_t = ph.enter_context(nc.sbuf_tensor("wost", [128, E], F32))
            wo_st = [wo_st_t, wo_st_t]
            oslab = [ph.enter_context(nc.sbuf_tensor(f"osl{i}", [128, 8192], BF16))
                     for i in range(2)]
            x_in = [ph.enter_context(nc.sbuf_tensor(f"xr{i}", [128, E], F32))
                    for i in range(2)]
            z = ph.enter_context(nc.sbuf_tensor("z", [128, E], F32))
            zsq = ph.enter_context(nc.sbuf_tensor("zsq", [128, E], BF16))
            yb = ph.enter_context(nc.sbuf_tensor("yb", [128, E], F32))
            ybf = ph.enter_context(nc.sbuf_tensor("ybf", [128, E], BF16))
            yT_buf = ph.enter_context(nc.sbuf_tensor("yT_buf", [128, 1024], BF16))
            p2st = ph.enter_context(nc.sbuf_tensor("p2st", [128, 8], F32))
            s1 = p2st[:, 0:1]
            s2 = p2st[:, 1:2]
            negmean = p2st[:, 2:3]
            msq = p2st[:, 3:4]
            var = p2st[:, 4:5]
            rstd = p2st[:, 5:6]
            psO = ph.enter_context(nc.psum_tensor("psO", [128, 2048], F32))
            pst = [ph.enter_context(nc.psum_tensor(f"pstp2{i}", [128, 1024], BF16))
                   for i in range(2)]

            wo_ev = None
            wost_free = [None, None]
            for i in range(NET):
                ev_l = P.dma("sync", wo_st[i % 2][:, :], wo_h[i * 128:(i + 1) * 128, :],
                             waits=[wost_free[0], wost_free[1]], sem=f"dwo{i % 2}")
                wo_ev = P.emit("scalar", lambda e, i=i: e.copy(
                    wo_b[i][:, :], wo_st[i % 2][:, :]), waits=[ev_l], inc="se")
                wost_free[0] = wo_ev
            oslab_free = [None, None]
            xslot_free = [None, None]
            x_free = None
            pst_free = [None, None]
            psO_free = None
            z_free = None
            yb_free = None
            zsq_free = None
            ybf_free = None
            yT_free = None
            stat_free = None
            ld_os = [None, None]
            ld_x = None

            def load_t512(tt, sl, waits):
                ev = None
                for dt in range(NET):
                    ev = P.dma("sync", oslab[sl][:, dt * 512:(dt + 1) * 512],
                               o_d[dt * 128:(dt + 1) * 128, tt * 512:(tt + 1) * 512],
                               waits=waits if dt == 0 else [], sem=f"dslo{sl}")
                return ev

            ld_os[0] = load_t512(0, 0, [])
            for tt in range(8):
                sl = tt % 2
                ld = ld_os[sl]
                xslot_ev = [None, None]
                if tt + 1 < 8:
                    ld_os[1 - sl] = load_t512(tt + 1, 1 - sl, [oslab_free[1 - sl]])
                for ts in range(4):
                    ldx = P.dma("sync", x_in[ts % 2][:, :],
                                x_h[tt * 512 + ts * 128: tt * 512 + (ts + 1) * 128, :],
                                waits=[xslot_free[ts % 2]], sem=f"dslx{ts % 2}")
                    ev_z = None
                    for ec in range(4):
                        ev_mm = None
                        for dt in range(NET):
                            w = [ld, ldx, wo_ev, psO_free] if (ec == 0 and dt == 0) else []
                            ev_mm = P.emit("tensor",
                                           lambda e, dt=dt, ec=ec, ts=ts, sl=sl: e.matmul(
                                               psO[:, ec * 512:(ec + 1) * 512],
                                               oslab[sl][:, dt * 512 + ts * 128: dt * 512 + (ts + 1) * 128],
                                               wo_b[dt][:, ec * 512:(ec + 1) * 512],
                                               start=(dt == 0), stop=(dt == NET - 1)),
                                           waits=w, inc="pe" if dt == NET - 1 else None)
                        ev_z = P.emit("vector",
                                      lambda e, ec=ec, ts=ts: e.tensor_tensor(
                                          z[:, ec * 512:(ec + 1) * 512],
                                          psO[:, ec * 512:(ec + 1) * 512],
                                          x_in[ts % 2][:, ec * 512:(ec + 1) * 512], ALU.add),
                                      waits=[ev_mm] + ([z_free] if ec == 0 else []),
                                      inc="ve")
                    psO_free = ev_z
                    xslot_free[ts % 2] = ev_z
                    if debug:
                        t0d = tt * 512 + ts * 128
                        P.dma("gpsimd", dbg["z"][t0d:t0d + 128, :], z[:, :],
                              waits=[ev_z], sem="dbz")
                        z_free = P.ev("dbz")
                    ev_sq = P.emit("scalar", lambda e: e.activation(
                        zsq[:, :], z[:, :], AF.Square), waits=[ev_z, zsq_free], inc="se")
                    ev_s1 = P.emit("vector", lambda e: e.reduce_sum(
                        s1, z[:, :], axis=AX.X), waits=[stat_free], inc="ve")
                    ev_s2 = P.emit("vector", lambda e: e.reduce_sum(
                        s2, zsq[:, :], axis=AX.X), waits=[ev_sq], inc="ve")
                    ev_nm = P.emit("vector", lambda e: e.tensor_scalar(
                        negmean, s1, -1.0 / E, None, ALU.mult),
                        waits=[ev_s1], inc="ve")
                    ev_mq = P.emit("vector", lambda e: e.tensor_tensor(
                        msq, negmean, negmean, ALU.mult),
                        waits=[ev_nm], inc="ve")
                    ev_vr = P.emit("vector", lambda e: e.scalar_tensor_tensor(
                        var, s2, 1.0 / E, msq, ALU.mult, ALU.subtract),
                        waits=[ev_mq, ev_s2], inc="ve")
                    ev_lv = P.emit("scalar", lambda e: e.activation(
                        rstd, var, AF.Ln, bias=LN_EPS), waits=[ev_vr], inc="se")
                    ev_rs = P.emit("scalar", lambda e: e.activation(
                        rstd, rstd, AF.Exp, scale=-0.5), waits=[ev_lv], inc="se")
                    ev_y0 = P.emit("vector", lambda e: e.tensor_scalar(
                        yb[:, :], z[:, :], negmean, None, ALU.add),
                        waits=[ev_rs, ev_s2, yb_free], inc="ve")
                    ev_y = P.emit("vector", lambda e: e.tensor_scalar(
                        z[:, :], yb[:, :], rstd, None, ALU.mult),
                        waits=[ev_y0], inc="ve")
                    z_free = ev_y
                    yb_free = ev_y
                    zsq_free = ev_s2
                    stat_free = ev_y
                    t0 = tt * 512 + ts * 128
                    ev_spy = P.dma("sync", y_d[t0:t0 + 128, :], z[:, :],
                                   waits=[ev_y], sem="dspy")
                    z_free = ev_spy
                    if debug:
                        ev_spy = P.dma("gpsimd", dbg["y"][t0:t0 + 128, :], z[:, :],
                                       waits=[ev_y], sem="dspy2")
                        z_free = ev_spy
                    ev_yb = P.emit("scalar", lambda e: e.copy(ybf[:, :], z[:, :]),
                                   waits=[ev_y, ybf_free], inc="se")
                    z_free = ev_yb
                    for bat in range(2):
                        pb = pst[bat]
                        ev_t = None
                        for j in range(8):
                            et = bat * 8 + j
                            w = [pst_free[bat], ev_yb] if j == 0 else []
                            ev_t = P.emit("tensor", lambda e, pb=pb, et=et, j=j: e.transpose(
                                pb[:, j * 128:(j + 1) * 128],
                                ybf[:, et * 128:(et + 1) * 128], identb[:, :]),
                                waits=w, inc="pe" if j == 7 else None)
                        ev_tb = P.emit("scalar", lambda e, pb=pb: e.copy(
                            yT_buf[:, :], pb[:, :]),
                            waits=[ev_t, yT_free], inc="se")
                        pst_free[bat] = ev_tb
                        out_ap = yT_d[tt, bat * 8:(bat + 1) * 8, :,
                                      ts * 128:(ts + 1) * 128].rearrange(
                            "et p k -> p et k")
                        in_ap = yT_buf[:, :].rearrange("p (et k) -> p et k", k=128)
                        yT_free = P.dma("sync", out_ap, in_ap, waits=[ev_tb],
                                        sem="dspyt")
                    ybf_free = P.ev("pe")
                oslab_free[sl] = P.ev("pe")
                if fw_state.get("pump"):
                    fw_state["pump"](12)
            if fw_state.get("pump"):
                fw_state["pump"](10 ** 9)
            P.barrier()
            P.flush()
    fw_stk.close()

    if "fw" in phases and "p1" not in phases:
        with contextlib.ExitStack() as ph:
            idxb = [ph.enter_context(nc.sbuf_tensor(f"idxb{i}", [128, INTER], BF16))
                    for i in range(2)]
            idxT = [ph.enter_context(nc.sbuf_tensor(f"idxT{i}", [128, INTER], F32))
                    for i in range(2)]
            maskb = ph.enter_context(nc.sbuf_tensor("maskb", [128, INTER], BF16))
            acc = [ph.enter_context(nc.sbuf_tensor(f"facc{i}", [128, INTER], BF16))
                   for i in range(2)]
            pst = [ph.enter_context(nc.psum_tensor(f"pstf{i}", [128, 1024], BF16))
                   for i in range(2)]

            pst_free = [None, None]
            idxb_free = [None, None]
            idxT_free = [None, None]
            acc_free = [None, None]
            mask_free = None

            def materialize(src_h, dst_d, dbg_d, tiles, width, row0fn):
                nonlocal mask_free
                for it in range(tiles):
                    sl = it % 2
                    r0 = row0fn(it)
                    ev_ld = P.dma("gpsimd", idxb[sl][:, 0:width],
                                  src_h[r0:r0 + 128, :], waits=[idxb_free[sl]],
                                  sem=f"dfw{sl}")
                    nbl = width // 128
                    ev_ev = None
                    for b0 in range(0, nbl, 8):
                        nb = min(8, nbl - b0)
                        pb = pst[(b0 // 8) % 2]
                        ev_t = None
                        for j in range(nb):
                            blk = b0 + j
                            w = [pst_free[(b0 // 8) % 2], ev_ld,
                                 idxT_free[sl] if b0 == 0 else None] if j == 0 else []
                            ev_t = P.emit("tensor", lambda e, pb=pb, sl=sl, blk=blk, j=j:
                                          e.transpose(pb[:, j * 128:(j + 1) * 128],
                                                      idxb[sl][:, blk * 128:(blk + 1) * 128],
                                                      identb[:, :]),
                                          waits=w, inc="pe" if j == nb - 1 else None)
                        ev_ev = P.emit("scalar", lambda e, pb=pb, sl=sl, b0=b0, nb=nb: e.copy(
                            idxT[sl][:, b0 * 128:(b0 + nb) * 128], pb[:, 0:nb * 128]),
                            waits=[ev_t], inc="se")
                        pst_free[(b0 // 8) % 2] = ev_ev
                    idxb_free[sl] = P.ev("pe")
                    ev_ms = P.emit("vector", lambda e, sl=sl, width=width: e.memset(
                        acc[sl][:, 0:width], 0.0), waits=[acc_free[sl]], inc="ve")
                    ev_k = ev_ms
                    for k in range(16):
                        ev_m = P.emit("vector", lambda e, sl=sl, k=k, width=width:
                                      e.tensor_scalar(maskb[:, 0:width],
                                                      idxT[sl][:, 0:width], float(k),
                                                      None, ALU.is_equal),
                                      waits=[ev_ev, mask_free] if k == 0 else [ev_k],
                                      inc="ve")
                        ev_k = P.emit("vector", lambda e, sl=sl, k=k, width=width:
                                      e.scalar_tensor_tensor(acc[sl][:, 0:width],
                                                             maskb[:, 0:width],
                                                             lut_col[:, k:k + 1],
                                                             acc[sl][:, 0:width],
                                                             ALU.mult, ALU.add),
                                      waits=[ev_m], inc="ve")
                    mask_free = ev_k
                    idxT_free[sl] = ev_k
                    ev_sp = P.dma("sync", dst_d[it], acc[sl][:, 0:width],
                                  waits=[ev_k], sem=f"dfsp{sl}")
                    acc_free[sl] = ev_sp
                    if debug and dbg_d is not None:
                        acc_free[sl] = P.dma("gpsimd", dbg_d[it], acc[sl][:, 0:width],
                                             waits=[ev_k], sem=f"dfdb{sl}")

            materialize(gidx_h, wgT_d, dbg.get("wgT"), NIT, E, lambda it: it * 128)
            materialize(uidx_h, wuT_d, None, NIT, E, lambda it: it * 128)
            materialize(didx_h, wdT_d, dbg.get("wdT"), NET, INTER, lambda it: it * 128)
            P.barrier()
            P.flush()

    if "f" in phases:
        with contextlib.ExitStack() as ph:
            ytT = ph.enter_context(nc.sbuf_tensor("ytT", [128, 8192], BF16))
            y_res = [ph.enter_context(nc.sbuf_tensor(f"yres{i}", [128, E], F32))
                     for i in range(4)]
            wg_i = [ph.enter_context(nc.sbuf_tensor(f"wgi{i}", [128, E], BF16))
                    for i in range(2)]
            wu_i = [ph.enter_context(nc.sbuf_tensor(f"wui{i}", [128, E], BF16))
                    for i in range(2)]
            wd_et = [ph.enter_context(nc.sbuf_tensor(f"wdet{i}", [128, INTER], BF16))
                     for i in range(2)]
            hT = ph.enter_context(nc.sbuf_tensor("hT", [128, NIT * 512], BF16))
            sil = ph.enter_context(nc.sbuf_tensor("sil", [128, 512], F32))
            dnT = ph.enter_context(nc.sbuf_tensor("dnT", [128, 8192], BF16))
            dn_tok = [ph.enter_context(nc.sbuf_tensor(f"dntok{i}", [128, E], BF16))
                      for i in range(4)]
            z2 = ph.enter_context(nc.sbuf_tensor("z2", [128, E], F32))
            z2sq = ph.enter_context(nc.sbuf_tensor("z2sq", [128, E], F32))
            outb = ph.enter_context(nc.sbuf_tensor("outb", [128, E], F32))
            fs1 = ph.enter_context(nc.sbuf_tensor("fs1", [128, 1], F32))
            fs2 = ph.enter_context(nc.sbuf_tensor("fs2", [128, 1], F32))
            fnm = ph.enter_context(nc.sbuf_tensor("fnm", [128, 1], F32))
            fmq = ph.enter_context(nc.sbuf_tensor("fmq", [128, 1], F32))
            fvr = ph.enter_context(nc.sbuf_tensor("fvr", [128, 1], F32))
            frs = ph.enter_context(nc.sbuf_tensor("frs", [128, 1], F32))
            psG_ = [ph.enter_context(nc.psum_tensor(f"psG{i}", [128, 512], F32))
                    for i in range(2)]
            psU_ = [ph.enter_context(nc.psum_tensor(f"psU{i}", [128, 512], F32))
                    for i in range(2)]
            psD = [ph.enter_context(nc.psum_tensor(f"psD{i}", [128, 512], F32))
                   for i in range(2)]
            pstf = ph.enter_context(nc.psum_tensor("pstff", [128, 1024], BF16))

            ytT_free = None
            yres_free = None
            wg_free = [None, None]
            wd_free = [None, None]
            psG_free = [None, None]
            psU_free = [None, None]
            psD_free = [None, None]
            pst_free = None
            sil_free = None
            hT_free = None
            dnT_free = None
            dntok_free = [None] * 4
            z2_free = None
            z2sq_free = None
            outb_free = None
            fstat_free = None

            def load_wgu(it, sl, waits):
                e1 = P.dma("sync", wg_i[sl][:, :], wgT_d[it], waits=waits, sem=f"dfg{sl}")
                e2 = P.dma("sync", wu_i[sl][:, :], wuT_d[it], waits=[], sem=f"dfg{sl}")
                return e2

            wg_ev = [None, None]
            wd_ev = [None, None]
            for tt in range(8):
                ev_yt = P.dma("sync", ytT[:, :].rearrange("p (et k) -> p et k", k=512),
                              yT_d[tt].rearrange("et p k -> p et k"),
                              waits=[ytT_free], sem="dfyt")
                ev_yr = None
                for ts in range(4):
                    ev_yr = P.dma("sync", y_res[ts][:, :],
                                  y_d[tt * 512 + ts * 128: tt * 512 + (ts + 1) * 128, :],
                                  waits=[yres_free] if ts == 0 else [], sem="dfyr")
                wg_ev[0] = load_wgu(0, 0, [wg_free[0]])
                for it in range(NIT):
                    sl = it % 2
                    if it + 1 < NIT:
                        wg_ev[1 - sl] = load_wgu(it + 1, 1 - sl, [wg_free[1 - sl]])
                    psG = psG_[sl]
                    psU = psU_[sl]
                    ev_g = None
                    ev_u = None
                    for ht in range(NET):
                        w = [wg_ev[sl], ev_yt, psG_free[sl]] if ht == 0 else []
                        ev_g = P.emit("tensor", lambda e, psG=psG, sl=sl, ht=ht: e.matmul(
                            psG[:, 0:512], wg_i[sl][:, ht * 128:(ht + 1) * 128],
                            ytT[:, ht * 512:(ht + 1) * 512],
                            start=(ht == 0), stop=(ht == NET - 1)),
                            waits=w, inc="pe" if ht == NET - 1 else None)
                    for ht in range(NET):
                        w = [psU_free[sl]] if ht == 0 else []
                        ev_u = P.emit("tensor", lambda e, psU=psU, sl=sl, ht=ht: e.matmul(
                            psU[:, 0:512], wu_i[sl][:, ht * 128:(ht + 1) * 128],
                            ytT[:, ht * 512:(ht + 1) * 512],
                            start=(ht == 0), stop=(ht == NET - 1)),
                            waits=w, inc="pe" if ht == NET - 1 else None)
                    ev_sl = P.emit("scalar", lambda e, psG=psG: e.activation(
                        sil[:, :], psG[:, 0:512], AF.Silu), waits=[ev_g, sil_free],
                        inc="se")
                    psG_free[sl] = ev_sl
                    ev_h = P.emit("vector", lambda e, psU=psU, it=it: e.tensor_tensor(
                        hT[:, it * 512:(it + 1) * 512], psU[:, 0:512], sil[:, :],
                        ALU.mult), waits=[ev_sl, ev_u, hT_free if it == 0 else None],
                        inc="ve")
                    psU_free[sl] = ev_h
                    sil_free = ev_h
                    wg_free[sl] = P.ev("pe")
                ev_h_all = P.ev("ve")
                wd_ev[0] = P.dma("sync", wd_et[0][:, :], wdT_d[0], waits=[wd_free[0]],
                                 sem="dfd0")
                for et in range(NET):
                    sl = et % 2
                    if et + 1 < NET:
                        wd_ev[1 - sl] = P.dma("sync", wd_et[1 - sl][:, :], wdT_d[et + 1],
                                              waits=[wd_free[1 - sl]], sem=f"dfd{1 - sl}")
                    ps = psD[et % 2]
                    ev_d = None
                    for it in range(NIT):
                        w = [wd_ev[sl], ev_h_all, psD_free[et % 2]] if it == 0 else []
                        ev_d = P.emit("tensor", lambda e, ps=ps, sl=sl, it=it: e.matmul(
                            ps[:, 0:512], wd_et[sl][:, it * 128:(it + 1) * 128],
                            hT[:, it * 512:(it + 1) * 512],
                            start=(it == 0), stop=(it == NIT - 1)),
                            waits=w, inc="pe" if it == NIT - 1 else None)
                    ev_de = P.emit("scalar", lambda e, ps=ps, et=et: e.copy(
                        dnT[:, et * 512:(et + 1) * 512], ps[:, 0:512]),
                        waits=[ev_d, dnT_free if et == 0 else None], inc="se")
                    psD_free[et % 2] = ev_de
                    wd_free[sl] = P.ev("pe")
                ev_dnT = P.ev("se")
                hT_free = P.ev("pe")
                for ts in range(4):
                    for bat in range(2):
                        ev_t = None
                        for j in range(8):
                            et = bat * 8 + j
                            w = [pst_free, ev_dnT, dntok_free[ts] if (bat == 0 and j == 0) else None] if j == 0 else []
                            ev_t = P.emit("tensor", lambda e, et=et, j=j, ts=ts: e.transpose(
                                pstf[:, j * 128:(j + 1) * 128],
                                dnT[:, et * 512 + ts * 128: et * 512 + (ts + 1) * 128],
                                identb[:, :]), waits=w, inc="pe" if j == 7 else None)
                        ev_te = P.emit("scalar", lambda e, ts=ts, bat=bat: e.copy(
                            dn_tok[ts][:, bat * 1024:(bat + 1) * 1024], pstf[:, :]),
                            waits=[ev_t], inc="se")
                        pst_free = ev_te
                    dntok_free[ts] = ev_te
                dnT_free = P.ev("pe")
                for ts in range(4):
                    ev_z2 = P.emit("vector", lambda e, ts=ts: e.tensor_tensor(
                        z2[:, :], dn_tok[ts][:, :], y_res[ts][:, :], ALU.add),
                        waits=[dntok_free[ts], ev_yr, z2_free], inc="ve")
                    ev_sq = P.emit("scalar", lambda e: e.activation(
                        z2sq[:, :], z2[:, :], AF.Square), waits=[ev_z2, z2sq_free],
                        inc="se")
                    ev_s1 = P.emit("vector", lambda e: e.reduce_sum(
                        fs1[:, :], z2[:, :], axis=AX.X), waits=[fstat_free], inc="ve")
                    ev_s2 = P.emit("vector", lambda e: e.reduce_sum(
                        fs2[:, :], z2sq[:, :], axis=AX.X), waits=[ev_sq], inc="ve")
                    ev_nm = P.emit("vector", lambda e: e.tensor_scalar(
                        fnm[:, :], fs1[:, :], -1.0 / E, None, ALU.mult),
                        waits=[ev_s1], inc="ve")
                    ev_mq = P.emit("vector", lambda e: e.tensor_tensor(
                        fmq[:, :], fnm[:, :], fnm[:, :], ALU.mult), waits=[ev_nm],
                        inc="ve")
                    ev_vr = P.emit("vector", lambda e: e.scalar_tensor_tensor(
                        fvr[:, :], fs2[:, :], 1.0 / E, fmq[:, :], ALU.mult,
                        ALU.subtract), waits=[ev_mq, ev_s2], inc="ve")
                    ev_lv = P.emit("scalar", lambda e: e.activation(
                        frs[:, :], fvr[:, :], AF.Ln, bias=LN_EPS), waits=[ev_vr],
                        inc="se")
                    ev_rs = P.emit("scalar", lambda e: e.activation(
                        frs[:, :], frs[:, :], AF.Exp, scale=-0.5), waits=[ev_lv],
                        inc="se")
                    ev_o0 = P.emit("vector", lambda e: e.tensor_scalar(
                        z2sq[:, :], z2[:, :], fnm[:, :], None, ALU.add),
                        waits=[ev_rs, ev_s2], inc="ve")
                    ev_o = P.emit("vector", lambda e: e.tensor_scalar(
                        outb[:, :], z2sq[:, :], frs[:, :], None, ALU.mult),
                        waits=[ev_o0, outb_free], inc="ve")
                    z2_free = ev_o
                    z2sq_free = ev_s2
                    fstat_free = ev_o
                    t0 = tt * 512 + ts * 128
                    outb_free = P.dma("sync", out_h[t0:t0 + 128, :], outb[:, :],
                                      waits=[ev_o], sem="dout")
                yres_free = P.ev("ve")
                ytT_free = P.ev("pe")
            P.barrier()
            P.flush()

    P.barrier()
    P.flush()
    stk.close()
    return nc


def _reorder_wq(wq):
    w = wq.reshape(E, HD, QH)
    nope = w[:, :, :NOPE].reshape(E, HD * NOPE)
    pe = w[:, :, NOPE:].reshape(E, HD, 2, 32)
    pe = pe.reshape(E, 8, 2, 2, 32)
    pe = pe.transpose(0, 1, 3, 2, 4).reshape(E, 8 * 128)
    return np.ascontiguousarray(np.concatenate([nope, pe], 1))


def _shard_inputs(inputs):
    x = np.ascontiguousarray(inputs["L0_attn_x"])
    consts = make_consts()
    gamma = np.ascontiguousarray(
        np.asarray(inputs["L0_attn_kv_norm_scale"], dtype=np.float32).reshape(4, 128).T)
    common = {
        "wq": _reorder_wq(np.asarray(inputs["L0_attn_wq"], dtype=np.float32)),
        "wkva": np.ascontiguousarray(inputs["L0_attn_wkv_a"], dtype=np.float32),
        "wkvb": np.ascontiguousarray(inputs["L0_attn_wkv_b"], dtype=np.float32),
        "wo": np.ascontiguousarray(inputs["L0_attn_wo"], dtype=np.float32),
        "gamma": gamma,
        "lut": np.asarray(inputs["L0_ffn_lut"], dtype=np.float32).reshape(1, 16),
        "gidx": np.ascontiguousarray(inputs["L0_ffn_gate_idx"], dtype=np.int32),
        "uidx": np.ascontiguousarray(inputs["L0_ffn_up_idx"], dtype=np.int32),
        "didx": np.ascontiguousarray(inputs["L0_ffn_down_idx"], dtype=np.int32),
        **consts,
    }
    in_maps = []
    for i in range(B):
        m = dict(common)
        m["x"] = np.ascontiguousarray(x[i], dtype=np.float32)
        in_maps.append(m)
    return in_maps


def kernel(**inputs):
    in_maps = _shard_inputs(inputs)
    nc = build_program(debug=DEBUG, phases=PHASES)
    res = run_bass_kernel_spmd(nc, in_maps, core_ids=list(range(B)))
    out = np.stack([np.asarray(r["out"], dtype=np.float32) for r in res.results], 0)
    return out


# version 18 (final)
# speedup vs baseline: 1.0701x; 1.0701x over previous
import contextlib
import math
from collections import defaultdict

import numpy as np
import ml_dtypes

import concourse.bass as bass
import concourse.mybir as mybir
from concourse.bass_utils import run_bass_kernel_spmd

F32 = mybir.dt.float32
BF16 = mybir.dt.bfloat16
I32 = mybir.dt.int32
AF = mybir.ActivationFunctionType
ALU = mybir.AluOpType
AX = mybir.AxisListType

B = 8
T = 4096
E = 2048
HD = 16
NOPE, RD, VD = 128, 64, 128
QH = NOPE + RD
LORA = 512
INTER = 5504
NIT = INTER // 128
NET = E // 128
SCALE = QH ** -0.5
RMS_EPS = 1e-6
LN_EPS = 1e-5
ROPE_THETA = 10000.0
NEGBIG = -30000.0

ENGINES = ("tensor", "vector", "scalar", "gpsimd", "sync")

DEBUG = False
PHASES = ("p0a", "p0c", "p0b", "p1", "p2", "fw", "f")


class Prog:

    def __init__(self, nc):
        self.nc = nc
        self.ctx = contextlib.ExitStack()
        self.sems = {}
        self.count = defaultdict(int)
        self.ops = defaultdict(list)
        self.last_wait = defaultdict(dict)

    def sem(self, name):
        if name not in self.sems:
            self.sems[name] = self.ctx.enter_context(self.nc.semaphore(name))
        return self.sems[name]

    def ev(self, sem_name):
        return (sem_name, self.count[sem_name])

    def emit(self, eng, fn, waits=(), inc=None, amt=1):
        after = None
        if inc is not None:
            self.sem(inc)
            self.count[inc] += amt
            after = (inc, self.count[inc])
        w = []
        for x in waits:
            if x is None:
                continue
            (s, v) = x
            if v <= 0:
                continue
            self.sem(s)
            assert v <= self.count[s], f"wait on future event {s}={v} (@{self.count[s]})"
            if self.last_wait[eng].get(s, 0) < v:
                self.last_wait[eng][s] = v
                w.append((s, v))
        self.ops[eng].append((fn, w, (inc, amt) if inc else None))
        return after

    def dma(self, queue, out, in_, waits=(), sem=None):
        sem = sem or ("d_" + queue)
        return self.emit(queue, lambda e, o=out, i=in_: e.dma_start(o, i),
                         waits=waits, inc=sem, amt=16)

    def flush(self):
        with self.nc.Block() as block:
            for eng in ENGINES:
                ops = self.ops[eng]
                if not ops:
                    continue

                def body(e, ops=list(ops)):
                    for (fn, w, inc) in ops:
                        for (s, v) in w:
                            e.wait_ge(self.sems[s], v)
                        inst = fn(e)
                        if inc is not None:
                            inst.then_inc(self.sems[inc[0]], inc[1])

                getattr(block, eng)(body)
                self.ops[eng] = []

    def barrier(self):
        evs = [(s, c) for s, c in self.count.items() if c > 0]
        for eng in ENGINES:
            w = []
            for (s, v) in evs:
                if self.last_wait[eng].get(s, 0) < v:
                    self.last_wait[eng][s] = v
                    w.append((s, v))
            if w:
                self.ops[eng].append(((lambda e: None), w, None))



def make_consts():
    c = {}
    ident = np.eye(128, dtype=np.float32)
    c["identb"] = ident
    c["identf"] = ident
    c["negI"] = NEGBIG * ident
    c["onesb"] = np.ones((128, 1), dtype=np.float32)
    c["onesf_col"] = np.ones((128, 1), dtype=np.float32)
    c["onesf_row"] = np.ones((1, 128), dtype=np.float32)
    inv = ROPE_THETA ** (-np.arange(0, RD, 2, dtype=np.float32) / RD)
    ang = np.arange(T, dtype=np.float32)[:, None] * inv[None, :]
    cosT = np.cos(ang).T.astype(np.float32)
    sinT = np.sin(ang).T.astype(np.float32)
    c["cos2"] = np.concatenate([cosT, cosT], 0)
    c["sin2"] = np.concatenate([sinT, sinT], 0)
    r = np.arange(128)[:, None]
    col = np.arange(512)[None, :]
    s1 = np.stack([(col > 128 * v + r) for v in range(4)], 0)
    s2 = np.stack([(col < 128 * d + r) for d in range(4)], 0)
    c["step1"] = s1.astype(np.float32).reshape(4 * 128, 512)
    c["step2"] = s2.astype(np.float32).reshape(4 * 128, 512)
    return c



def build_program(debug=False, phases=PHASES):
    nc = bass.Bass("TRN2", target_bir_lowering=False)

    H = {}
    def din(name, shape, dt):
        H[name] = nc.dram_tensor(name, shape, dt, kind="ExternalInput").ap()
        return H[name]

    x_h = din("x", [T, E], F32)
    wq_h = din("wq", [E, HD * QH], F32)
    wkva_h = din("wkva", [E, LORA + RD], F32)
    wkvb_h = din("wkvb", [LORA, HD * (NOPE + VD)], F32)
    wo_h = din("wo", [HD * VD, E], F32)
    gamma_h = din("gamma", [128, 4], F32)
    lut_h = din("lut", [1, 16], F32)
    gidx_h = din("gidx", [INTER, E], I32)
    uidx_h = din("uidx", [INTER, E], I32)
    didx_h = din("didx", [E, INTER], I32)
    identb_h = din("identb", [128, 128], F32)
    identf_h = din("identf", [128, 128], F32)
    negI_h = din("negI", [128, 128], F32)
    onesb_h = din("onesb", [128, 1], F32)
    onesf_col_h = din("onesf_col", [128, 1], F32)
    onesf_row_h = din("onesf_row", [1, 128], F32)
    cos2_h = din("cos2", [64, T], F32)
    sin2_h = din("sin2", [64, T], F32)
    step1_h = din("step1", [4 * 128, 512], F32)
    step2_h = din("step2", [4 * 128, 512], F32)

    out_h = nc.dram_tensor("out", [T, E], F32, kind="ExternalOutput").ap()

    q_flat = nc.dram_tensor("q_flat", [HD * QH, T], BF16, kind="Internal").ap()
    ks1_d = nc.dram_tensor("ks1_d", [HD * 128, T], BF16, kind="Internal").ap()
    vtok_d = nc.dram_tensor("vtok_d", [HD * 128, T], BF16, kind="Internal").ap()
    o_d = nc.dram_tensor("o_d", [HD * VD, T], BF16, kind="Internal").ap()
    y_d = nc.dram_tensor("y_d", [T, E], F32, kind="Internal").ap()
    yT_d = nc.dram_tensor("yT_d", [8, NET, 128, 512], BF16, kind="Internal").ap()
    wgT_d = nc.dram_tensor("wgT_d", [NIT, 128, E], BF16, kind="Internal").ap()
    wuT_d = nc.dram_tensor("wuT_d", [NIT, 128, E], BF16, kind="Internal").ap()
    wdT_d = nc.dram_tensor("wdT_d", [NET, 128, INTER], BF16, kind="Internal").ap()

    dbg = {}
    if debug:
        def dout(name, shape, dt):
            dbg[name] = nc.dram_tensor("dbg_" + name, shape, dt, kind="ExternalOutput").ap()
        dout("ckv", [LORA, T], F32)
        dout("kpe", [66, T], F32)
        dout("q", [HD * QH, T], F32)
        dout("ks1", [HD * 128, T], F32)
        dout("vtok", [HD * 128, T], F32)
        dout("o", [HD * VD, T], F32)
        dout("y", [T, E], F32)
        dout("wgT", [NIT, 128, E], F32)
        dout("wdT", [NET, 128, INTER], F32)
        dout("z", [T, E], F32)

    P = Prog(nc)
    stk = contextlib.ExitStack()

    identb = stk.enter_context(nc.sbuf_tensor("s_identb", [128, 128], BF16))
    identf = stk.enter_context(nc.sbuf_tensor("s_identf", [128, 128], F32))
    negI = stk.enter_context(nc.sbuf_tensor("s_negI", [128, 128], BF16))
    onesb = stk.enter_context(nc.sbuf_tensor("s_onesb", [128, 1], BF16))
    onesf_col = stk.enter_context(nc.sbuf_tensor("s_onesf_col", [128, 1], F32))
    onesf_row = stk.enter_context(nc.sbuf_tensor("s_onesf_row", [1, 128], F32))
    step1 = stk.enter_context(nc.sbuf_tensor("s_step1", [128, 4 * 512], BF16))
    step2 = stk.enter_context(nc.sbuf_tensor("s_step2", [128, 4 * 512], BF16))
    ks2 = stk.enter_context(nc.sbuf_tensor("ks2", [66, T], BF16))
    gamma = stk.enter_context(nc.sbuf_tensor("s_gamma", [128, 4], F32))
    lut_col = stk.enter_context(nc.sbuf_tensor("s_lut_col", [128, 16], F32))

    for (t, h) in [(identf, identf_h), (onesf_col, onesf_col_h),
                   (onesf_row, onesf_row_h), (gamma, gamma_h)]:
        P.dma("sync", t[:, :], h)
    for (t, h) in [(identb, identb_h), (negI, negI_h), (onesb, onesb_h)]:
        P.dma("gpsimd", t[:, :], h)
    P.dma("gpsimd", step1[:, :].rearrange("p (v c) -> p v c", c=512),
          step1_h.rearrange("(v p) c -> p v c", p=128))
    P.dma("gpsimd", step2[:, :].rearrange("p (v c) -> p v c", c=512),
          step2_h.rearrange("(v p) c -> p v c", p=128))
    P.dma("gpsimd", lut_col[:, :], lut_h.partition_broadcast(128))
    P.emit("vector", lambda e: e.memset(ks2[64:66, :], 1.0), inc="ve")
    const_pool = stk.enter_context(nc.sbuf_tensor("s_constpool", [128, 8], F32))
    for j, val in enumerate([RMS_EPS, LN_EPS, 1e-20]):
        P.emit("vector", lambda e, j=j, val=val: e.memset(const_pool[:, j:j + 1], val),
               inc="ve")
        nc.const_aps.register(F32, val, const_pool[:, j:j + 1])
    P.barrier()

    cs_stk = contextlib.ExitStack()
    cos2 = cs_stk.enter_context(nc.sbuf_tensor("s_cos2", [64, T], F32))
    sin2 = cs_stk.enter_context(nc.sbuf_tensor("s_sin2", [64, T], F32))
    P.dma("sync", cos2[:, :], cos2_h)
    P.dma("sync", sin2[:, :], sin2_h)
    cs_ev = P.ev("d_sync")
    ckv_stk = contextlib.ExitStack()
    ckv_n = [ckv_stk.enter_context(nc.sbuf_tensor(f"ckvn{i}", [128, T], BF16))
             for i in range(4)]

    def emit_x_chunk_transposes(tc, x_in, xTb, pst, pst_free, xin_free, xT_free):
        t0 = tc * 512
        x_ev = None
        for ts in range(4):
            x_ev = P.dma("gpsimd", x_in[ts][:, :],
                         x_h[t0 + ts * 128: t0 + (ts + 1) * 128, :],
                         waits=[xin_free[ts]])
        xT_ready = None
        for bat in range(8):
            ts, half = divmod(bat, 2)
            pb = pst[bat % 2]
            ev_t = None
            for j in range(8):
                et = half * 8 + j
                w = [pst_free[bat % 2], x_ev, xT_free] if j == 0 else []
                ev_t = P.emit("tensor",
                              lambda e, pb=pb, xx=x_in[ts], et=et, j=j:
                              e.transpose(pb[:, j * 128:(j + 1) * 128],
                                          xx[:, et * 128:(et + 1) * 128],
                                          identb[:, :]),
                              waits=w, inc="pe" if j == 7 else None)
            out_ap = xTb[:, :].rearrange("p (et f) -> p et f", f=512)[
                :, half * 8:(half + 1) * 8, ts * 128:(ts + 1) * 128]
            in_ap = pb[:, :].rearrange("p (j f) -> p j f", f=128)
            ev_e = P.emit("scalar",
                          lambda e, oa=out_ap, ia=in_ap: e.copy(oa, ia),
                          waits=[ev_t], inc="se")
            pst_free[bat % 2] = ev_e
            xT_ready = ev_e
        return xT_ready, x_ev

    if "p0a" in phases:
        with contextlib.ExitStack() as ph:
            wkva = [ph.enter_context(nc.sbuf_tensor(f"wkva{i}", [128, LORA + RD], BF16))
                    for i in range(NET)]
            x_in = [ph.enter_context(nc.sbuf_tensor(f"xin{i}", [128, E], BF16))
                    for i in range(4)]
            xTb = ph.enter_context(nc.sbuf_tensor("xTb", [128, 8192], BF16))
            ckv_f = ph.enter_context(nc.sbuf_tensor("ckv_f", [128, 4 * 512], F32))
            sq = ph.enter_context(nc.sbuf_tensor("sq", [128, 512], F32))
            ropescr_a = ph.enter_context(nc.sbuf_tensor("ropescr_a", [32, 512], F32))
            ropescr_b = ph.enter_context(nc.sbuf_tensor("ropescr_b", [32, 512], F32))
            rmsrow = ph.enter_context(nc.sbuf_tensor("rmsrow", [1, 512], F32))

            pst = [ph.enter_context(nc.psum_tensor(f"pst{i}", [128, 1024], BF16))
                   for i in range(2)]
            psA = ph.enter_context(nc.psum_tensor("psA", [128, 512], F32))
            psB = ph.enter_context(nc.psum_tensor("psB", [128, 512], F32))
            psR = ph.enter_context(nc.psum_tensor("psR", [128, 512], F32))
            psBC = ph.enter_context(nc.psum_tensor("psBC", [128, 512], F32))

            wk_ev = None
            for i in range(NET):
                wk_ev = P.dma("gpsimd", wkva[i][:, :],
                              wkva_h[i * 128:(i + 1) * 128, :])

            xin_free = [None] * 4
            xT_free = None
            pst_free = [None, None]
            psAB_free = [None, None]
            psR_free = None
            psBC_free = None
            sq_free = None
            ckvf_free = None
            for tc in range(8):
                xT_ready, x_ev = emit_x_chunk_transposes(
                    tc, x_in, xTb, pst, pst_free, xin_free, xT_free)
                ev_r = None
                for mc in range(5):
                    ps = psA if mc % 2 == 0 else psB
                    mw = 128 if mc < 4 else 64
                    ev_mm = None
                    for et in range(NET):
                        w = [psAB_free[mc % 2], xT_ready] if et == 0 else []
                        ev_mm = P.emit("tensor",
                                       lambda e, ps=ps, et=et, mc=mc, mw=mw:
                                       e.matmul(ps[0:mw, 0:512],
                                                wkva[et][:, mc * 128: mc * 128 + mw],
                                                xTb[:, et * 512:(et + 1) * 512],
                                                start=(et == 0), stop=(et == NET - 1)),
                                       waits=w, inc="pe" if et == NET - 1 else None)
                    if mc < 4:
                        ev_cp = P.emit("scalar",
                                       lambda e, ps=ps, mc=mc:
                                       e.copy(ckv_f[:, mc * 512:(mc + 1) * 512], ps[:, 0:512]),
                                       waits=[ev_mm, ckvf_free], inc="se")
                        psAB_free[mc % 2] = ev_cp
                        ev_sq = P.emit("scalar",
                                       lambda e, mc=mc:
                                       e.activation(sq[:, :],
                                                    ckv_f[:, mc * 512:(mc + 1) * 512],
                                                    AF.Square),
                                       waits=[sq_free], inc="se")
                        ev_r = P.emit("tensor",
                                      lambda e, mc=mc:
                                      e.matmul(psR[0:1, 0:512], onesf_col[:, :],
                                               sq[:, :], start=(mc == 0), stop=(mc == 3)),
                                      waits=[ev_sq] + ([psR_free] if mc == 0 else []),
                                      inc="pe")
                        sq_free = ev_r
                    else:
                        w = [ev_mm, cs_ev]
                        t0 = tc * 512
                        x1 = ps[0:32, 0:512]
                        x2 = ps[32:64, 0:512]
                        cS = cos2[0:32, t0:t0 + 512]
                        sS = sin2[0:32, t0:t0 + 512]
                        a = ropescr_a[:, :]
                        b = ropescr_b[:, :]
                        P.emit("vector", lambda e, x1=x1, cS=cS, a=a:
                               e.tensor_tensor(a, x1, cS, ALU.mult), waits=w)
                        P.emit("vector", lambda e, x2=x2, sS=sS, b=b:
                               e.tensor_tensor(b, x2, sS, ALU.mult))
                        P.emit("vector", lambda e, a=a, b=b, t0=t0:
                               e.tensor_tensor(ks2[0:32, t0:t0 + 512], a, b, ALU.subtract))
                        P.emit("vector", lambda e, x2=x2, cS=cS, a=a:
                               e.tensor_tensor(a, x2, cS, ALU.mult))
                        P.emit("vector", lambda e, x1=x1, sS=sS, b=b:
                               e.tensor_tensor(b, x1, sS, ALU.mult))
                        ev_kpe = P.emit("vector", lambda e, a=a, b=b, t0=t0:
                                        e.tensor_tensor(ks2[32:64, t0:t0 + 512], a, b, ALU.add),
                                        inc="ve")
                        psAB_free[mc % 2] = ev_kpe
                ev_ln = P.emit("scalar",
                               lambda e: e.activation(rmsrow[:, :], psR[0:1, 0:512],
                                                      AF.Ln, bias=RMS_EPS, scale=1.0 / LORA),
                               waits=[ev_r], inc="se")
                psR_free = ev_ln
                ev_rstd = P.emit("scalar",
                                 lambda e: e.activation(rmsrow[:, :], rmsrow[:, :],
                                                        AF.Exp, scale=-0.5),
                                 waits=[], inc="se")
                ev_bc = P.emit("tensor",
                               lambda e: e.matmul(psBC[:, 0:512], onesf_row[:, :],
                                                  rmsrow[:, :], start=True, stop=True),
                               waits=[ev_rstd, psBC_free], inc="pe")
                ev_cn = None
                for mc in range(4):
                    ev_cn = P.emit("vector",
                                   lambda e, mc=mc, tc=tc:
                                   e.scalar_tensor_tensor(
                                       ckv_n[mc][:, tc * 512:(tc + 1) * 512],
                                       psBC[:, 0:512], gamma[:, mc:mc + 1],
                                       ckv_f[:, mc * 512:(mc + 1) * 512],
                                       ALU.mult, ALU.mult),
                                   waits=[ev_bc] if mc == 0 else [],
                                   inc="ve" if mc == 3 else None)
                psBC_free = ev_cn
                ckvf_free = ev_cn
                xin_free = [xT_ready] * 4
                xT_free = P.ev("pe")
            if debug:
                wv = [P.ev("ve"), P.ev("se"), P.ev("pe")]
                for mc in range(4):
                    P.dma("gpsimd", dbg["ckv"][mc * 128:(mc + 1) * 128, :],
                          ckv_n[mc][:, :], waits=wv)
                P.dma("gpsimd", dbg["kpe"], ks2[:, :], waits=wv)
            P.barrier()
            P.flush()

    if "p0c" in phases:
        with contextlib.ExitStack() as ph:
            wkvb = [ph.enter_context(nc.sbuf_tensor(f"wkvb{i}", [128, HD * 256], BF16))
                    for i in range(4)]
            ks1buf = [ph.enter_context(nc.sbuf_tensor(f"ks1b{i}", [128, T], BF16))
                      for i in range(2)]
            vbuf = [ph.enter_context(nc.sbuf_tensor(f"vb{i}", [128, T], BF16))
                    for i in range(2)]
            vtkbuf = [ph.enter_context(nc.sbuf_tensor(f"vtk{i}", [128, T], BF16))
                      for i in range(2)]
            psK = [ph.enter_context(nc.psum_tensor(f"psK{i}", [128, 512], F32))
                   for i in range(2)]
            pst = [ph.enter_context(nc.psum_tensor(f"pstc{i}", [128, 1024], BF16))
                   for i in range(2)]

            wb_ev = None
            for i in range(4):
                wb_ev = P.dma("gpsimd", wkvb[i][:, :],
                              wkvb_h[i * 128:(i + 1) * 128, :])
            psK_free = [None, None]
            pst_free = [None, None]
            buf_free = [[None, None], [None, None], [None, None]]
            for h in range(HD):
                sl = h % 2
                for part in range(2):
                    col0 = h * 256 + part * 128
                    dstb = ks1buf[sl] if part == 0 else vbuf[sl]
                    ev_ev = None
                    for tcc in range(8):
                        ps = psK[tcc % 2]
                        ev_mm = None
                        for lt in range(4):
                            w = []
                            if lt == 0:
                                w = [psK_free[tcc % 2], wb_ev,
                                     buf_free[part][sl] if tcc == 0 else None]
                            ev_mm = P.emit("tensor",
                                           lambda e, ps=ps, lt=lt, col0=col0, tcc=tcc:
                                           e.matmul(ps[:, 0:512],
                                                    wkvb[lt][:, col0:col0 + 128],
                                                    ckv_n[lt][:, tcc * 512:(tcc + 1) * 512],
                                                    start=(lt == 0), stop=(lt == 3)),
                                           waits=w, inc="pe" if lt == 3 else None)
                        ev_ev = P.emit("scalar",
                                       lambda e, ps=ps, dstb=dstb, tcc=tcc:
                                       e.copy(dstb[:, tcc * 512:(tcc + 1) * 512],
                                              ps[:, 0:512]),
                                       waits=[ev_mm], inc="se")
                        psK_free[tcc % 2] = ev_ev
                    if part == 0:
                        ev_sp = P.dma("sync", ks1_d[h * 128:(h + 1) * 128, :],
                                      ks1buf[sl][:, :], waits=[ev_ev],
                                      sem=f"dspk{sl}")
                        buf_free[0][sl] = ev_sp
                        if debug:
                            buf_free[0][sl] = P.dma(
                                "gpsimd", dbg["ks1"][h * 128:(h + 1) * 128, :],
                                ks1buf[sl][:, :], waits=[ev_ev], sem=f"dbk{sl}")
                    else:
                        ev_tk = None
                        for b8 in range(4):
                            pb = pst[b8 % 2]
                            ev_t = None
                            for j in range(8):
                                kt = b8 * 8 + j
                                w = [pst_free[b8 % 2], ev_ev,
                                     buf_free[2][sl] if b8 == 0 else None] if j == 0 else []
                                ev_t = P.emit("tensor",
                                              lambda e, pb=pb, vv=vbuf[sl], kt=kt, j=j:
                                              e.transpose(pb[:, j * 128:(j + 1) * 128],
                                                          vv[:, kt * 128:(kt + 1) * 128],
                                                          identb[:, :]),
                                              waits=w, inc="pe" if j == 7 else None)
                            ev_tk = P.emit("scalar",
                                           lambda e, pb=pb, dst=vtkbuf[sl], b8=b8:
                                           e.copy(dst[:, b8 * 1024:(b8 + 1) * 1024],
                                                  pb[:, :]),
                                           waits=[ev_t], inc="se")
                            pst_free[b8 % 2] = ev_tk
                        ev_sp = P.dma("sync", vtok_d[h * 128:(h + 1) * 128, :],
                                      vtkbuf[sl][:, :], waits=[ev_tk],
                                      sem=f"dspv{sl}")
                        buf_free[2][sl] = ev_sp
                        buf_free[1][sl] = P.ev("pe")
                        if debug:
                            buf_free[2][sl] = P.dma(
                                "gpsimd", dbg["vtok"][h * 128:(h + 1) * 128, :],
                                vtkbuf[sl][:, :], waits=[ev_tk], sem=f"dbv{sl}")
            P.barrier()
            P.flush()
    ckv_stk.close()

    if "p0b" in phases:
        with contextlib.ExitStack() as ph:
            wqb = [ph.enter_context(nc.sbuf_tensor(f"wqb{i}", [128, HD * QH], BF16))
                   for i in range(NET)]
            x_in = [ph.enter_context(nc.sbuf_tensor(f"xinb{i}", [128, E], BF16))
                    for i in range(4)]
            xTb = ph.enter_context(nc.sbuf_tensor("xTbb", [128, 8192], BF16))
            qev = [ph.enter_context(nc.sbuf_tensor(f"qev{i}", [128, 512], BF16))
                   for i in range(2)]
            rs_a = ph.enter_context(nc.sbuf_tensor("rs_a", [64, 512], F32))
            rs_b = ph.enter_context(nc.sbuf_tensor("rs_b", [64, 512], F32))
            ropeo = [ph.enter_context(nc.sbuf_tensor(f"ropeo{i}", [128, 512], BF16))
                     for i in range(2)]
            pst = [ph.enter_context(nc.psum_tensor(f"pstb{i}", [128, 1024], BF16))
                   for i in range(2)]
            psQ = [ph.enter_context(nc.psum_tensor(f"psQ{i}", [128, 512], F32))
                   for i in range(2)]

            wq_ev = None
            for i in range(NET):
                wq_ev = P.dma("gpsimd", wqb[i][:, :],
                              wq_h[i * 128:(i + 1) * 128, :])
            xin_free = [None] * 4
            xT_free = None
            pst_free = [None, None]
            psQ_free = [None, None]
            qev_free = [None, None]
            ropeo_free = [None, None]
            for tc in range(8):
                t0 = tc * 512
                xT_ready, x_ev = emit_x_chunk_transposes(
                    tc, x_in, xTb, pst, pst_free, xin_free, xT_free)
                for m in range(24):
                    ps = psQ[m % 2]
                    ev_mm = None
                    for et in range(NET):
                        w = [psQ_free[m % 2], xT_ready, wq_ev] if et == 0 else []
                        ev_mm = P.emit("tensor",
                                       lambda e, ps=ps, et=et, m=m:
                                       e.matmul(ps[:, 0:512],
                                                wqb[et][:, m * 128:(m + 1) * 128],
                                                xTb[:, et * 512:(et + 1) * 512],
                                                start=(et == 0), stop=(et == NET - 1)),
                                       waits=w, inc="pe" if et == NET - 1 else None)
                    if m < 16:
                        ev_c = P.emit("scalar",
                                      lambda e, ps=ps, m=m, sl=m % 2:
                                      e.copy(qev[sl][:, :], ps[:, 0:512]),
                                      waits=[ev_mm, qev_free[m % 2]], inc="se")
                        psQ_free[m % 2] = ev_c
                        ev_sp = P.dma("sync", q_flat[m * QH: m * QH + 128, t0:t0 + 512],
                                      qev[m % 2][:, :], waits=[ev_c],
                                      sem=f"dspq{m % 2}")
                        qev_free[m % 2] = ev_sp
                        if debug:
                            qev_free[m % 2] = P.dma(
                                "gpsimd", dbg["q"][m * QH: m * QH + 128, t0:t0 + 512],
                                qev[m % 2][:, :], waits=[ev_c], sem=f"dbq{m % 2}")
                    else:
                        j = m - 16
                        x1 = ps[0:64, 0:512]
                        x2 = ps[64:128, 0:512]
                        cS = cos2[:, t0:t0 + 512]
                        sS = sin2[:, t0:t0 + 512]
                        ro = ropeo[j % 2]
                        w0 = [ev_mm, ropeo_free[j % 2]]
                        P.emit("vector", lambda e, x1=x1, cS=cS, a=rs_a:
                               e.tensor_tensor(a[:, :], x1, cS, ALU.mult), waits=w0)
                        P.emit("vector", lambda e, x2=x2, sS=sS, b=rs_b:
                               e.tensor_tensor(b[:, :], x2, sS, ALU.mult))
                        P.emit("vector", lambda e, ro=ro:
                               e.tensor_tensor(ro[0:64, :], rs_a[:, :], rs_b[:, :],
                                               ALU.subtract))
                        P.emit("vector", lambda e, x2=x2, cS=cS, a=rs_a:
                               e.tensor_tensor(a[:, :], x2, cS, ALU.mult))
                        P.emit("vector", lambda e, x1=x1, sS=sS, b=rs_b:
                               e.tensor_tensor(b[:, :], x1, sS, ALU.mult))
                        ev_ro = P.emit("vector", lambda e, ro=ro:
                                       e.tensor_tensor(ro[64:128, :], rs_a[:, :], rs_b[:, :],
                                                       ALU.add), inc="ve")
                        psQ_free[m % 2] = ev_ro
                        for two in range(2):
                            hh = 2 * j + two
                            for half in range(2):
                                in_ap = ro[half * 64 + two * 32: half * 64 + two * 32 + 32, :]
                                r0 = hh * QH + 128 + half * 32
                                ropeo_free[j % 2] = P.dma(
                                    "sync", q_flat[r0:r0 + 32, t0:t0 + 512],
                                    in_ap, waits=[ev_ro], sem=f"dspr{j % 2}")
                                if debug:
                                    ropeo_free[j % 2] = P.dma(
                                        "gpsimd", dbg["q"][r0:r0 + 32, t0:t0 + 512],
                                        in_ap, waits=[ev_ro], sem=f"dbr{j % 2}")
                xin_free = [xT_ready] * 4
                xT_free = P.ev("pe")
            P.barrier()
            P.flush()


    cs_stk.close()

    if "p1" in phases:
        with contextlib.ExitStack() as ph:
            qs1 = [ph.enter_context(nc.sbuf_tensor(f"qs1_{i}", [128, T], BF16)) for i in range(2)]
            qs2 = [ph.enter_context(nc.sbuf_tensor(f"qs2_{i}", [66, T], BF16)) for i in range(2)]
            ks1 = [ph.enter_context(nc.sbuf_tensor(f"ks1p_{i}", [128, T], BF16)) for i in range(2)]
            vtk = [ph.enter_context(nc.sbuf_tensor(f"vtkp_{i}", [128, T], BF16)) for i in range(2)]
            sqb = ph.enter_context(nc.sbuf_tensor("sqb", [128, T], BF16))
            sq2b = ph.enter_context(nc.sbuf_tensor("sq2b", [64, T], BF16))
            nk2sl = [ph.enter_context(nc.sbuf_tensor(f"nk2_{i}", [1, T], F32))
                     for i in range(2)]
            nk2pe = ph.enter_context(nc.sbuf_tensor("nk2pe", [1, T], BF16))
            pmax = ph.enter_context(nc.sbuf_tensor("pmax", [1, T], F32))
            rs_row_t = ph.enter_context(nc.sbuf_tensor("rs_row", [1, 512], F32))
            rs_row = [rs_row_t, rs_row_t]
            rSb = [ph.enter_context(nc.sbuf_tensor(f"rSb{i}", [128, 512], F32))
                   for i in range(2)]
            pT = [ph.enter_context(nc.sbuf_tensor(f"pT{i}", [128, 512], BF16)) for i in range(2)]
            ob = [ph.enter_context(nc.sbuf_tensor(f"ob{i}", [128, 512], BF16)) for i in range(2)]

            psST = [ph.enter_context(nc.psum_tensor(f"psST_{i}", [128, 512], F32)) for i in range(2)]
            psAV_ = [ph.enter_context(nc.psum_tensor(f"psAV{i}", [128, 512], F32)) for i in range(2)]
            psS_t = ph.enter_context(nc.psum_tensor("psS0", [1, 512], F32))
            psS = [psS_t, psS_t]
            psB = ph.enter_context(nc.psum_tensor("psBn", [128, 512], F32))
            psN = ph.enter_context(nc.psum_tensor("psNn", [1, 512], F32))
            fw_inline = ("fw" in phases)
            if fw_inline:
                fwi32_t = ph.enter_context(nc.sbuf_tensor("fwi32", [128, E], I32))
                fwi32 = [fwi32_t, fwi32_t]
                fwidxb = ph.enter_context(nc.sbuf_tensor("fwidxb", [128, E], BF16))
                fwidxT_ = [ph.enter_context(nc.sbuf_tensor(f"fwidxT{i}", [128, E], F32))
                           for i in range(2)]
                fwmask = ph.enter_context(nc.sbuf_tensor("fwmask", [128, E], BF16))
                fwtmp = ph.enter_context(nc.sbuf_tensor("fwtmp", [128, E], BF16))
                fwacc = [ph.enter_context(nc.sbuf_tensor(f"fwacc{i}", [128, E], BF16))
                         for i in range(2)]
                psFT = ph.enter_context(nc.psum_tensor("psFT", [128, 1024], BF16))

            ev0 = P.emit("scalar", lambda e: e.activation(sq2b[:, :], ks2[0:64, :], AF.Square),
                         inc="se")
            psN_free = None
            for c8 in range(8):
                ev_m = P.emit("tensor",
                              lambda e, c8=c8: e.matmul(psN[0:1, 0:512], onesb[0:64, :],
                                                        sq2b[:, c8 * 512:(c8 + 1) * 512],
                                                        start=True, stop=True),
                              waits=[ev0, psN_free], inc="pe")
                psN_free = P.emit("vector",
                                  lambda e, c8=c8: e.tensor_copy(
                                      nk2pe[0:1, c8 * 512:(c8 + 1) * 512], psN[0:1, 0:512]),
                                  waits=[ev_m], inc="ve")

            def load_head(h, slot, waits):
                e1 = P.dma("sync", qs1[slot][:, :], q_flat[h * QH: h * QH + 128, :],
                           waits=waits, sem=f"dldq{slot}")
                P.dma("sync", qs2[slot][0:64, :], q_flat[h * QH + 128: h * QH + QH, :],
                      waits=[], sem=f"dldq{slot}")
                P.dma("sync", ks1[slot][:, :], ks1_d[h * 128:(h + 1) * 128, :],
                      waits=[], sem=f"dldq{slot}")
                ev = P.dma("sync", vtk[slot][:, :], vtok_d[h * 128:(h + 1) * 128, :],
                           waits=[], sem=f"dldq{slot}")
                return ev

            slot_free = [None, None]
            ld_ev = [None, None]
            ld_ev[0] = load_head(0, 0, [])
            ld_ev[1] = load_head(1, 1, [])
            psST_free = [None, None]
            psAV_free = [None, None]
            psS_free = [None, None]
            pend_fin = [None]
            psB_free = None
            pT_free = [None, None]
            ob_free = [None, None]
            rs_free = [None]
            rSb_free = [None, None]
            sq_free = None
            ct_done = [None, None]
            head_ct = [None, None]
            aug_done = [None, None]

            def emit_norms(h, sl, ld):
                nonlocal psN_free, sq_free
                nk2 = nk2sl[sl]
                ev_k1 = P.emit("scalar", lambda e, sl=sl: e.activation(
                    sqb[:, :], ks1[sl][:, :], AF.Square), waits=[ld, sq_free], inc="se")
                for c8 in range(8):
                    P.emit("tensor", lambda e, c8=c8: e.matmul(
                        psN[0:1, 0:512], onesb[:, :], sqb[:, c8 * 512:(c8 + 1) * 512],
                        start=True, stop=False),
                        waits=[ev_k1 if c8 == 0 else None, psN_free], inc=None)
                    ev_m = P.emit("tensor", lambda e, c8=c8: e.matmul(
                        psN[0:1, 0:512], onesb[0:1, :],
                        nk2pe[0:1, c8 * 512:(c8 + 1) * 512],
                        start=False, stop=True), waits=[], inc="pe")
                    psN_free = P.emit("scalar", lambda e, nk2=nk2, c8=c8: e.copy(
                        nk2[0:1, c8 * 512:(c8 + 1) * 512], psN[0:1, 0:512]),
                        waits=[ev_m, ct_done[sl]], inc="se")
                ev_pm = P.emit("vector", lambda e, nk2=nk2: e.tensor_tensor_scan(
                    pmax[0:1, :], nk2[0:1, :], nk2[0:1, :], -1e30, ALU.max, ALU.bypass),
                    waits=[psN_free], inc="ve")
                ev_q1 = P.emit("scalar", lambda e, sl=sl: e.activation(
                    sqb[:, :], qs1[sl][:, :], AF.Square), waits=[P.ev("pe")], inc="se")
                ev_q2 = P.emit("scalar", lambda e, sl=sl: e.activation(
                    sq2b[:, :], qs2[sl][0:64, :], AF.Square), waits=[], inc="se")
                ev_ct = None
                for c8 in range(8):
                    P.emit("tensor", lambda e, c8=c8: e.matmul(
                        psN[0:1, 0:512], onesb[:, :], sqb[:, c8 * 512:(c8 + 1) * 512],
                        start=True, stop=False),
                        waits=[ev_q1 if c8 == 0 else None, psN_free], inc=None)
                    ev_m = P.emit("tensor", lambda e, c8=c8: e.matmul(
                        psN[0:1, 0:512], onesb[0:64, :], sq2b[:, c8 * 512:(c8 + 1) * 512],
                        start=False, stop=True),
                        waits=[ev_q2] if c8 == 0 else [], inc="pe")
                    psN_free = P.emit("vector", lambda e, nk2=nk2, c8=c8: e.tensor_tensor(
                        nk2[0:1, c8 * 512:(c8 + 1) * 512], psN[0:1, 0:512],
                        pmax[0:1, c8 * 512:(c8 + 1) * 512], ALU.add),
                        waits=[ev_m, ev_pm if c8 == 0 else None], inc="ve")
                    ev_ct = psN_free
                sq_free = P.ev("pe")
                ct_done[sl] = ev_ct
                ev_aug = P.emit("scalar", lambda e, sl=sl, nk2=nk2: e.mul(
                    qs2[sl][64:65, :], nk2[0:1, :], -0.5), waits=[ev_ct], inc="se")
                aug_done[sl] = ev_aug
                return ev_ct

            def emit_fin(h, sl, J, uu, ev_av, ev_be):
                nonlocal psAV_free, pT_free, ob_free
                qb5 = J * 512
                ev_ob = P.emit("vector", lambda e, uu=uu, J=J: e.tensor_tensor(
                    ob[uu % 2][:, :], psAV_[uu % 2][:, 0:512], rSb[uu % 2][:, :],
                    ALU.mult),
                    waits=[ev_av, ev_be, ob_free[uu % 2]], inc="ve")
                rSb_free[uu % 2] = ev_ob
                psAV_free[uu % 2] = ev_ob
                ev_sp = P.dma("sync", o_d[h * 128:(h + 1) * 128, qb5:qb5 + 512],
                              ob[uu % 2][:, :], waits=[ev_ob], sem=f"dspo{uu % 2}")
                ob_free[uu % 2] = ev_sp
                if debug:
                    ob_free[uu % 2] = P.dma(
                        "gpsimd", dbg["o"][h * 128:(h + 1) * 128, qb5:qb5 + 512],
                        ob[uu % 2][:, :], waits=[ev_ob], sem=f"dbo{uu % 2}")

            def emit_pass2(h, sl, J, uu):
                nonlocal psST_free, pT_free, psB_free
                qb5 = J * 512
                ktmax = 4 * (J + 1)
                ev_av = None
                ev_s = None
                sS = psS[0]
                psAV = psAV_[uu % 2]

                def emit_av(kt, ev_pt):
                    nonlocal ev_av, ev_s
                    w_av = [ev_pt] if kt > 0 else [ev_pt, psAV_free[uu % 2]]
                    ev_av = P.emit("tensor", lambda e, psAV=psAV, sl=sl, kt=kt, ktmax=ktmax: e.matmul(
                        psAV[:, 0:512], vtk[sl][:, kt * 128:(kt + 1) * 128],
                        pT[kt % 2][:, :], start=(kt == 0), stop=(kt == ktmax - 1)),
                        waits=w_av, inc="pe")
                    ev_s = P.emit("tensor", lambda e, sS=sS, kt=kt, ktmax=ktmax: e.matmul(
                        sS[0:1, 0:512], onesb[:, :], pT[kt % 2][:, :],
                        start=(kt == 0), stop=(kt == ktmax - 1)),
                        waits=[psS_free[0]] if kt == 0 else [],
                        inc="pe" if kt == ktmax - 1 else None)
                    pT_free[kt % 2] = ev_s

                pend_av = None
                for kt in range(ktmax):
                    st = psST[kt % 2]
                    d = kt - 4 * J
                    P.emit("tensor", lambda e, st=st, sl=sl, kt=kt, qb5=qb5: e.matmul(
                        st[:, 0:512], ks1[sl][:, kt * 128:(kt + 1) * 128],
                        qs1[sl][:, qb5:qb5 + 512], start=True, stop=False),
                        waits=[psST_free[kt % 2], aug_done[sl]] if kt == 0
                        else [psST_free[kt % 2]])
                    ev_mm = P.emit("tensor",
                                   lambda e, st=st, sl=sl, kt=kt, qb5=qb5, d=d: e.matmul(
                                       st[:, 0:512], ks2[0:65, kt * 128:(kt + 1) * 128],
                                       qs2[sl][0:65, qb5:qb5 + 512],
                                       start=False, stop=(d < 0)),
                                   inc=None if d >= 0 else "pe")
                    if d >= 0:
                        ev_mm = P.emit("tensor", lambda e, st=st, d=d: e.matmul(
                            st[:, 0:512], negI[:, :], step2[:, d * 512:(d + 1) * 512],
                            start=False, stop=True), inc="pe")
                    ev_pt = P.emit("scalar", lambda e, st=st, kt=kt: e.activation(
                        pT[kt % 2][:, :], st[:, 0:512], AF.Exp, scale=SCALE),
                        waits=[ev_mm, pT_free[kt % 2]], inc="se")
                    psST_free[kt % 2] = ev_pt
                    if pend_av is not None:
                        emit_av(*pend_av)
                    pend_av = (kt, ev_pt)
                emit_av(*pend_av)
                ev_ln = P.emit("scalar", lambda e, sS=sS: e.activation(
                    rs_row[0][0:1, :], sS[0:1, 0:512], AF.Ln),
                    waits=[ev_s, rs_free[0]], inc="se")
                psS_free[0] = ev_ln
                ev_rs = P.emit("scalar", lambda e: e.activation(
                    rs_row[0][0:1, :], rs_row[0][0:1, :], AF.Exp, scale=-1.0),
                    waits=[ev_ln], inc="se")
                ev_bc = P.emit("tensor", lambda e: e.matmul(
                    psB[:, 0:512], onesf_row[:, :], rs_row[0][0:1, :],
                    start=True, stop=True), waits=[ev_rs, psB_free], inc="pe")
                rs_free[0] = ev_bc
                ev_be = P.emit("scalar", lambda e, uu=uu: e.copy(
                    rSb[uu % 2][:, :], psB[:, 0:512]),
                    waits=[ev_bc, rSb_free[uu % 2]], inc="se")
                psB_free = ev_be
                return (ev_av, ev_be)

            def fw_generator():
                i32_free = [None, None]
                idxb_free = None
                idxT_free = [None, None]
                mask_free = None
                accf = [None, None]
                jobs = []
                for it in range(NIT):
                    jobs.append((gidx_h, it * 128, 0, E, wgT_d[it],
                                 dbg.get("wgT")[it] if (debug and dbg.get("wgT") is not None) else None))
                for it in range(NIT):
                    jobs.append((uidx_h, it * 128, 0, E, wuT_d[it], None))
                for et in range(NET):
                    for (c0, w) in ((0, 2048), (2048, 2048), (4096, 1408)):
                        jobs.append((didx_h, et * 128, c0, w,
                                     wdT_d[et][:, c0:c0 + w],
                                     dbg.get("wdT")[et][:, c0:c0 + w]
                                     if (debug and dbg.get("wdT") is not None) else None))
                def prep(si):
                    nonlocal idxb_free
                    (src_h, r0, c0, width, dst_ap, dbg_ap) = jobs[si]
                    sl = si % 2
                    fwidxT = fwidxT_[sl]
                    ev_ld = P.dma("sync", fwi32[sl][:, 0:width],
                                  src_h[r0:r0 + 128, c0:c0 + width],
                                  waits=[i32_free[0], i32_free[1]], sem=f"dfwl{sl}")
                    yield
                    ev_cv = P.emit("scalar", lambda e, sl=sl, width=width: e.copy(
                        fwidxb[:, 0:width], fwi32[sl][:, 0:width]),
                        waits=[ev_ld, idxb_free], inc="se")
                    i32_free[0] = ev_cv
                    nbl = width // 128
                    ev_ev = None
                    for b0 in range(0, nbl, 8):
                        nb = min(8, nbl - b0)
                        ev_t = None
                        for j in range(nb):
                            blk = b0 + j
                            w = ([psFT_free[0], ev_cv] +
                                 ([idxT_free[sl]] if b0 == 0 else [])) if j == 0 else []
                            ev_t = P.emit("tensor", lambda e, fwidxT=fwidxT, blk=blk, j=j:
                                          e.transpose(psFT[:, j * 128:(j + 1) * 128],
                                                      fwidxb[:, blk * 128:(blk + 1) * 128],
                                                      identb[:, :]),
                                          waits=w, inc="pe" if j == nb - 1 else None)
                        ev_ev = P.emit("scalar", lambda e, fwidxT=fwidxT, b0=b0, nb=nb:
                                       e.copy(fwidxT[:, b0 * 128:(b0 + nb) * 128],
                                              psFT[:, 0:nb * 128]),
                                       waits=[ev_t], inc="se")
                        psFT_free[0] = ev_ev
                        yield
                    idxb_free = P.ev("pe")
                    prep_ev[si % 2] = ev_ev

                prep_ev = [None, None]
                g0 = prep(0)
                for _ in g0:
                    pass
                for si in range(len(jobs)):
                    (src_h, r0, c0, width, dst_ap, dbg_ap) = jobs[si]
                    sl = si % 2
                    fwidxT = fwidxT_[sl]
                    ev_ev = prep_ev[sl]
                    pgen = prep(si + 1) if si + 1 < len(jobs) else None
                    ev_k = None
                    for k in range(16):
                        P.emit("vector", lambda e, fwidxT=fwidxT, k=k, width=width:
                               e.tensor_scalar(fwmask[:, 0:width], fwidxT[:, 0:width],
                                               float(k), None, ALU.is_equal),
                               waits=[ev_ev, mask_free] if k == 0 else [])
                        last = (k == 15)
                        if k == 0:
                            ev_k = P.emit("vector", lambda e, sl=sl, width=width:
                                          e.tensor_scalar(
                                              fwacc[sl][:, 0:width], fwmask[:, 0:width],
                                              lut_col[:, 0:1], None, ALU.mult),
                                          waits=[accf[sl]])
                        else:
                            P.emit("vector", lambda e, k=k, width=width:
                                   e.tensor_scalar(fwtmp[:, 0:width], fwmask[:, 0:width],
                                                   lut_col[:, k:k + 1], None, ALU.mult),
                                   waits=[])
                            ev_k = P.emit("vector", lambda e, sl=sl, width=width:
                                          e.tensor_tensor(
                                              fwacc[sl][:, 0:width], fwtmp[:, 0:width],
                                              fwacc[sl][:, 0:width], ALU.add),
                                          waits=[], inc="ve" if last else None)
                        if pgen is not None and k % 4 == 3:
                            for _ in pgen:
                                break
                        if k % 4 == 3:
                            yield
                    if pgen is not None:
                        for _ in pgen:
                            pass
                    mask_free = ev_k
                    idxT_free[sl] = ev_k
                    ev_sp = P.dma("sync", dst_ap, fwacc[sl][:, 0:width],
                                  waits=[ev_k], sem=f"dfws{sl}")
                    accf[sl] = ev_sp
                    if dbg_ap is not None:
                        accf[sl] = P.dma("gpsimd", dbg_ap, fwacc[sl][:, 0:width],
                                         waits=[ev_k], sem=f"dfwd{sl}")
                    yield

            psFT_free = [None]
            fw_it = fw_generator() if fw_inline else None
            fw_live = fw_inline

            def pump(n):
                nonlocal fw_live
                if not fw_live:
                    return
                try:
                    for _ in range(n):
                        next(fw_it)
                except StopIteration:
                    fw_live = False

            units = [(h, J) for h in range(HD) for J in range(8)]

            def after_pass2(hh, ssl, JJ):
                if JJ == 7:
                    slot_free[ssl] = P.ev("pe")
                    if hh + 2 < HD:
                        ld_ev[ssl] = load_head(hh + 2, ssl, [slot_free[ssl]])

            head_ct[0] = emit_norms(0, 0, ld_ev[0])
            for uu, (h, J) in enumerate(units):
                sl = h % 2
                pump(4)
                evs = emit_pass2(h, sl, J, uu)
                if pend_fin[0] is not None:
                    emit_fin(*pend_fin[0])
                    after_pass2(pend_fin[0][0], pend_fin[0][1], pend_fin[0][2])
                pend_fin[0] = (h, sl, J, uu, evs[0], evs[1])
                if J == 4 and h + 1 < HD:
                    head_ct[1 - sl] = emit_norms(h + 1, 1 - sl, ld_ev[1 - sl])
                pump(4)
            emit_fin(*pend_fin[0])
            after_pass2(pend_fin[0][0], pend_fin[0][1], pend_fin[0][2])
            pump(10 ** 9)
            P.barrier()
            P.flush()

    if "p2" in phases:
        with contextlib.ExitStack() as ph:
            wo_b = [ph.enter_context(nc.sbuf_tensor(f"wo{i}", [128, E], BF16))
                    for i in range(NET)]
            oslab = [ph.enter_context(nc.sbuf_tensor(f"osl{i}", [128, 8192], BF16))
                     for i in range(2)]
            x_in = [ph.enter_context(nc.sbuf_tensor(f"xr{i}", [128, E], F32))
                    for i in range(4)]
            z = ph.enter_context(nc.sbuf_tensor("z", [128, E], F32))
            zsq = ph.enter_context(nc.sbuf_tensor("zsq", [128, E], F32))
            yb = ph.enter_context(nc.sbuf_tensor("yb", [128, E], F32))
            ybf = ph.enter_context(nc.sbuf_tensor("ybf", [128, E], BF16))
            yT_buf = ph.enter_context(nc.sbuf_tensor("yT_buf", [128, E], BF16))
            s1 = ph.enter_context(nc.sbuf_tensor("s1", [128, 1], F32))
            s2 = ph.enter_context(nc.sbuf_tensor("s2", [128, 1], F32))
            negmean = ph.enter_context(nc.sbuf_tensor("negmean", [128, 1], F32))
            msq = ph.enter_context(nc.sbuf_tensor("msq", [128, 1], F32))
            var = ph.enter_context(nc.sbuf_tensor("var", [128, 1], F32))
            rstd = ph.enter_context(nc.sbuf_tensor("rstd", [128, 1], F32))
            psO = ph.enter_context(nc.psum_tensor("psO", [128, 2048], F32))
            pst = [ph.enter_context(nc.psum_tensor(f"pstp2{i}", [128, 1024], BF16))
                   for i in range(2)]

            wo_ev = None
            for i in range(NET):
                wo_ev = P.dma("gpsimd", wo_b[i][:, :], wo_h[i * 128:(i + 1) * 128, :])
            oslab_free = [None, None]
            x_free = None
            pst_free = [None, None]
            psO_free = None
            z_free = None
            zsq_free = None
            ybf_free = None
            yT_free = None
            stat_free = None
            ld_os = [None, None]
            ld_x = None

            def load_t512(tt, sl, waits):
                ev = None
                for dt in range(NET):
                    ev = P.dma("sync", oslab[sl][:, dt * 512:(dt + 1) * 512],
                               o_d[dt * 128:(dt + 1) * 128, tt * 512:(tt + 1) * 512],
                               waits=waits if dt == 0 else [], sem=f"dslo{sl}")
                return ev

            ld_os[0] = load_t512(0, 0, [])
            for tt in range(8):
                sl = tt % 2
                ld = ld_os[sl]
                ldx = None
                for ts4 in range(4):
                    ldx = P.dma("sync", x_in[ts4][:, :],
                                x_h[tt * 512 + ts4 * 128: tt * 512 + (ts4 + 1) * 128, :],
                                waits=[x_free] if ts4 == 0 else [], sem="dslx")
                if tt + 1 < 8:
                    ld_os[1 - sl] = load_t512(tt + 1, 1 - sl, [oslab_free[1 - sl]])
                for ts in range(4):
                    ev_z = None
                    for ec in range(4):
                        ev_mm = None
                        for dt in range(NET):
                            w = [ld, ldx, wo_ev, psO_free] if (ec == 0 and dt == 0) else []
                            ev_mm = P.emit("tensor",
                                           lambda e, dt=dt, ec=ec, ts=ts, sl=sl: e.matmul(
                                               psO[:, ec * 512:(ec + 1) * 512],
                                               oslab[sl][:, dt * 512 + ts * 128: dt * 512 + (ts + 1) * 128],
                                               wo_b[dt][:, ec * 512:(ec + 1) * 512],
                                               start=(dt == 0), stop=(dt == NET - 1)),
                                           waits=w, inc="pe" if dt == NET - 1 else None)
                        ev_z = P.emit("vector",
                                      lambda e, ec=ec, ts=ts: e.tensor_tensor(
                                          z[:, ec * 512:(ec + 1) * 512],
                                          psO[:, ec * 512:(ec + 1) * 512],
                                          x_in[ts][:, ec * 512:(ec + 1) * 512], ALU.add),
                                      waits=[ev_mm] + ([z_free] if ec == 0 else []),
                                      inc="ve")
                    psO_free = ev_z
                    if debug:
                        t0d = tt * 512 + ts * 128
                        P.dma("gpsimd", dbg["z"][t0d:t0d + 128, :], z[:, :],
                              waits=[ev_z], sem="dbz")
                        z_free = P.ev("dbz")
                    ev_sq = P.emit("scalar", lambda e: e.activation(
                        zsq[:, :], z[:, :], AF.Square), waits=[ev_z, zsq_free], inc="se")
                    ev_s1 = P.emit("vector", lambda e: e.reduce_sum(
                        s1[:, :], z[:, :], axis=AX.X), waits=[stat_free], inc="ve")
                    ev_s2 = P.emit("vector", lambda e: e.reduce_sum(
                        s2[:, :], zsq[:, :], axis=AX.X), waits=[ev_sq], inc="ve")
                    ev_nm = P.emit("vector", lambda e: e.tensor_scalar(
                        negmean[:, :], s1[:, :], -1.0 / E, None, ALU.mult),
                        waits=[ev_s1], inc="ve")
                    ev_mq = P.emit("vector", lambda e: e.tensor_tensor(
                        msq[:, :], negmean[:, :], negmean[:, :], ALU.mult),
                        waits=[ev_nm], inc="ve")
                    ev_vr = P.emit("vector", lambda e: e.scalar_tensor_tensor(
                        var[:, :], s2[:, :], 1.0 / E, msq[:, :], ALU.mult, ALU.subtract),
                        waits=[ev_mq, ev_s2], inc="ve")
                    ev_lv = P.emit("scalar", lambda e: e.activation(
                        rstd[:, :], var[:, :], AF.Ln, bias=LN_EPS), waits=[ev_vr], inc="se")
                    ev_rs = P.emit("scalar", lambda e: e.activation(
                        rstd[:, :], rstd[:, :], AF.Exp, scale=-0.5), waits=[ev_lv], inc="se")
                    ev_y0 = P.emit("vector", lambda e: e.tensor_scalar(
                        zsq[:, :], z[:, :], negmean[:, :], None, ALU.add),
                        waits=[ev_rs, ev_s2], inc="ve")
                    ev_y = P.emit("vector", lambda e: e.tensor_scalar(
                        yb[:, :], zsq[:, :], rstd[:, :], None, ALU.mult),
                        waits=[ev_y0], inc="ve")
                    z_free = ev_y
                    zsq_free = ev_s2
                    stat_free = ev_y
                    t0 = tt * 512 + ts * 128
                    ev_spy = P.dma("sync", y_d[t0:t0 + 128, :], yb[:, :],
                                   waits=[ev_y], sem="dspy")
                    if debug:
                        ev_spy = P.dma("gpsimd", dbg["y"][t0:t0 + 128, :], yb[:, :],
                                       waits=[ev_y], sem="dspy2")
                    ev_yb = P.emit("scalar", lambda e: e.copy(ybf[:, :], yb[:, :]),
                                   waits=[ev_y, ybf_free], inc="se")
                    ev_tb = None
                    for bat in range(2):
                        pb = pst[bat]
                        ev_t = None
                        for j in range(8):
                            et = bat * 8 + j
                            w = [pst_free[bat], ev_yb] if j == 0 else []
                            ev_t = P.emit("tensor", lambda e, pb=pb, et=et, j=j: e.transpose(
                                pb[:, j * 128:(j + 1) * 128],
                                ybf[:, et * 128:(et + 1) * 128], identb[:, :]),
                                waits=w, inc="pe" if j == 7 else None)
                        ev_tb = P.emit("scalar", lambda e, pb=pb, bat=bat: e.copy(
                            yT_buf[:, bat * 1024:(bat + 1) * 1024], pb[:, :]),
                            waits=[ev_t] + ([yT_free] if bat == 0 else []), inc="se")
                        pst_free[bat] = ev_tb
                    ybf_free = P.ev("pe")
                    out_ap = yT_d[tt, :, :, ts * 128:(ts + 1) * 128].rearrange(
                        "et p k -> p et k")
                    in_ap = yT_buf[:, :].rearrange("p (et k) -> p et k", k=128)
                    yT_free = P.dma("sync", out_ap, in_ap, waits=[ev_tb], sem="dspyt")
                oslab_free[sl] = P.ev("pe")
                x_free = P.ev("ve")
            P.barrier()
            P.flush()

    if "fw" in phases and "p1" not in phases:
        with contextlib.ExitStack() as ph:
            idxb = [ph.enter_context(nc.sbuf_tensor(f"idxb{i}", [128, INTER], BF16))
                    for i in range(2)]
            idxT = [ph.enter_context(nc.sbuf_tensor(f"idxT{i}", [128, INTER], F32))
                    for i in range(2)]
            maskb = ph.enter_context(nc.sbuf_tensor("maskb", [128, INTER], BF16))
            acc = [ph.enter_context(nc.sbuf_tensor(f"facc{i}", [128, INTER], BF16))
                   for i in range(2)]
            pst = [ph.enter_context(nc.psum_tensor(f"pstf{i}", [128, 1024], BF16))
                   for i in range(2)]

            pst_free = [None, None]
            idxb_free = [None, None]
            idxT_free = [None, None]
            acc_free = [None, None]
            mask_free = None

            def materialize(src_h, dst_d, dbg_d, tiles, width, row0fn):
                nonlocal mask_free
                for it in range(tiles):
                    sl = it % 2
                    r0 = row0fn(it)
                    ev_ld = P.dma("gpsimd", idxb[sl][:, 0:width],
                                  src_h[r0:r0 + 128, :], waits=[idxb_free[sl]],
                                  sem=f"dfw{sl}")
                    nbl = width // 128
                    ev_ev = None
                    for b0 in range(0, nbl, 8):
                        nb = min(8, nbl - b0)
                        pb = pst[(b0 // 8) % 2]
                        ev_t = None
                        for j in range(nb):
                            blk = b0 + j
                            w = [pst_free[(b0 // 8) % 2], ev_ld,
                                 idxT_free[sl] if b0 == 0 else None] if j == 0 else []
                            ev_t = P.emit("tensor", lambda e, pb=pb, sl=sl, blk=blk, j=j:
                                          e.transpose(pb[:, j * 128:(j + 1) * 128],
                                                      idxb[sl][:, blk * 128:(blk + 1) * 128],
                                                      identb[:, :]),
                                          waits=w, inc="pe" if j == nb - 1 else None)
                        ev_ev = P.emit("scalar", lambda e, pb=pb, sl=sl, b0=b0, nb=nb: e.copy(
                            idxT[sl][:, b0 * 128:(b0 + nb) * 128], pb[:, 0:nb * 128]),
                            waits=[ev_t], inc="se")
                        pst_free[(b0 // 8) % 2] = ev_ev
                    idxb_free[sl] = P.ev("pe")
                    ev_ms = P.emit("vector", lambda e, sl=sl, width=width: e.memset(
                        acc[sl][:, 0:width], 0.0), waits=[acc_free[sl]], inc="ve")
                    ev_k = ev_ms
                    for k in range(16):
                        ev_m = P.emit("vector", lambda e, sl=sl, k=k, width=width:
                                      e.tensor_scalar(maskb[:, 0:width],
                                                      idxT[sl][:, 0:width], float(k),
                                                      None, ALU.is_equal),
                                      waits=[ev_ev, mask_free] if k == 0 else [ev_k],
                                      inc="ve")
                        ev_k = P.emit("vector", lambda e, sl=sl, k=k, width=width:
                                      e.scalar_tensor_tensor(acc[sl][:, 0:width],
                                                             maskb[:, 0:width],
                                                             lut_col[:, k:k + 1],
                                                             acc[sl][:, 0:width],
                                                             ALU.mult, ALU.add),
                                      waits=[ev_m], inc="ve")
                    mask_free = ev_k
                    idxT_free[sl] = ev_k
                    ev_sp = P.dma("sync", dst_d[it], acc[sl][:, 0:width],
                                  waits=[ev_k], sem=f"dfsp{sl}")
                    acc_free[sl] = ev_sp
                    if debug and dbg_d is not None:
                        acc_free[sl] = P.dma("gpsimd", dbg_d[it], acc[sl][:, 0:width],
                                             waits=[ev_k], sem=f"dfdb{sl}")

            materialize(gidx_h, wgT_d, dbg.get("wgT"), NIT, E, lambda it: it * 128)
            materialize(uidx_h, wuT_d, None, NIT, E, lambda it: it * 128)
            materialize(didx_h, wdT_d, dbg.get("wdT"), NET, INTER, lambda it: it * 128)
            P.barrier()
            P.flush()

    if "f" in phases:
        with contextlib.ExitStack() as ph:
            ytT = ph.enter_context(nc.sbuf_tensor("ytT", [128, 8192], BF16))
            y_res = [ph.enter_context(nc.sbuf_tensor(f"yres{i}", [128, E], F32))
                     for i in range(4)]
            wg_i = [ph.enter_context(nc.sbuf_tensor(f"wgi{i}", [128, E], BF16))
                    for i in range(2)]
            wu_i = [ph.enter_context(nc.sbuf_tensor(f"wui{i}", [128, E], BF16))
                    for i in range(2)]
            wd_et = [ph.enter_context(nc.sbuf_tensor(f"wdet{i}", [128, INTER], BF16))
                     for i in range(2)]
            hT = ph.enter_context(nc.sbuf_tensor("hT", [128, NIT * 512], BF16))
            sil = ph.enter_context(nc.sbuf_tensor("sil", [128, 512], F32))
            dnT = ph.enter_context(nc.sbuf_tensor("dnT", [128, 8192], BF16))
            dn_tok = [ph.enter_context(nc.sbuf_tensor(f"dntok{i}", [128, E], BF16))
                      for i in range(4)]
            z2 = ph.enter_context(nc.sbuf_tensor("z2", [128, E], F32))
            z2sq = ph.enter_context(nc.sbuf_tensor("z2sq", [128, E], F32))
            outb = ph.enter_context(nc.sbuf_tensor("outb", [128, E], F32))
            fs1 = ph.enter_context(nc.sbuf_tensor("fs1", [128, 1], F32))
            fs2 = ph.enter_context(nc.sbuf_tensor("fs2", [128, 1], F32))
            fnm = ph.enter_context(nc.sbuf_tensor("fnm", [128, 1], F32))
            fmq = ph.enter_context(nc.sbuf_tensor("fmq", [128, 1], F32))
            fvr = ph.enter_context(nc.sbuf_tensor("fvr", [128, 1], F32))
            frs = ph.enter_context(nc.sbuf_tensor("frs", [128, 1], F32))
            psG_ = [ph.enter_context(nc.psum_tensor(f"psG{i}", [128, 512], F32))
                    for i in range(2)]
            psU_ = [ph.enter_context(nc.psum_tensor(f"psU{i}", [128, 512], F32))
                    for i in range(2)]
            psD = [ph.enter_context(nc.psum_tensor(f"psD{i}", [128, 512], F32))
                   for i in range(2)]
            pstf = ph.enter_context(nc.psum_tensor("pstff", [128, 1024], BF16))

            ytT_free = None
            yres_free = None
            wg_free = [None, None]
            wd_free = [None, None]
            psG_free = [None, None]
            psU_free = [None, None]
            psD_free = [None, None]
            pst_free = None
            sil_free = None
            hT_free = None
            dnT_free = None
            dntok_free = [None] * 4
            z2_free = None
            z2sq_free = None
            outb_free = None
            fstat_free = None

            def load_wgu(it, sl, waits):
                e1 = P.dma("sync", wg_i[sl][:, :], wgT_d[it], waits=waits, sem=f"dfg{sl}")
                e2 = P.dma("sync", wu_i[sl][:, :], wuT_d[it], waits=[], sem=f"dfg{sl}")
                return e2

            wg_ev = [None, None]
            wd_ev = [None, None]
            for tt in range(8):
                ev_yt = P.dma("sync", ytT[:, :].rearrange("p (et k) -> p et k", k=512),
                              yT_d[tt].rearrange("et p k -> p et k"),
                              waits=[ytT_free], sem="dfyt")
                ev_yr = None
                for ts in range(4):
                    ev_yr = P.dma("sync", y_res[ts][:, :],
                                  y_d[tt * 512 + ts * 128: tt * 512 + (ts + 1) * 128, :],
                                  waits=[yres_free] if ts == 0 else [], sem="dfyr")
                wg_ev[0] = load_wgu(0, 0, [wg_free[0]])
                for it in range(NIT):
                    sl = it % 2
                    if it + 1 < NIT:
                        wg_ev[1 - sl] = load_wgu(it + 1, 1 - sl, [wg_free[1 - sl]])
                    psG = psG_[sl]
                    psU = psU_[sl]
                    ev_g = None
                    ev_u = None
                    for ht in range(NET):
                        w = [wg_ev[sl], ev_yt, psG_free[sl]] if ht == 0 else []
                        ev_g = P.emit("tensor", lambda e, psG=psG, sl=sl, ht=ht: e.matmul(
                            psG[:, 0:512], wg_i[sl][:, ht * 128:(ht + 1) * 128],
                            ytT[:, ht * 512:(ht + 1) * 512],
                            start=(ht == 0), stop=(ht == NET - 1)),
                            waits=w, inc="pe" if ht == NET - 1 else None)
                    for ht in range(NET):
                        w = [psU_free[sl]] if ht == 0 else []
                        ev_u = P.emit("tensor", lambda e, psU=psU, sl=sl, ht=ht: e.matmul(
                            psU[:, 0:512], wu_i[sl][:, ht * 128:(ht + 1) * 128],
                            ytT[:, ht * 512:(ht + 1) * 512],
                            start=(ht == 0), stop=(ht == NET - 1)),
                            waits=w, inc="pe" if ht == NET - 1 else None)
                    ev_sl = P.emit("scalar", lambda e, psG=psG: e.activation(
                        sil[:, :], psG[:, 0:512], AF.Silu), waits=[ev_g, sil_free],
                        inc="se")
                    psG_free[sl] = ev_sl
                    ev_h = P.emit("vector", lambda e, psU=psU, it=it: e.tensor_tensor(
                        hT[:, it * 512:(it + 1) * 512], psU[:, 0:512], sil[:, :],
                        ALU.mult), waits=[ev_sl, ev_u, hT_free if it == 0 else None],
                        inc="ve")
                    psU_free[sl] = ev_h
                    sil_free = ev_h
                    wg_free[sl] = P.ev("pe")
                ev_h_all = P.ev("ve")
                wd_ev[0] = P.dma("sync", wd_et[0][:, :], wdT_d[0], waits=[wd_free[0]],
                                 sem="dfd0")
                for et in range(NET):
                    sl = et % 2
                    if et + 1 < NET:
                        wd_ev[1 - sl] = P.dma("sync", wd_et[1 - sl][:, :], wdT_d[et + 1],
                                              waits=[wd_free[1 - sl]], sem=f"dfd{1 - sl}")
                    ps = psD[et % 2]
                    ev_d = None
                    for it in range(NIT):
                        w = [wd_ev[sl], ev_h_all, psD_free[et % 2]] if it == 0 else []
                        ev_d = P.emit("tensor", lambda e, ps=ps, sl=sl, it=it: e.matmul(
                            ps[:, 0:512], wd_et[sl][:, it * 128:(it + 1) * 128],
                            hT[:, it * 512:(it + 1) * 512],
                            start=(it == 0), stop=(it == NIT - 1)),
                            waits=w, inc="pe" if it == NIT - 1 else None)
                    ev_de = P.emit("scalar", lambda e, ps=ps, et=et: e.copy(
                        dnT[:, et * 512:(et + 1) * 512], ps[:, 0:512]),
                        waits=[ev_d, dnT_free if et == 0 else None], inc="se")
                    psD_free[et % 2] = ev_de
                    wd_free[sl] = P.ev("pe")
                ev_dnT = P.ev("se")
                hT_free = P.ev("pe")
                for ts in range(4):
                    for bat in range(2):
                        ev_t = None
                        for j in range(8):
                            et = bat * 8 + j
                            w = [pst_free, ev_dnT, dntok_free[ts] if (bat == 0 and j == 0) else None] if j == 0 else []
                            ev_t = P.emit("tensor", lambda e, et=et, j=j, ts=ts: e.transpose(
                                pstf[:, j * 128:(j + 1) * 128],
                                dnT[:, et * 512 + ts * 128: et * 512 + (ts + 1) * 128],
                                identb[:, :]), waits=w, inc="pe" if j == 7 else None)
                        ev_te = P.emit("scalar", lambda e, ts=ts, bat=bat: e.copy(
                            dn_tok[ts][:, bat * 1024:(bat + 1) * 1024], pstf[:, :]),
                            waits=[ev_t], inc="se")
                        pst_free = ev_te
                    dntok_free[ts] = ev_te
                dnT_free = P.ev("pe")
                for ts in range(4):
                    ev_z2 = P.emit("vector", lambda e, ts=ts: e.tensor_tensor(
                        z2[:, :], dn_tok[ts][:, :], y_res[ts][:, :], ALU.add),
                        waits=[dntok_free[ts], ev_yr, z2_free], inc="ve")
                    ev_sq = P.emit("scalar", lambda e: e.activation(
                        z2sq[:, :], z2[:, :], AF.Square), waits=[ev_z2, z2sq_free],
                        inc="se")
                    ev_s1 = P.emit("vector", lambda e: e.reduce_sum(
                        fs1[:, :], z2[:, :], axis=AX.X), waits=[fstat_free], inc="ve")
                    ev_s2 = P.emit("vector", lambda e: e.reduce_sum(
                        fs2[:, :], z2sq[:, :], axis=AX.X), waits=[ev_sq], inc="ve")
                    ev_nm = P.emit("vector", lambda e: e.tensor_scalar(
                        fnm[:, :], fs1[:, :], -1.0 / E, None, ALU.mult),
                        waits=[ev_s1], inc="ve")
                    ev_mq = P.emit("vector", lambda e: e.tensor_tensor(
                        fmq[:, :], fnm[:, :], fnm[:, :], ALU.mult), waits=[ev_nm],
                        inc="ve")
                    ev_vr = P.emit("vector", lambda e: e.scalar_tensor_tensor(
                        fvr[:, :], fs2[:, :], 1.0 / E, fmq[:, :], ALU.mult,
                        ALU.subtract), waits=[ev_mq, ev_s2], inc="ve")
                    ev_lv = P.emit("scalar", lambda e: e.activation(
                        frs[:, :], fvr[:, :], AF.Ln, bias=LN_EPS), waits=[ev_vr],
                        inc="se")
                    ev_rs = P.emit("scalar", lambda e: e.activation(
                        frs[:, :], frs[:, :], AF.Exp, scale=-0.5), waits=[ev_lv],
                        inc="se")
                    ev_o0 = P.emit("vector", lambda e: e.tensor_scalar(
                        z2sq[:, :], z2[:, :], fnm[:, :], None, ALU.add),
                        waits=[ev_rs, ev_s2], inc="ve")
                    ev_o = P.emit("vector", lambda e: e.tensor_scalar(
                        outb[:, :], z2sq[:, :], frs[:, :], None, ALU.mult),
                        waits=[ev_o0, outb_free], inc="ve")
                    z2_free = ev_o
                    z2sq_free = ev_s2
                    fstat_free = ev_o
                    t0 = tt * 512 + ts * 128
                    outb_free = P.dma("sync", out_h[t0:t0 + 128, :], outb[:, :],
                                      waits=[ev_o], sem="dout")
                yres_free = P.ev("ve")
                ytT_free = P.ev("pe")
            P.barrier()
            P.flush()

    P.barrier()
    P.flush()
    stk.close()
    return nc


def _reorder_wq(wq):
    w = wq.reshape(E, HD, QH)
    nope = w[:, :, :NOPE].reshape(E, HD * NOPE)
    pe = w[:, :, NOPE:].reshape(E, HD, 2, 32)
    pe = pe.reshape(E, 8, 2, 2, 32)
    pe = pe.transpose(0, 1, 3, 2, 4).reshape(E, 8 * 128)
    return np.ascontiguousarray(np.concatenate([nope, pe], 1))


def _shard_inputs(inputs):
    x = np.ascontiguousarray(inputs["L0_attn_x"])
    consts = make_consts()
    gamma = np.ascontiguousarray(
        np.asarray(inputs["L0_attn_kv_norm_scale"], dtype=np.float32).reshape(4, 128).T)
    common = {
        "wq": _reorder_wq(np.asarray(inputs["L0_attn_wq"], dtype=np.float32)),
        "wkva": np.ascontiguousarray(inputs["L0_attn_wkv_a"], dtype=np.float32),
        "wkvb": np.ascontiguousarray(inputs["L0_attn_wkv_b"], dtype=np.float32),
        "wo": np.ascontiguousarray(inputs["L0_attn_wo"], dtype=np.float32),
        "gamma": gamma,
        "lut": np.asarray(inputs["L0_ffn_lut"], dtype=np.float32).reshape(1, 16),
        "gidx": np.ascontiguousarray(inputs["L0_ffn_gate_idx"], dtype=np.int32),
        "uidx": np.ascontiguousarray(inputs["L0_ffn_up_idx"], dtype=np.int32),
        "didx": np.ascontiguousarray(inputs["L0_ffn_down_idx"], dtype=np.int32),
        **consts,
    }
    in_maps = []
    for i in range(B):
        m = dict(common)
        m["x"] = np.ascontiguousarray(x[i], dtype=np.float32)
        in_maps.append(m)
    return in_maps


def kernel(**inputs):
    in_maps = _shard_inputs(inputs)
    nc = build_program(debug=DEBUG, phases=PHASES)
    res = run_bass_kernel_spmd(nc, in_maps, core_ids=list(range(B)))
    out = np.stack([np.asarray(r["out"], dtype=np.float32) for r in res.results], 0)
    return out


# version 19
# speedup vs baseline: 1.0803x; 1.0095x over previous
import contextlib
import math
from collections import defaultdict

import numpy as np
import ml_dtypes

import concourse.bass as bass
import concourse.mybir as mybir
from concourse.bass_utils import run_bass_kernel_spmd

F32 = mybir.dt.float32
BF16 = mybir.dt.bfloat16
I32 = mybir.dt.int32
AF = mybir.ActivationFunctionType
ALU = mybir.AluOpType
AX = mybir.AxisListType

B = 8
T = 4096
E = 2048
HD = 16
NOPE, RD, VD = 128, 64, 128
QH = NOPE + RD
LORA = 512
INTER = 5504
NIT = INTER // 128
NET = E // 128
SCALE = QH ** -0.5
RMS_EPS = 1e-6
LN_EPS = 1e-5
ROPE_THETA = 10000.0
NEGBIG = -30000.0

ENGINES = ("tensor", "vector", "scalar", "gpsimd", "sync")

DEBUG = False
PHASES = ("p0a", "p0c", "p0b", "p1", "p2", "fw", "f")


class Prog:

    def __init__(self, nc):
        self.nc = nc
        self.ctx = contextlib.ExitStack()
        self.sems = {}
        self.count = defaultdict(int)
        self.ops = defaultdict(list)
        self.last_wait = defaultdict(dict)

    def sem(self, name):
        if name not in self.sems:
            self.sems[name] = self.ctx.enter_context(self.nc.semaphore(name))
        return self.sems[name]

    def ev(self, sem_name):
        return (sem_name, self.count[sem_name])

    def emit(self, eng, fn, waits=(), inc=None, amt=1):
        after = None
        if inc is not None:
            self.sem(inc)
            self.count[inc] += amt
            after = (inc, self.count[inc])
        w = []
        for x in waits:
            if x is None:
                continue
            (s, v) = x
            if v <= 0:
                continue
            self.sem(s)
            assert v <= self.count[s], f"wait on future event {s}={v} (@{self.count[s]})"
            if self.last_wait[eng].get(s, 0) < v:
                self.last_wait[eng][s] = v
                w.append((s, v))
        self.ops[eng].append((fn, w, (inc, amt) if inc else None))
        return after

    def dma(self, queue, out, in_, waits=(), sem=None):
        sem = sem or ("d_" + queue)
        return self.emit(queue, lambda e, o=out, i=in_: e.dma_start(o, i),
                         waits=waits, inc=sem, amt=16)

    def flush(self):
        with self.nc.Block() as block:
            for eng in ENGINES:
                ops = self.ops[eng]
                if not ops:
                    continue

                def body(e, ops=list(ops)):
                    for (fn, w, inc) in ops:
                        for (s, v) in w:
                            e.wait_ge(self.sems[s], v)
                        inst = fn(e)
                        if inc is not None:
                            inst.then_inc(self.sems[inc[0]], inc[1])

                getattr(block, eng)(body)
                self.ops[eng] = []

    def barrier(self):
        evs = [(s, c) for s, c in self.count.items() if c > 0]
        for eng in ENGINES:
            w = []
            for (s, v) in evs:
                if self.last_wait[eng].get(s, 0) < v:
                    self.last_wait[eng][s] = v
                    w.append((s, v))
            if w:
                self.ops[eng].append(((lambda e: None), w, None))



def make_consts():
    c = {}
    ident = np.eye(128, dtype=np.float32)
    c["identb"] = ident
    c["identf"] = ident
    c["negI"] = NEGBIG * ident
    c["onesb"] = np.ones((128, 1), dtype=np.float32)
    c["onesf_col"] = np.ones((128, 1), dtype=np.float32)
    c["onesf_row"] = np.ones((1, 128), dtype=np.float32)
    inv = ROPE_THETA ** (-np.arange(0, RD, 2, dtype=np.float32) / RD)
    ang = np.arange(T, dtype=np.float32)[:, None] * inv[None, :]
    cosT = np.cos(ang).T.astype(np.float32)
    sinT = np.sin(ang).T.astype(np.float32)
    c["cos2"] = np.concatenate([cosT, cosT], 0)
    c["sin2"] = np.concatenate([sinT, sinT], 0)
    r = np.arange(128)[:, None]
    col = np.arange(512)[None, :]
    s1 = np.stack([(col > 128 * v + r) for v in range(4)], 0)
    s2 = np.stack([(col < 128 * d + r) for d in range(4)], 0)
    c["step1"] = s1.astype(np.float32).reshape(4 * 128, 512)
    c["step2"] = s2.astype(np.float32).reshape(4 * 128, 512)
    return c



def build_program(debug=False, phases=PHASES):
    nc = bass.Bass("TRN2", target_bir_lowering=False)

    H = {}
    def din(name, shape, dt):
        H[name] = nc.dram_tensor(name, shape, dt, kind="ExternalInput").ap()
        return H[name]

    x_h = din("x", [T, E], F32)
    wq_h = din("wq", [E, HD * QH], F32)
    wkva_h = din("wkva", [E, LORA + RD], F32)
    wkvb_h = din("wkvb", [LORA, HD * (NOPE + VD)], F32)
    wo_h = din("wo", [HD * VD, E], F32)
    gamma_h = din("gamma", [128, 4], F32)
    lut_h = din("lut", [1, 16], F32)
    gidx_h = din("gidx", [INTER, E], I32)
    uidx_h = din("uidx", [INTER, E], I32)
    didx_h = din("didx", [E, INTER], I32)
    identb_h = din("identb", [128, 128], F32)
    identf_h = din("identf", [128, 128], F32)
    negI_h = din("negI", [128, 128], F32)
    onesb_h = din("onesb", [128, 1], F32)
    onesf_col_h = din("onesf_col", [128, 1], F32)
    onesf_row_h = din("onesf_row", [1, 128], F32)
    cos2_h = din("cos2", [64, T], F32)
    sin2_h = din("sin2", [64, T], F32)
    step1_h = din("step1", [4 * 128, 512], F32)
    step2_h = din("step2", [4 * 128, 512], F32)

    out_h = nc.dram_tensor("out", [T, E], F32, kind="ExternalOutput").ap()

    q_flat = nc.dram_tensor("q_flat", [HD * QH, T], BF16, kind="Internal").ap()
    ks1_d = nc.dram_tensor("ks1_d", [HD * 128, T], BF16, kind="Internal").ap()
    vtok_d = nc.dram_tensor("vtok_d", [HD * 128, T], BF16, kind="Internal").ap()
    o_d = nc.dram_tensor("o_d", [HD * VD, T], BF16, kind="Internal").ap()
    y_d = nc.dram_tensor("y_d", [T, E], F32, kind="Internal").ap()
    yT_d = nc.dram_tensor("yT_d", [8, NET, 128, 512], BF16, kind="Internal").ap()
    wgT_d = nc.dram_tensor("wgT_d", [NIT, 128, E], BF16, kind="Internal").ap()
    wuT_d = nc.dram_tensor("wuT_d", [NIT, 128, E], BF16, kind="Internal").ap()
    wdT_d = nc.dram_tensor("wdT_d", [NET, 128, INTER], BF16, kind="Internal").ap()

    dbg = {}
    if debug:
        def dout(name, shape, dt):
            dbg[name] = nc.dram_tensor("dbg_" + name, shape, dt, kind="ExternalOutput").ap()
        dout("ckv", [LORA, T], F32)
        dout("kpe", [66, T], F32)
        dout("q", [HD * QH, T], F32)
        dout("ks1", [HD * 128, T], F32)
        dout("vtok", [HD * 128, T], F32)
        dout("o", [HD * VD, T], F32)
        dout("y", [T, E], F32)
        dout("wgT", [NIT, 128, E], F32)
        dout("wdT", [NET, 128, INTER], F32)
        dout("z", [T, E], F32)

    P = Prog(nc)
    stk = contextlib.ExitStack()

    identb = stk.enter_context(nc.sbuf_tensor("s_identb", [128, 128], BF16))
    identf = stk.enter_context(nc.sbuf_tensor("s_identf", [128, 128], F32))
    negI = stk.enter_context(nc.sbuf_tensor("s_negI", [128, 128], BF16))
    onesb = stk.enter_context(nc.sbuf_tensor("s_onesb", [128, 1], BF16))
    onesf_col = stk.enter_context(nc.sbuf_tensor("s_onesf_col", [128, 1], F32))
    onesf_row = stk.enter_context(nc.sbuf_tensor("s_onesf_row", [1, 128], F32))
    step1 = stk.enter_context(nc.sbuf_tensor("s_step1", [128, 4 * 512], BF16))
    step2 = stk.enter_context(nc.sbuf_tensor("s_step2", [128, 4 * 512], BF16))
    ks2 = stk.enter_context(nc.sbuf_tensor("ks2", [66, T], BF16))
    gamma = stk.enter_context(nc.sbuf_tensor("s_gamma", [128, 4], F32))
    lut_col = stk.enter_context(nc.sbuf_tensor("s_lut_col", [128, 16], F32))

    for (t, h) in [(identf, identf_h), (onesf_col, onesf_col_h),
                   (onesf_row, onesf_row_h), (gamma, gamma_h)]:
        P.dma("sync", t[:, :], h)
    for (t, h) in [(identb, identb_h), (negI, negI_h), (onesb, onesb_h)]:
        P.dma("gpsimd", t[:, :], h)
    P.dma("gpsimd", step1[:, :].rearrange("p (v c) -> p v c", c=512),
          step1_h.rearrange("(v p) c -> p v c", p=128))
    P.dma("gpsimd", step2[:, :].rearrange("p (v c) -> p v c", c=512),
          step2_h.rearrange("(v p) c -> p v c", p=128))
    P.dma("gpsimd", lut_col[:, :], lut_h.partition_broadcast(128))
    P.emit("vector", lambda e: e.memset(ks2[64:66, :], 1.0), inc="ve")
    const_pool = stk.enter_context(nc.sbuf_tensor("s_constpool", [128, 8], F32))
    for j, val in enumerate([RMS_EPS, LN_EPS, 1e-20]):
        P.emit("vector", lambda e, j=j, val=val: e.memset(const_pool[:, j:j + 1], val),
               inc="ve")
        nc.const_aps.register(F32, val, const_pool[:, j:j + 1])
    P.barrier()

    cs_stk = contextlib.ExitStack()
    cos2 = cs_stk.enter_context(nc.sbuf_tensor("s_cos2", [64, T], F32))
    sin2 = cs_stk.enter_context(nc.sbuf_tensor("s_sin2", [64, T], F32))
    xf32 = [cs_stk.enter_context(nc.sbuf_tensor(f"xf32_{i}", [128, E], F32))
            for i in range(2)]
    xf32_free = [None, None]
    P.dma("sync", cos2[:, :], cos2_h)
    P.dma("sync", sin2[:, :], sin2_h)
    cs_ev = P.ev("d_sync")
    ckv_stk = contextlib.ExitStack()
    ckv_n = [ckv_stk.enter_context(nc.sbuf_tensor(f"ckvn{i}", [128, T], BF16))
             for i in range(4)]

    def emit_x_chunk_transposes(tc, x_in, xTb, pst, pst_free, xin_free, xT_free):
        t0 = tc * 512
        cv_evs = []
        for ts in range(4):
            sl = ts % 2
            ld = P.dma("sync", xf32[sl][:, :],
                       x_h[t0 + ts * 128: t0 + (ts + 1) * 128, :],
                       waits=[xf32_free[sl]], sem=f"dxld{sl}")
            cv = P.emit("scalar", lambda e, sl=sl, xx=x_in[ts]: e.copy(
                xx[:, :], xf32[sl][:, :]),
                waits=[ld, xin_free[ts]], inc="se")
            xf32_free[sl] = cv
            cv_evs.append(cv)
        x_ev = cv_evs[3]
        xT_ready = None
        for bat in range(8):
            ts, half = divmod(bat, 2)
            pb = pst[bat % 2]
            ev_t = None
            for j in range(8):
                et = half * 8 + j
                w = [pst_free[bat % 2], cv_evs[ts], xT_free] if j == 0 else []
                ev_t = P.emit("tensor",
                              lambda e, pb=pb, xx=x_in[ts], et=et, j=j:
                              e.transpose(pb[:, j * 128:(j + 1) * 128],
                                          xx[:, et * 128:(et + 1) * 128],
                                          identb[:, :]),
                              waits=w, inc="pe" if j == 7 else None)
            out_ap = xTb[:, :].rearrange("p (et f) -> p et f", f=512)[
                :, half * 8:(half + 1) * 8, ts * 128:(ts + 1) * 128]
            in_ap = pb[:, :].rearrange("p (j f) -> p j f", f=128)
            ev_e = P.emit("scalar",
                          lambda e, oa=out_ap, ia=in_ap: e.copy(oa, ia),
                          waits=[ev_t], inc="se")
            pst_free[bat % 2] = ev_e
            xT_ready = ev_e
        return xT_ready, x_ev

    if "p0a" in phases:
        with contextlib.ExitStack() as ph:
            wkva = [ph.enter_context(nc.sbuf_tensor(f"wkva{i}", [128, LORA + RD], BF16))
                    for i in range(NET)]
            x_in = [ph.enter_context(nc.sbuf_tensor(f"xin{i}", [128, E], BF16))
                    for i in range(4)]
            xTb = ph.enter_context(nc.sbuf_tensor("xTb", [128, 8192], BF16))
            ckv_f = ph.enter_context(nc.sbuf_tensor("ckv_f", [128, 4 * 512], F32))
            sq = ph.enter_context(nc.sbuf_tensor("sq", [128, 512], F32))
            ropescr_a = ph.enter_context(nc.sbuf_tensor("ropescr_a", [32, 512], F32))
            ropescr_b = ph.enter_context(nc.sbuf_tensor("ropescr_b", [32, 512], F32))
            rmsrow = ph.enter_context(nc.sbuf_tensor("rmsrow", [1, 512], F32))

            pst = [ph.enter_context(nc.psum_tensor(f"pst{i}", [128, 1024], BF16))
                   for i in range(2)]
            psA = ph.enter_context(nc.psum_tensor("psA", [128, 512], F32))
            psB = ph.enter_context(nc.psum_tensor("psB", [128, 512], F32))
            psR = ph.enter_context(nc.psum_tensor("psR", [128, 512], F32))
            psBC = ph.enter_context(nc.psum_tensor("psBC", [128, 512], F32))

            wk_ev = None
            for i in range(NET):
                wk_ev = P.dma("gpsimd", wkva[i][:, :],
                              wkva_h[i * 128:(i + 1) * 128, :])

            xin_free = [None] * 4
            xT_free = None
            pst_free = [None, None]
            psAB_free = [None, None]
            psR_free = None
            psBC_free = None
            sq_free = None
            ckvf_free = None
            for tc in range(8):
                xT_ready, x_ev = emit_x_chunk_transposes(
                    tc, x_in, xTb, pst, pst_free, xin_free, xT_free)
                ev_r = None
                for mc in range(5):
                    ps = psA if mc % 2 == 0 else psB
                    mw = 128 if mc < 4 else 64
                    ev_mm = None
                    for et in range(NET):
                        w = [psAB_free[mc % 2], xT_ready] if et == 0 else []
                        ev_mm = P.emit("tensor",
                                       lambda e, ps=ps, et=et, mc=mc, mw=mw:
                                       e.matmul(ps[0:mw, 0:512],
                                                wkva[et][:, mc * 128: mc * 128 + mw],
                                                xTb[:, et * 512:(et + 1) * 512],
                                                start=(et == 0), stop=(et == NET - 1)),
                                       waits=w, inc="pe" if et == NET - 1 else None)
                    if mc < 4:
                        ev_cp = P.emit("scalar",
                                       lambda e, ps=ps, mc=mc:
                                       e.copy(ckv_f[:, mc * 512:(mc + 1) * 512], ps[:, 0:512]),
                                       waits=[ev_mm, ckvf_free], inc="se")
                        psAB_free[mc % 2] = ev_cp
                        ev_sq = P.emit("scalar",
                                       lambda e, mc=mc:
                                       e.activation(sq[:, :],
                                                    ckv_f[:, mc * 512:(mc + 1) * 512],
                                                    AF.Square),
                                       waits=[sq_free], inc="se")
                        ev_r = P.emit("tensor",
                                      lambda e, mc=mc:
                                      e.matmul(psR[0:1, 0:512], onesf_col[:, :],
                                               sq[:, :], start=(mc == 0), stop=(mc == 3)),
                                      waits=[ev_sq] + ([psR_free] if mc == 0 else []),
                                      inc="pe")
                        sq_free = ev_r
                    else:
                        w = [ev_mm, cs_ev]
                        t0 = tc * 512
                        x1 = ps[0:32, 0:512]
                        x2 = ps[32:64, 0:512]
                        cS = cos2[0:32, t0:t0 + 512]
                        sS = sin2[0:32, t0:t0 + 512]
                        a = ropescr_a[:, :]
                        b = ropescr_b[:, :]
                        P.emit("vector", lambda e, x1=x1, cS=cS, a=a:
                               e.tensor_tensor(a, x1, cS, ALU.mult), waits=w)
                        P.emit("vector", lambda e, x2=x2, sS=sS, b=b:
                               e.tensor_tensor(b, x2, sS, ALU.mult))
                        P.emit("vector", lambda e, a=a, b=b, t0=t0:
                               e.tensor_tensor(ks2[0:32, t0:t0 + 512], a, b, ALU.subtract))
                        P.emit("vector", lambda e, x2=x2, cS=cS, a=a:
                               e.tensor_tensor(a, x2, cS, ALU.mult))
                        P.emit("vector", lambda e, x1=x1, sS=sS, b=b:
                               e.tensor_tensor(b, x1, sS, ALU.mult))
                        ev_kpe = P.emit("vector", lambda e, a=a, b=b, t0=t0:
                                        e.tensor_tensor(ks2[32:64, t0:t0 + 512], a, b, ALU.add),
                                        inc="ve")
                        psAB_free[mc % 2] = ev_kpe
                ev_ln = P.emit("scalar",
                               lambda e: e.activation(rmsrow[:, :], psR[0:1, 0:512],
                                                      AF.Ln, bias=RMS_EPS, scale=1.0 / LORA),
                               waits=[ev_r], inc="se")
                psR_free = ev_ln
                ev_rstd = P.emit("scalar",
                                 lambda e: e.activation(rmsrow[:, :], rmsrow[:, :],
                                                        AF.Exp, scale=-0.5),
                                 waits=[], inc="se")
                ev_bc = P.emit("tensor",
                               lambda e: e.matmul(psBC[:, 0:512], onesf_row[:, :],
                                                  rmsrow[:, :], start=True, stop=True),
                               waits=[ev_rstd, psBC_free], inc="pe")
                ev_cn = None
                for mc in range(4):
                    ev_cn = P.emit("vector",
                                   lambda e, mc=mc, tc=tc:
                                   e.scalar_tensor_tensor(
                                       ckv_n[mc][:, tc * 512:(tc + 1) * 512],
                                       psBC[:, 0:512], gamma[:, mc:mc + 1],
                                       ckv_f[:, mc * 512:(mc + 1) * 512],
                                       ALU.mult, ALU.mult),
                                   waits=[ev_bc] if mc == 0 else [],
                                   inc="ve" if mc == 3 else None)
                psBC_free = ev_cn
                ckvf_free = ev_cn
                xin_free = [xT_ready] * 4
                xT_free = P.ev("pe")
            if debug:
                wv = [P.ev("ve"), P.ev("se"), P.ev("pe")]
                for mc in range(4):
                    P.dma("gpsimd", dbg["ckv"][mc * 128:(mc + 1) * 128, :],
                          ckv_n[mc][:, :], waits=wv)
                P.dma("gpsimd", dbg["kpe"], ks2[:, :], waits=wv)
            P.barrier()
            P.flush()

    if "p0c" in phases:
        with contextlib.ExitStack() as ph:
            wkvb = [ph.enter_context(nc.sbuf_tensor(f"wkvb{i}", [128, HD * 256], BF16))
                    for i in range(4)]
            ks1buf = [ph.enter_context(nc.sbuf_tensor(f"ks1b{i}", [128, T], BF16))
                      for i in range(2)]
            vbuf = [ph.enter_context(nc.sbuf_tensor(f"vb{i}", [128, T], BF16))
                    for i in range(2)]
            vtkbuf = [ph.enter_context(nc.sbuf_tensor(f"vtk{i}", [128, T], BF16))
                      for i in range(2)]
            psK = [ph.enter_context(nc.psum_tensor(f"psK{i}", [128, 512], F32))
                   for i in range(2)]
            pst = [ph.enter_context(nc.psum_tensor(f"pstc{i}", [128, 1024], BF16))
                   for i in range(2)]

            wb_ev = None
            for i in range(4):
                wb_ev = P.dma("gpsimd", wkvb[i][:, :],
                              wkvb_h[i * 128:(i + 1) * 128, :])
            psK_free = [None, None]
            pst_free = [None, None]
            buf_free = [[None, None], [None, None], [None, None]]
            for h in range(HD):
                sl = h % 2
                for part in range(2):
                    col0 = h * 256 + part * 128
                    dstb = ks1buf[sl] if part == 0 else vbuf[sl]
                    ev_ev = None
                    for tcc in range(8):
                        ps = psK[tcc % 2]
                        ev_mm = None
                        for lt in range(4):
                            w = []
                            if lt == 0:
                                w = [psK_free[tcc % 2], wb_ev,
                                     buf_free[part][sl] if tcc == 0 else None]
                            ev_mm = P.emit("tensor",
                                           lambda e, ps=ps, lt=lt, col0=col0, tcc=tcc:
                                           e.matmul(ps[:, 0:512],
                                                    wkvb[lt][:, col0:col0 + 128],
                                                    ckv_n[lt][:, tcc * 512:(tcc + 1) * 512],
                                                    start=(lt == 0), stop=(lt == 3)),
                                           waits=w, inc="pe" if lt == 3 else None)
                        ev_ev = P.emit("scalar",
                                       lambda e, ps=ps, dstb=dstb, tcc=tcc:
                                       e.copy(dstb[:, tcc * 512:(tcc + 1) * 512],
                                              ps[:, 0:512]),
                                       waits=[ev_mm], inc="se")
                        psK_free[tcc % 2] = ev_ev
                    if part == 0:
                        ev_sp = P.dma("sync", ks1_d[h * 128:(h + 1) * 128, :],
                                      ks1buf[sl][:, :], waits=[ev_ev],
                                      sem=f"dspk{sl}")
                        buf_free[0][sl] = ev_sp
                        if debug:
                            buf_free[0][sl] = P.dma(
                                "gpsimd", dbg["ks1"][h * 128:(h + 1) * 128, :],
                                ks1buf[sl][:, :], waits=[ev_ev], sem=f"dbk{sl}")
                    else:
                        ev_tk = None
                        for b8 in range(4):
                            pb = pst[b8 % 2]
                            ev_t = None
                            for j in range(8):
                                kt = b8 * 8 + j
                                w = [pst_free[b8 % 2], ev_ev,
                                     buf_free[2][sl] if b8 == 0 else None] if j == 0 else []
                                ev_t = P.emit("tensor",
                                              lambda e, pb=pb, vv=vbuf[sl], kt=kt, j=j:
                                              e.transpose(pb[:, j * 128:(j + 1) * 128],
                                                          vv[:, kt * 128:(kt + 1) * 128],
                                                          identb[:, :]),
                                              waits=w, inc="pe" if j == 7 else None)
                            ev_tk = P.emit("scalar",
                                           lambda e, pb=pb, dst=vtkbuf[sl], b8=b8:
                                           e.copy(dst[:, b8 * 1024:(b8 + 1) * 1024],
                                                  pb[:, :]),
                                           waits=[ev_t], inc="se")
                            pst_free[b8 % 2] = ev_tk
                        ev_sp = P.dma("sync", vtok_d[h * 128:(h + 1) * 128, :],
                                      vtkbuf[sl][:, :], waits=[ev_tk],
                                      sem=f"dspv{sl}")
                        buf_free[2][sl] = ev_sp
                        buf_free[1][sl] = P.ev("pe")
                        if debug:
                            buf_free[2][sl] = P.dma(
                                "gpsimd", dbg["vtok"][h * 128:(h + 1) * 128, :],
                                vtkbuf[sl][:, :], waits=[ev_tk], sem=f"dbv{sl}")
            P.barrier()
            P.flush()
    ckv_stk.close()

    if "p0b" in phases:
        with contextlib.ExitStack() as ph:
            wqb = [ph.enter_context(nc.sbuf_tensor(f"wqb{i}", [128, HD * QH], BF16))
                   for i in range(NET)]
            x_in = [ph.enter_context(nc.sbuf_tensor(f"xinb{i}", [128, E], BF16))
                    for i in range(4)]
            xTb = ph.enter_context(nc.sbuf_tensor("xTbb", [128, 8192], BF16))
            qev = [ph.enter_context(nc.sbuf_tensor(f"qev{i}", [128, 512], BF16))
                   for i in range(2)]
            rs_a = ph.enter_context(nc.sbuf_tensor("rs_a", [64, 512], F32))
            rs_b = ph.enter_context(nc.sbuf_tensor("rs_b", [64, 512], F32))
            ropeo = [ph.enter_context(nc.sbuf_tensor(f"ropeo{i}", [128, 512], BF16))
                     for i in range(2)]
            pst = [ph.enter_context(nc.psum_tensor(f"pstb{i}", [128, 1024], BF16))
                   for i in range(2)]
            psQ = [ph.enter_context(nc.psum_tensor(f"psQ{i}", [128, 512], F32))
                   for i in range(2)]

            wq_ev = None
            for i in range(NET):
                wq_ev = P.dma("gpsimd", wqb[i][:, :],
                              wq_h[i * 128:(i + 1) * 128, :])
            xin_free = [None] * 4
            xT_free = None
            pst_free = [None, None]
            psQ_free = [None, None]
            qev_free = [None, None]
            ropeo_free = [None, None]
            for tc in range(8):
                t0 = tc * 512
                xT_ready, x_ev = emit_x_chunk_transposes(
                    tc, x_in, xTb, pst, pst_free, xin_free, xT_free)
                for m in range(24):
                    ps = psQ[m % 2]
                    ev_mm = None
                    for et in range(NET):
                        w = [psQ_free[m % 2], xT_ready, wq_ev] if et == 0 else []
                        ev_mm = P.emit("tensor",
                                       lambda e, ps=ps, et=et, m=m:
                                       e.matmul(ps[:, 0:512],
                                                wqb[et][:, m * 128:(m + 1) * 128],
                                                xTb[:, et * 512:(et + 1) * 512],
                                                start=(et == 0), stop=(et == NET - 1)),
                                       waits=w, inc="pe" if et == NET - 1 else None)
                    if m < 16:
                        ev_c = P.emit("scalar",
                                      lambda e, ps=ps, m=m, sl=m % 2:
                                      e.copy(qev[sl][:, :], ps[:, 0:512]),
                                      waits=[ev_mm, qev_free[m % 2]], inc="se")
                        psQ_free[m % 2] = ev_c
                        ev_sp = P.dma("sync", q_flat[m * QH: m * QH + 128, t0:t0 + 512],
                                      qev[m % 2][:, :], waits=[ev_c],
                                      sem=f"dspq{m % 2}")
                        qev_free[m % 2] = ev_sp
                        if debug:
                            qev_free[m % 2] = P.dma(
                                "gpsimd", dbg["q"][m * QH: m * QH + 128, t0:t0 + 512],
                                qev[m % 2][:, :], waits=[ev_c], sem=f"dbq{m % 2}")
                    else:
                        j = m - 16
                        x1 = ps[0:64, 0:512]
                        x2 = ps[64:128, 0:512]
                        cS = cos2[:, t0:t0 + 512]
                        sS = sin2[:, t0:t0 + 512]
                        ro = ropeo[j % 2]
                        w0 = [ev_mm, ropeo_free[j % 2]]
                        P.emit("vector", lambda e, x1=x1, cS=cS, a=rs_a:
                               e.tensor_tensor(a[:, :], x1, cS, ALU.mult), waits=w0)
                        P.emit("vector", lambda e, x2=x2, sS=sS, b=rs_b:
                               e.tensor_tensor(b[:, :], x2, sS, ALU.mult))
                        P.emit("vector", lambda e, ro=ro:
                               e.tensor_tensor(ro[0:64, :], rs_a[:, :], rs_b[:, :],
                                               ALU.subtract))
                        P.emit("vector", lambda e, x2=x2, cS=cS, a=rs_a:
                               e.tensor_tensor(a[:, :], x2, cS, ALU.mult))
                        P.emit("vector", lambda e, x1=x1, sS=sS, b=rs_b:
                               e.tensor_tensor(b[:, :], x1, sS, ALU.mult))
                        ev_ro = P.emit("vector", lambda e, ro=ro:
                                       e.tensor_tensor(ro[64:128, :], rs_a[:, :], rs_b[:, :],
                                                       ALU.add), inc="ve")
                        psQ_free[m % 2] = ev_ro
                        for two in range(2):
                            hh = 2 * j + two
                            for half in range(2):
                                in_ap = ro[half * 64 + two * 32: half * 64 + two * 32 + 32, :]
                                r0 = hh * QH + 128 + half * 32
                                ropeo_free[j % 2] = P.dma(
                                    "sync", q_flat[r0:r0 + 32, t0:t0 + 512],
                                    in_ap, waits=[ev_ro], sem=f"dspr{j % 2}")
                                if debug:
                                    ropeo_free[j % 2] = P.dma(
                                        "gpsimd", dbg["q"][r0:r0 + 32, t0:t0 + 512],
                                        in_ap, waits=[ev_ro], sem=f"dbr{j % 2}")
                xin_free = [xT_ready] * 4
                xT_free = P.ev("pe")
            P.barrier()
            P.flush()


    cs_stk.close()

    if "p1" in phases:
        with contextlib.ExitStack() as ph:
            qs1 = [ph.enter_context(nc.sbuf_tensor(f"qs1_{i}", [128, T], BF16)) for i in range(2)]
            qs2 = [ph.enter_context(nc.sbuf_tensor(f"qs2_{i}", [66, T], BF16)) for i in range(2)]
            ks1 = [ph.enter_context(nc.sbuf_tensor(f"ks1p_{i}", [128, T], BF16)) for i in range(2)]
            vtk = [ph.enter_context(nc.sbuf_tensor(f"vtkp_{i}", [128, T], BF16)) for i in range(2)]
            sqb = ph.enter_context(nc.sbuf_tensor("sqb", [128, T], BF16))
            sq2b = ph.enter_context(nc.sbuf_tensor("sq2b", [64, T], BF16))
            nk2sl = [ph.enter_context(nc.sbuf_tensor(f"nk2_{i}", [1, T], F32))
                     for i in range(2)]
            nk2pe = ph.enter_context(nc.sbuf_tensor("nk2pe", [1, T], BF16))
            pmax = ph.enter_context(nc.sbuf_tensor("pmax", [1, T], F32))
            rs_row_t = ph.enter_context(nc.sbuf_tensor("rs_row", [1, 512], F32))
            rs_row = [rs_row_t, rs_row_t]
            rSb = [ph.enter_context(nc.sbuf_tensor(f"rSb{i}", [128, 512], F32))
                   for i in range(2)]
            pT = [ph.enter_context(nc.sbuf_tensor(f"pT{i}", [128, 512], BF16)) for i in range(2)]
            ob = [ph.enter_context(nc.sbuf_tensor(f"ob{i}", [128, 512], BF16)) for i in range(2)]

            psST = [ph.enter_context(nc.psum_tensor(f"psST_{i}", [128, 512], F32)) for i in range(2)]
            psAV_ = [ph.enter_context(nc.psum_tensor(f"psAV{i}", [128, 512], F32)) for i in range(2)]
            psS_t = ph.enter_context(nc.psum_tensor("psS0", [1, 512], F32))
            psS = [psS_t, psS_t]
            psB = ph.enter_context(nc.psum_tensor("psBn", [128, 512], F32))
            psN = ph.enter_context(nc.psum_tensor("psNn", [1, 512], F32))
            fw_inline = ("fw" in phases)
            if fw_inline:
                fwi32_t = ph.enter_context(nc.sbuf_tensor("fwi32", [128, E], I32))
                fwi32 = [fwi32_t, fwi32_t]
                fwidxb = ph.enter_context(nc.sbuf_tensor("fwidxb", [128, E], BF16))
                fwidxT_ = [ph.enter_context(nc.sbuf_tensor(f"fwidxT{i}", [128, E], F32))
                           for i in range(2)]
                fwmask = ph.enter_context(nc.sbuf_tensor("fwmask", [128, E], BF16))
                fwtmp = ph.enter_context(nc.sbuf_tensor("fwtmp", [128, E], BF16))
                fwacc = [ph.enter_context(nc.sbuf_tensor(f"fwacc{i}", [128, E], BF16))
                         for i in range(2)]
                psFT = ph.enter_context(nc.psum_tensor("psFT", [128, 1024], BF16))

            ev0 = P.emit("scalar", lambda e: e.activation(sq2b[:, :], ks2[0:64, :], AF.Square),
                         inc="se")
            psN_free = None
            for c8 in range(8):
                ev_m = P.emit("tensor",
                              lambda e, c8=c8: e.matmul(psN[0:1, 0:512], onesb[0:64, :],
                                                        sq2b[:, c8 * 512:(c8 + 1) * 512],
                                                        start=True, stop=True),
                              waits=[ev0, psN_free], inc="pe")
                psN_free = P.emit("vector",
                                  lambda e, c8=c8: e.tensor_copy(
                                      nk2pe[0:1, c8 * 512:(c8 + 1) * 512], psN[0:1, 0:512]),
                                  waits=[ev_m], inc="ve")

            def load_head(h, slot, waits):
                e1 = P.dma("sync", qs1[slot][:, :], q_flat[h * QH: h * QH + 128, :],
                           waits=waits, sem=f"dldq{slot}")
                P.dma("sync", qs2[slot][0:64, :], q_flat[h * QH + 128: h * QH + QH, :],
                      waits=[], sem=f"dldq{slot}")
                P.dma("sync", ks1[slot][:, :], ks1_d[h * 128:(h + 1) * 128, :],
                      waits=[], sem=f"dldq{slot}")
                ev = P.dma("sync", vtk[slot][:, :], vtok_d[h * 128:(h + 1) * 128, :],
                           waits=[], sem=f"dldq{slot}")
                return ev

            slot_free = [None, None]
            ld_ev = [None, None]
            ld_ev[0] = load_head(0, 0, [])
            ld_ev[1] = load_head(1, 1, [])
            psST_free = [None, None]
            psAV_free = [None, None]
            psS_free = [None, None]
            pend_fin = [None]
            psB_free = None
            pT_free = [None, None]
            ob_free = [None, None]
            rs_free = [None]
            rSb_free = [None, None]
            sq_free = None
            ct_done = [None, None]
            head_ct = [None, None]
            aug_done = [None, None]

            def emit_norms(h, sl, ld):
                nonlocal psN_free, sq_free
                nk2 = nk2sl[sl]
                ev_k1 = P.emit("scalar", lambda e, sl=sl: e.activation(
                    sqb[:, :], ks1[sl][:, :], AF.Square), waits=[ld, sq_free], inc="se")
                for c8 in range(8):
                    P.emit("tensor", lambda e, c8=c8: e.matmul(
                        psN[0:1, 0:512], onesb[:, :], sqb[:, c8 * 512:(c8 + 1) * 512],
                        start=True, stop=False),
                        waits=[ev_k1 if c8 == 0 else None, psN_free], inc=None)
                    ev_m = P.emit("tensor", lambda e, c8=c8: e.matmul(
                        psN[0:1, 0:512], onesb[0:1, :],
                        nk2pe[0:1, c8 * 512:(c8 + 1) * 512],
                        start=False, stop=True), waits=[], inc="pe")
                    psN_free = P.emit("scalar", lambda e, nk2=nk2, c8=c8: e.copy(
                        nk2[0:1, c8 * 512:(c8 + 1) * 512], psN[0:1, 0:512]),
                        waits=[ev_m, ct_done[sl]], inc="se")
                ev_pm = P.emit("vector", lambda e, nk2=nk2: e.tensor_tensor_scan(
                    pmax[0:1, :], nk2[0:1, :], nk2[0:1, :], -1e30, ALU.max, ALU.bypass),
                    waits=[psN_free], inc="ve")
                ev_q1 = P.emit("scalar", lambda e, sl=sl: e.activation(
                    sqb[:, :], qs1[sl][:, :], AF.Square), waits=[P.ev("pe")], inc="se")
                ev_q2 = P.emit("scalar", lambda e, sl=sl: e.activation(
                    sq2b[:, :], qs2[sl][0:64, :], AF.Square), waits=[], inc="se")
                ev_ct = None
                for c8 in range(8):
                    P.emit("tensor", lambda e, c8=c8: e.matmul(
                        psN[0:1, 0:512], onesb[:, :], sqb[:, c8 * 512:(c8 + 1) * 512],
                        start=True, stop=False),
                        waits=[ev_q1 if c8 == 0 else None, psN_free], inc=None)
                    ev_m = P.emit("tensor", lambda e, c8=c8: e.matmul(
                        psN[0:1, 0:512], onesb[0:64, :], sq2b[:, c8 * 512:(c8 + 1) * 512],
                        start=False, stop=True),
                        waits=[ev_q2] if c8 == 0 else [], inc="pe")
                    psN_free = P.emit("vector", lambda e, nk2=nk2, c8=c8: e.tensor_tensor(
                        nk2[0:1, c8 * 512:(c8 + 1) * 512], psN[0:1, 0:512],
                        pmax[0:1, c8 * 512:(c8 + 1) * 512], ALU.add),
                        waits=[ev_m, ev_pm if c8 == 0 else None], inc="ve")
                    ev_ct = psN_free
                sq_free = P.ev("pe")
                ct_done[sl] = ev_ct
                ev_aug = P.emit("scalar", lambda e, sl=sl, nk2=nk2: e.mul(
                    qs2[sl][64:65, :], nk2[0:1, :], -0.5), waits=[ev_ct], inc="se")
                aug_done[sl] = ev_aug
                return ev_ct

            def emit_fin(h, sl, J, uu, ev_av, ev_be):
                nonlocal psAV_free, pT_free, ob_free
                qb5 = J * 512
                ev_ob = P.emit("vector", lambda e, uu=uu, J=J: e.tensor_tensor(
                    ob[uu % 2][:, :], psAV_[uu % 2][:, 0:512], rSb[uu % 2][:, :],
                    ALU.mult),
                    waits=[ev_av, ev_be, ob_free[uu % 2]], inc="ve")
                rSb_free[uu % 2] = ev_ob
                psAV_free[uu % 2] = ev_ob
                ev_sp = P.dma("sync", o_d[h * 128:(h + 1) * 128, qb5:qb5 + 512],
                              ob[uu % 2][:, :], waits=[ev_ob], sem=f"dspo{uu % 2}")
                ob_free[uu % 2] = ev_sp
                if debug:
                    ob_free[uu % 2] = P.dma(
                        "gpsimd", dbg["o"][h * 128:(h + 1) * 128, qb5:qb5 + 512],
                        ob[uu % 2][:, :], waits=[ev_ob], sem=f"dbo{uu % 2}")

            def emit_pass2(h, sl, J, uu):
                nonlocal psST_free, pT_free, psB_free
                qb5 = J * 512
                ktmax = 4 * (J + 1)
                ev_av = None
                ev_s = None
                sS = psS[0]
                psAV = psAV_[uu % 2]

                def emit_av(kt, ev_pt):
                    nonlocal ev_av, ev_s
                    w_av = [ev_pt] if kt > 0 else [ev_pt, psAV_free[uu % 2]]
                    ev_av = P.emit("tensor", lambda e, psAV=psAV, sl=sl, kt=kt, ktmax=ktmax: e.matmul(
                        psAV[:, 0:512], vtk[sl][:, kt * 128:(kt + 1) * 128],
                        pT[kt % 2][:, :], start=(kt == 0), stop=(kt == ktmax - 1)),
                        waits=w_av, inc="pe")
                    ev_s = P.emit("tensor", lambda e, sS=sS, kt=kt, ktmax=ktmax: e.matmul(
                        sS[0:1, 0:512], onesb[:, :], pT[kt % 2][:, :],
                        start=(kt == 0), stop=(kt == ktmax - 1)),
                        waits=[psS_free[0]] if kt == 0 else [],
                        inc="pe" if kt == ktmax - 1 else None)
                    pT_free[kt % 2] = ev_s

                pend_av = None
                for kt in range(ktmax):
                    st = psST[kt % 2]
                    d = kt - 4 * J
                    P.emit("tensor", lambda e, st=st, sl=sl, kt=kt, qb5=qb5: e.matmul(
                        st[:, 0:512], ks1[sl][:, kt * 128:(kt + 1) * 128],
                        qs1[sl][:, qb5:qb5 + 512], start=True, stop=False),
                        waits=[psST_free[kt % 2], aug_done[sl]] if kt == 0
                        else [psST_free[kt % 2]])
                    ev_mm = P.emit("tensor",
                                   lambda e, st=st, sl=sl, kt=kt, qb5=qb5, d=d: e.matmul(
                                       st[:, 0:512], ks2[0:65, kt * 128:(kt + 1) * 128],
                                       qs2[sl][0:65, qb5:qb5 + 512],
                                       start=False, stop=(d < 0)),
                                   inc=None if d >= 0 else "pe")
                    if d >= 0:
                        ev_mm = P.emit("tensor", lambda e, st=st, d=d: e.matmul(
                            st[:, 0:512], negI[:, :], step2[:, d * 512:(d + 1) * 512],
                            start=False, stop=True), inc="pe")
                    ev_pt = P.emit("scalar", lambda e, st=st, kt=kt: e.activation(
                        pT[kt % 2][:, :], st[:, 0:512], AF.Exp, scale=SCALE),
                        waits=[ev_mm, pT_free[kt % 2]], inc="se")
                    psST_free[kt % 2] = ev_pt
                    if pend_av is not None:
                        emit_av(*pend_av)
                    pend_av = (kt, ev_pt)
                emit_av(*pend_av)
                ev_ln = P.emit("scalar", lambda e, sS=sS: e.activation(
                    rs_row[0][0:1, :], sS[0:1, 0:512], AF.Ln),
                    waits=[ev_s, rs_free[0]], inc="se")
                psS_free[0] = ev_ln
                ev_rs = P.emit("scalar", lambda e: e.activation(
                    rs_row[0][0:1, :], rs_row[0][0:1, :], AF.Exp, scale=-1.0),
                    waits=[ev_ln], inc="se")
                ev_bc = P.emit("tensor", lambda e: e.matmul(
                    psB[:, 0:512], onesf_row[:, :], rs_row[0][0:1, :],
                    start=True, stop=True), waits=[ev_rs, psB_free], inc="pe")
                rs_free[0] = ev_bc
                ev_be = P.emit("scalar", lambda e, uu=uu: e.copy(
                    rSb[uu % 2][:, :], psB[:, 0:512]),
                    waits=[ev_bc, rSb_free[uu % 2]], inc="se")
                psB_free = ev_be
                return (ev_av, ev_be)

            def fw_generator():
                i32_free = [None, None]
                idxb_free = None
                idxT_free = [None, None]
                mask_free = None
                accf = [None, None]
                jobs = []
                for it in range(NIT):
                    jobs.append((gidx_h, it * 128, 0, E, wgT_d[it],
                                 dbg.get("wgT")[it] if (debug and dbg.get("wgT") is not None) else None))
                for it in range(NIT):
                    jobs.append((uidx_h, it * 128, 0, E, wuT_d[it], None))
                for et in range(NET):
                    for (c0, w) in ((0, 2048), (2048, 2048), (4096, 1408)):
                        jobs.append((didx_h, et * 128, c0, w,
                                     wdT_d[et][:, c0:c0 + w],
                                     dbg.get("wdT")[et][:, c0:c0 + w]
                                     if (debug and dbg.get("wdT") is not None) else None))
                def prep(si):
                    nonlocal idxb_free
                    (src_h, r0, c0, width, dst_ap, dbg_ap) = jobs[si]
                    sl = si % 2
                    fwidxT = fwidxT_[sl]
                    ev_ld = P.dma("sync", fwi32[sl][:, 0:width],
                                  src_h[r0:r0 + 128, c0:c0 + width],
                                  waits=[i32_free[0], i32_free[1]], sem=f"dfwl{sl}")
                    yield
                    ev_cv = P.emit("scalar", lambda e, sl=sl, width=width: e.copy(
                        fwidxb[:, 0:width], fwi32[sl][:, 0:width]),
                        waits=[ev_ld, idxb_free], inc="se")
                    i32_free[0] = ev_cv
                    nbl = width // 128
                    ev_ev = None
                    for b0 in range(0, nbl, 8):
                        nb = min(8, nbl - b0)
                        ev_t = None
                        for j in range(nb):
                            blk = b0 + j
                            w = ([psFT_free[0], ev_cv] +
                                 ([idxT_free[sl]] if b0 == 0 else [])) if j == 0 else []
                            ev_t = P.emit("tensor", lambda e, fwidxT=fwidxT, blk=blk, j=j:
                                          e.transpose(psFT[:, j * 128:(j + 1) * 128],
                                                      fwidxb[:, blk * 128:(blk + 1) * 128],
                                                      identb[:, :]),
                                          waits=w, inc="pe" if j == nb - 1 else None)
                        ev_ev = P.emit("scalar", lambda e, fwidxT=fwidxT, b0=b0, nb=nb:
                                       e.copy(fwidxT[:, b0 * 128:(b0 + nb) * 128],
                                              psFT[:, 0:nb * 128]),
                                       waits=[ev_t], inc="se")
                        psFT_free[0] = ev_ev
                        yield
                    idxb_free = P.ev("pe")
                    prep_ev[si % 2] = ev_ev

                prep_ev = [None, None]
                g0 = prep(0)
                for _ in g0:
                    pass
                for si in range(len(jobs)):
                    (src_h, r0, c0, width, dst_ap, dbg_ap) = jobs[si]
                    sl = si % 2
                    fwidxT = fwidxT_[sl]
                    ev_ev = prep_ev[sl]
                    pgen = prep(si + 1) if si + 1 < len(jobs) else None
                    ev_k = None
                    for k in range(16):
                        P.emit("vector", lambda e, fwidxT=fwidxT, k=k, width=width:
                               e.tensor_scalar(fwmask[:, 0:width], fwidxT[:, 0:width],
                                               float(k), None, ALU.is_equal),
                               waits=[ev_ev, mask_free] if k == 0 else [])
                        last = (k == 15)
                        if k == 0:
                            ev_k = P.emit("vector", lambda e, sl=sl, width=width:
                                          e.tensor_scalar(
                                              fwacc[sl][:, 0:width], fwmask[:, 0:width],
                                              lut_col[:, 0:1], None, ALU.mult),
                                          waits=[accf[sl]])
                        else:
                            P.emit("vector", lambda e, k=k, width=width:
                                   e.tensor_scalar(fwtmp[:, 0:width], fwmask[:, 0:width],
                                                   lut_col[:, k:k + 1], None, ALU.mult),
                                   waits=[])
                            ev_k = P.emit("vector", lambda e, sl=sl, width=width:
                                          e.tensor_tensor(
                                              fwacc[sl][:, 0:width], fwtmp[:, 0:width],
                                              fwacc[sl][:, 0:width], ALU.add),
                                          waits=[], inc="ve" if last else None)
                        if pgen is not None and k % 4 == 3:
                            for _ in pgen:
                                break
                        if k % 4 == 3:
                            yield
                    if pgen is not None:
                        for _ in pgen:
                            pass
                    mask_free = ev_k
                    idxT_free[sl] = ev_k
                    ev_sp = P.dma("sync", dst_ap, fwacc[sl][:, 0:width],
                                  waits=[ev_k], sem=f"dfws{sl}")
                    accf[sl] = ev_sp
                    if dbg_ap is not None:
                        accf[sl] = P.dma("gpsimd", dbg_ap, fwacc[sl][:, 0:width],
                                         waits=[ev_k], sem=f"dfwd{sl}")
                    yield

            psFT_free = [None]
            fw_it = fw_generator() if fw_inline else None
            fw_live = fw_inline

            def pump(n):
                nonlocal fw_live
                if not fw_live:
                    return
                try:
                    for _ in range(n):
                        next(fw_it)
                except StopIteration:
                    fw_live = False

            units = [(h, J) for h in range(HD) for J in range(8)]

            def after_pass2(hh, ssl, JJ):
                if JJ == 7:
                    slot_free[ssl] = P.ev("pe")
                    if hh + 2 < HD:
                        ld_ev[ssl] = load_head(hh + 2, ssl, [slot_free[ssl]])

            head_ct[0] = emit_norms(0, 0, ld_ev[0])
            for uu, (h, J) in enumerate(units):
                sl = h % 2
                pump(4)
                evs = emit_pass2(h, sl, J, uu)
                if pend_fin[0] is not None:
                    emit_fin(*pend_fin[0])
                    after_pass2(pend_fin[0][0], pend_fin[0][1], pend_fin[0][2])
                pend_fin[0] = (h, sl, J, uu, evs[0], evs[1])
                if J == 4 and h + 1 < HD:
                    head_ct[1 - sl] = emit_norms(h + 1, 1 - sl, ld_ev[1 - sl])
                pump(4)
            emit_fin(*pend_fin[0])
            after_pass2(pend_fin[0][0], pend_fin[0][1], pend_fin[0][2])
            pump(10 ** 9)
            P.barrier()
            P.flush()

    if "p2" in phases:
        with contextlib.ExitStack() as ph:
            wo_b = [ph.enter_context(nc.sbuf_tensor(f"wo{i}", [128, E], BF16))
                    for i in range(NET)]
            oslab = [ph.enter_context(nc.sbuf_tensor(f"osl{i}", [128, 8192], BF16))
                     for i in range(2)]
            x_in = [ph.enter_context(nc.sbuf_tensor(f"xr{i}", [128, E], F32))
                    for i in range(4)]
            z = ph.enter_context(nc.sbuf_tensor("z", [128, E], F32))
            zsq = ph.enter_context(nc.sbuf_tensor("zsq", [128, E], F32))
            yb = ph.enter_context(nc.sbuf_tensor("yb", [128, E], F32))
            ybf = ph.enter_context(nc.sbuf_tensor("ybf", [128, E], BF16))
            yT_buf = ph.enter_context(nc.sbuf_tensor("yT_buf", [128, E], BF16))
            s1 = ph.enter_context(nc.sbuf_tensor("s1", [128, 1], F32))
            s2 = ph.enter_context(nc.sbuf_tensor("s2", [128, 1], F32))
            negmean = ph.enter_context(nc.sbuf_tensor("negmean", [128, 1], F32))
            msq = ph.enter_context(nc.sbuf_tensor("msq", [128, 1], F32))
            var = ph.enter_context(nc.sbuf_tensor("var", [128, 1], F32))
            rstd = ph.enter_context(nc.sbuf_tensor("rstd", [128, 1], F32))
            psO = ph.enter_context(nc.psum_tensor("psO", [128, 2048], F32))
            pst = [ph.enter_context(nc.psum_tensor(f"pstp2{i}", [128, 1024], BF16))
                   for i in range(2)]

            wo_ev = None
            for i in range(NET):
                wo_ev = P.dma("gpsimd", wo_b[i][:, :], wo_h[i * 128:(i + 1) * 128, :])
            oslab_free = [None, None]
            x_free = None
            pst_free = [None, None]
            psO_free = None
            z_free = None
            zsq_free = None
            ybf_free = None
            yT_free = None
            stat_free = None
            ld_os = [None, None]
            ld_x = None

            def load_t512(tt, sl, waits):
                ev = None
                for dt in range(NET):
                    ev = P.dma("sync", oslab[sl][:, dt * 512:(dt + 1) * 512],
                               o_d[dt * 128:(dt + 1) * 128, tt * 512:(tt + 1) * 512],
                               waits=waits if dt == 0 else [], sem=f"dslo{sl}")
                return ev

            ld_os[0] = load_t512(0, 0, [])
            for tt in range(8):
                sl = tt % 2
                ld = ld_os[sl]
                ldx = None
                for ts4 in range(4):
                    ldx = P.dma("sync", x_in[ts4][:, :],
                                x_h[tt * 512 + ts4 * 128: tt * 512 + (ts4 + 1) * 128, :],
                                waits=[x_free] if ts4 == 0 else [], sem="dslx")
                if tt + 1 < 8:
                    ld_os[1 - sl] = load_t512(tt + 1, 1 - sl, [oslab_free[1 - sl]])
                for ts in range(4):
                    ev_z = None
                    for ec in range(4):
                        ev_mm = None
                        for dt in range(NET):
                            w = [ld, ldx, wo_ev, psO_free] if (ec == 0 and dt == 0) else []
                            ev_mm = P.emit("tensor",
                                           lambda e, dt=dt, ec=ec, ts=ts, sl=sl: e.matmul(
                                               psO[:, ec * 512:(ec + 1) * 512],
                                               oslab[sl][:, dt * 512 + ts * 128: dt * 512 + (ts + 1) * 128],
                                               wo_b[dt][:, ec * 512:(ec + 1) * 512],
                                               start=(dt == 0), stop=(dt == NET - 1)),
                                           waits=w, inc="pe" if dt == NET - 1 else None)
                        ev_z = P.emit("vector",
                                      lambda e, ec=ec, ts=ts: e.tensor_tensor(
                                          z[:, ec * 512:(ec + 1) * 512],
                                          psO[:, ec * 512:(ec + 1) * 512],
                                          x_in[ts][:, ec * 512:(ec + 1) * 512], ALU.add),
                                      waits=[ev_mm] + ([z_free] if ec == 0 else []),
                                      inc="ve")
                    psO_free = ev_z
                    if debug:
                        t0d = tt * 512 + ts * 128
                        P.dma("gpsimd", dbg["z"][t0d:t0d + 128, :], z[:, :],
                              waits=[ev_z], sem="dbz")
                        z_free = P.ev("dbz")
                    ev_sq = P.emit("scalar", lambda e: e.activation(
                        zsq[:, :], z[:, :], AF.Square), waits=[ev_z, zsq_free], inc="se")
                    ev_s1 = P.emit("vector", lambda e: e.reduce_sum(
                        s1[:, :], z[:, :], axis=AX.X), waits=[stat_free], inc="ve")
                    ev_s2 = P.emit("vector", lambda e: e.reduce_sum(
                        s2[:, :], zsq[:, :], axis=AX.X), waits=[ev_sq], inc="ve")
                    ev_nm = P.emit("vector", lambda e: e.tensor_scalar(
                        negmean[:, :], s1[:, :], -1.0 / E, None, ALU.mult),
                        waits=[ev_s1], inc="ve")
                    ev_mq = P.emit("vector", lambda e: e.tensor_tensor(
                        msq[:, :], negmean[:, :], negmean[:, :], ALU.mult),
                        waits=[ev_nm], inc="ve")
                    ev_vr = P.emit("vector", lambda e: e.scalar_tensor_tensor(
                        var[:, :], s2[:, :], 1.0 / E, msq[:, :], ALU.mult, ALU.subtract),
                        waits=[ev_mq, ev_s2], inc="ve")
                    ev_lv = P.emit("scalar", lambda e: e.activation(
                        rstd[:, :], var[:, :], AF.Ln, bias=LN_EPS), waits=[ev_vr], inc="se")
                    ev_rs = P.emit("scalar", lambda e: e.activation(
                        rstd[:, :], rstd[:, :], AF.Exp, scale=-0.5), waits=[ev_lv], inc="se")
                    ev_y0 = P.emit("vector", lambda e: e.tensor_scalar(
                        zsq[:, :], z[:, :], negmean[:, :], None, ALU.add),
                        waits=[ev_rs, ev_s2], inc="ve")
                    ev_y = P.emit("vector", lambda e: e.tensor_scalar(
                        yb[:, :], zsq[:, :], rstd[:, :], None, ALU.mult),
                        waits=[ev_y0], inc="ve")
                    z_free = ev_y
                    zsq_free = ev_s2
                    stat_free = ev_y
                    t0 = tt * 512 + ts * 128
                    ev_spy = P.dma("sync", y_d[t0:t0 + 128, :], yb[:, :],
                                   waits=[ev_y], sem="dspy")
                    if debug:
                        ev_spy = P.dma("gpsimd", dbg["y"][t0:t0 + 128, :], yb[:, :],
                                       waits=[ev_y], sem="dspy2")
                    ev_yb = P.emit("scalar", lambda e: e.copy(ybf[:, :], yb[:, :]),
                                   waits=[ev_y, ybf_free], inc="se")
                    ev_tb = None
                    for bat in range(2):
                        pb = pst[bat]
                        ev_t = None
                        for j in range(8):
                            et = bat * 8 + j
                            w = [pst_free[bat], ev_yb] if j == 0 else []
                            ev_t = P.emit("tensor", lambda e, pb=pb, et=et, j=j: e.transpose(
                                pb[:, j * 128:(j + 1) * 128],
                                ybf[:, et * 128:(et + 1) * 128], identb[:, :]),
                                waits=w, inc="pe" if j == 7 else None)
                        ev_tb = P.emit("scalar", lambda e, pb=pb, bat=bat: e.copy(
                            yT_buf[:, bat * 1024:(bat + 1) * 1024], pb[:, :]),
                            waits=[ev_t] + ([yT_free] if bat == 0 else []), inc="se")
                        pst_free[bat] = ev_tb
                    ybf_free = P.ev("pe")
                    out_ap = yT_d[tt, :, :, ts * 128:(ts + 1) * 128].rearrange(
                        "et p k -> p et k")
                    in_ap = yT_buf[:, :].rearrange("p (et k) -> p et k", k=128)
                    yT_free = P.dma("sync", out_ap, in_ap, waits=[ev_tb], sem="dspyt")
                oslab_free[sl] = P.ev("pe")
                x_free = P.ev("ve")
            P.barrier()
            P.flush()

    if "fw" in phases and "p1" not in phases:
        with contextlib.ExitStack() as ph:
            idxb = [ph.enter_context(nc.sbuf_tensor(f"idxb{i}", [128, INTER], BF16))
                    for i in range(2)]
            idxT = [ph.enter_context(nc.sbuf_tensor(f"idxT{i}", [128, INTER], F32))
                    for i in range(2)]
            maskb = ph.enter_context(nc.sbuf_tensor("maskb", [128, INTER], BF16))
            acc = [ph.enter_context(nc.sbuf_tensor(f"facc{i}", [128, INTER], BF16))
                   for i in range(2)]
            pst = [ph.enter_context(nc.psum_tensor(f"pstf{i}", [128, 1024], BF16))
                   for i in range(2)]

            pst_free = [None, None]
            idxb_free = [None, None]
            idxT_free = [None, None]
            acc_free = [None, None]
            mask_free = None

            def materialize(src_h, dst_d, dbg_d, tiles, width, row0fn):
                nonlocal mask_free
                for it in range(tiles):
                    sl = it % 2
                    r0 = row0fn(it)
                    ev_ld = P.dma("gpsimd", idxb[sl][:, 0:width],
                                  src_h[r0:r0 + 128, :], waits=[idxb_free[sl]],
                                  sem=f"dfw{sl}")
                    nbl = width // 128
                    ev_ev = None
                    for b0 in range(0, nbl, 8):
                        nb = min(8, nbl - b0)
                        pb = pst[(b0 // 8) % 2]
                        ev_t = None
                        for j in range(nb):
                            blk = b0 + j
                            w = [pst_free[(b0 // 8) % 2], ev_ld,
                                 idxT_free[sl] if b0 == 0 else None] if j == 0 else []
                            ev_t = P.emit("tensor", lambda e, pb=pb, sl=sl, blk=blk, j=j:
                                          e.transpose(pb[:, j * 128:(j + 1) * 128],
                                                      idxb[sl][:, blk * 128:(blk + 1) * 128],
                                                      identb[:, :]),
                                          waits=w, inc="pe" if j == nb - 1 else None)
                        ev_ev = P.emit("scalar", lambda e, pb=pb, sl=sl, b0=b0, nb=nb: e.copy(
                            idxT[sl][:, b0 * 128:(b0 + nb) * 128], pb[:, 0:nb * 128]),
                            waits=[ev_t], inc="se")
                        pst_free[(b0 // 8) % 2] = ev_ev
                    idxb_free[sl] = P.ev("pe")
                    ev_ms = P.emit("vector", lambda e, sl=sl, width=width: e.memset(
                        acc[sl][:, 0:width], 0.0), waits=[acc_free[sl]], inc="ve")
                    ev_k = ev_ms
                    for k in range(16):
                        ev_m = P.emit("vector", lambda e, sl=sl, k=k, width=width:
                                      e.tensor_scalar(maskb[:, 0:width],
                                                      idxT[sl][:, 0:width], float(k),
                                                      None, ALU.is_equal),
                                      waits=[ev_ev, mask_free] if k == 0 else [ev_k],
                                      inc="ve")
                        ev_k = P.emit("vector", lambda e, sl=sl, k=k, width=width:
                                      e.scalar_tensor_tensor(acc[sl][:, 0:width],
                                                             maskb[:, 0:width],
                                                             lut_col[:, k:k + 1],
                                                             acc[sl][:, 0:width],
                                                             ALU.mult, ALU.add),
                                      waits=[ev_m], inc="ve")
                    mask_free = ev_k
                    idxT_free[sl] = ev_k
                    ev_sp = P.dma("sync", dst_d[it], acc[sl][:, 0:width],
                                  waits=[ev_k], sem=f"dfsp{sl}")
                    acc_free[sl] = ev_sp
                    if debug and dbg_d is not None:
                        acc_free[sl] = P.dma("gpsimd", dbg_d[it], acc[sl][:, 0:width],
                                             waits=[ev_k], sem=f"dfdb{sl}")

            materialize(gidx_h, wgT_d, dbg.get("wgT"), NIT, E, lambda it: it * 128)
            materialize(uidx_h, wuT_d, None, NIT, E, lambda it: it * 128)
            materialize(didx_h, wdT_d, dbg.get("wdT"), NET, INTER, lambda it: it * 128)
            P.barrier()
            P.flush()

    if "f" in phases:
        with contextlib.ExitStack() as ph:
            ytT = ph.enter_context(nc.sbuf_tensor("ytT", [128, 8192], BF16))
            y_res = [ph.enter_context(nc.sbuf_tensor(f"yres{i}", [128, E], F32))
                     for i in range(4)]
            wg_i = [ph.enter_context(nc.sbuf_tensor(f"wgi{i}", [128, E], BF16))
                    for i in range(2)]
            wu_i = [ph.enter_context(nc.sbuf_tensor(f"wui{i}", [128, E], BF16))
                    for i in range(2)]
            wd_et = [ph.enter_context(nc.sbuf_tensor(f"wdet{i}", [128, INTER], BF16))
                     for i in range(2)]
            hT = ph.enter_context(nc.sbuf_tensor("hT", [128, NIT * 512], BF16))
            sil = ph.enter_context(nc.sbuf_tensor("sil", [128, 512], F32))
            dnT = ph.enter_context(nc.sbuf_tensor("dnT", [128, 8192], BF16))
            dn_tok = [ph.enter_context(nc.sbuf_tensor(f"dntok{i}", [128, E], BF16))
                      for i in range(4)]
            z2 = ph.enter_context(nc.sbuf_tensor("z2", [128, E], F32))
            z2sq = ph.enter_context(nc.sbuf_tensor("z2sq", [128, E], F32))
            outb = ph.enter_context(nc.sbuf_tensor("outb", [128, E], F32))
            fs1 = ph.enter_context(nc.sbuf_tensor("fs1", [128, 1], F32))
            fs2 = ph.enter_context(nc.sbuf_tensor("fs2", [128, 1], F32))
            fnm = ph.enter_context(nc.sbuf_tensor("fnm", [128, 1], F32))
            fmq = ph.enter_context(nc.sbuf_tensor("fmq", [128, 1], F32))
            fvr = ph.enter_context(nc.sbuf_tensor("fvr", [128, 1], F32))
            frs = ph.enter_context(nc.sbuf_tensor("frs", [128, 1], F32))
            psG_ = [ph.enter_context(nc.psum_tensor(f"psG{i}", [128, 512], F32))
                    for i in range(2)]
            psU_ = [ph.enter_context(nc.psum_tensor(f"psU{i}", [128, 512], F32))
                    for i in range(2)]
            psD = [ph.enter_context(nc.psum_tensor(f"psD{i}", [128, 512], F32))
                   for i in range(2)]
            pstf = ph.enter_context(nc.psum_tensor("pstff", [128, 1024], BF16))

            ytT_free = None
            yres_free = None
            wg_free = [None, None]
            wd_free = [None, None]
            psG_free = [None, None]
            psU_free = [None, None]
            psD_free = [None, None]
            pst_free = None
            sil_free = None
            hT_free = None
            dnT_free = None
            dntok_free = [None] * 4
            z2_free = None
            z2sq_free = None
            outb_free = None
            fstat_free = None

            def load_wgu(it, sl, waits):
                e1 = P.dma("sync", wg_i[sl][:, :], wgT_d[it], waits=waits, sem=f"dfg{sl}")
                e2 = P.dma("sync", wu_i[sl][:, :], wuT_d[it], waits=[], sem=f"dfg{sl}")
                return e2

            wg_ev = [None, None]
            wd_ev = [None, None]
            for tt in range(8):
                ev_yt = P.dma("sync", ytT[:, :].rearrange("p (et k) -> p et k", k=512),
                              yT_d[tt].rearrange("et p k -> p et k"),
                              waits=[ytT_free], sem="dfyt")
                ev_yr = None
                for ts in range(4):
                    ev_yr = P.dma("sync", y_res[ts][:, :],
                                  y_d[tt * 512 + ts * 128: tt * 512 + (ts + 1) * 128, :],
                                  waits=[yres_free] if ts == 0 else [], sem="dfyr")
                wg_ev[0] = load_wgu(0, 0, [wg_free[0]])
                for it in range(NIT):
                    sl = it % 2
                    if it + 1 < NIT:
                        wg_ev[1 - sl] = load_wgu(it + 1, 1 - sl, [wg_free[1 - sl]])
                    psG = psG_[sl]
                    psU = psU_[sl]
                    ev_g = None
                    ev_u = None
                    for ht in range(NET):
                        w = [wg_ev[sl], ev_yt, psG_free[sl]] if ht == 0 else []
                        ev_g = P.emit("tensor", lambda e, psG=psG, sl=sl, ht=ht: e.matmul(
                            psG[:, 0:512], wg_i[sl][:, ht * 128:(ht + 1) * 128],
                            ytT[:, ht * 512:(ht + 1) * 512],
                            start=(ht == 0), stop=(ht == NET - 1)),
                            waits=w, inc="pe" if ht == NET - 1 else None)
                    for ht in range(NET):
                        w = [psU_free[sl]] if ht == 0 else []
                        ev_u = P.emit("tensor", lambda e, psU=psU, sl=sl, ht=ht: e.matmul(
                            psU[:, 0:512], wu_i[sl][:, ht * 128:(ht + 1) * 128],
                            ytT[:, ht * 512:(ht + 1) * 512],
                            start=(ht == 0), stop=(ht == NET - 1)),
                            waits=w, inc="pe" if ht == NET - 1 else None)
                    ev_sl = P.emit("scalar", lambda e, psG=psG: e.activation(
                        sil[:, :], psG[:, 0:512], AF.Silu), waits=[ev_g, sil_free],
                        inc="se")
                    psG_free[sl] = ev_sl
                    ev_h = P.emit("vector", lambda e, psU=psU, it=it: e.tensor_tensor(
                        hT[:, it * 512:(it + 1) * 512], psU[:, 0:512], sil[:, :],
                        ALU.mult), waits=[ev_sl, ev_u, hT_free if it == 0 else None],
                        inc="ve")
                    psU_free[sl] = ev_h
                    sil_free = ev_h
                    wg_free[sl] = P.ev("pe")
                ev_h_all = P.ev("ve")
                wd_ev[0] = P.dma("sync", wd_et[0][:, :], wdT_d[0], waits=[wd_free[0]],
                                 sem="dfd0")
                for et in range(NET):
                    sl = et % 2
                    if et + 1 < NET:
                        wd_ev[1 - sl] = P.dma("sync", wd_et[1 - sl][:, :], wdT_d[et + 1],
                                              waits=[wd_free[1 - sl]], sem=f"dfd{1 - sl}")
                    ps = psD[et % 2]
                    ev_d = None
                    for it in range(NIT):
                        w = [wd_ev[sl], ev_h_all, psD_free[et % 2]] if it == 0 else []
                        ev_d = P.emit("tensor", lambda e, ps=ps, sl=sl, it=it: e.matmul(
                            ps[:, 0:512], wd_et[sl][:, it * 128:(it + 1) * 128],
                            hT[:, it * 512:(it + 1) * 512],
                            start=(it == 0), stop=(it == NIT - 1)),
                            waits=w, inc="pe" if it == NIT - 1 else None)
                    ev_de = P.emit("scalar", lambda e, ps=ps, et=et: e.copy(
                        dnT[:, et * 512:(et + 1) * 512], ps[:, 0:512]),
                        waits=[ev_d, dnT_free if et == 0 else None], inc="se")
                    psD_free[et % 2] = ev_de
                    wd_free[sl] = P.ev("pe")
                ev_dnT = P.ev("se")
                hT_free = P.ev("pe")
                for ts in range(4):
                    for bat in range(2):
                        ev_t = None
                        for j in range(8):
                            et = bat * 8 + j
                            w = [pst_free, ev_dnT, dntok_free[ts] if (bat == 0 and j == 0) else None] if j == 0 else []
                            ev_t = P.emit("tensor", lambda e, et=et, j=j, ts=ts: e.transpose(
                                pstf[:, j * 128:(j + 1) * 128],
                                dnT[:, et * 512 + ts * 128: et * 512 + (ts + 1) * 128],
                                identb[:, :]), waits=w, inc="pe" if j == 7 else None)
                        ev_te = P.emit("scalar", lambda e, ts=ts, bat=bat: e.copy(
                            dn_tok[ts][:, bat * 1024:(bat + 1) * 1024], pstf[:, :]),
                            waits=[ev_t], inc="se")
                        pst_free = ev_te
                    dntok_free[ts] = ev_te
                dnT_free = P.ev("pe")
                for ts in range(4):
                    ev_z2 = P.emit("vector", lambda e, ts=ts: e.tensor_tensor(
                        z2[:, :], dn_tok[ts][:, :], y_res[ts][:, :], ALU.add),
                        waits=[dntok_free[ts], ev_yr, z2_free], inc="ve")
                    ev_sq = P.emit("scalar", lambda e: e.activation(
                        z2sq[:, :], z2[:, :], AF.Square), waits=[ev_z2, z2sq_free],
                        inc="se")
                    ev_s1 = P.emit("vector", lambda e: e.reduce_sum(
                        fs1[:, :], z2[:, :], axis=AX.X), waits=[fstat_free], inc="ve")
                    ev_s2 = P.emit("vector", lambda e: e.reduce_sum(
                        fs2[:, :], z2sq[:, :], axis=AX.X), waits=[ev_sq], inc="ve")
                    ev_nm = P.emit("vector", lambda e: e.tensor_scalar(
                        fnm[:, :], fs1[:, :], -1.0 / E, None, ALU.mult),
                        waits=[ev_s1], inc="ve")
                    ev_mq = P.emit("vector", lambda e: e.tensor_tensor(
                        fmq[:, :], fnm[:, :], fnm[:, :], ALU.mult), waits=[ev_nm],
                        inc="ve")
                    ev_vr = P.emit("vector", lambda e: e.scalar_tensor_tensor(
                        fvr[:, :], fs2[:, :], 1.0 / E, fmq[:, :], ALU.mult,
                        ALU.subtract), waits=[ev_mq, ev_s2], inc="ve")
                    ev_lv = P.emit("scalar", lambda e: e.activation(
                        frs[:, :], fvr[:, :], AF.Ln, bias=LN_EPS), waits=[ev_vr],
                        inc="se")
                    ev_rs = P.emit("scalar", lambda e: e.activation(
                        frs[:, :], frs[:, :], AF.Exp, scale=-0.5), waits=[ev_lv],
                        inc="se")
                    ev_o0 = P.emit("vector", lambda e: e.tensor_scalar(
                        z2sq[:, :], z2[:, :], fnm[:, :], None, ALU.add),
                        waits=[ev_rs, ev_s2], inc="ve")
                    ev_o = P.emit("vector", lambda e: e.tensor_scalar(
                        outb[:, :], z2sq[:, :], frs[:, :], None, ALU.mult),
                        waits=[ev_o0, outb_free], inc="ve")
                    z2_free = ev_o
                    z2sq_free = ev_s2
                    fstat_free = ev_o
                    t0 = tt * 512 + ts * 128
                    outb_free = P.dma("sync", out_h[t0:t0 + 128, :], outb[:, :],
                                      waits=[ev_o], sem="dout")
                yres_free = P.ev("ve")
                ytT_free = P.ev("pe")
            P.barrier()
            P.flush()

    P.barrier()
    P.flush()
    stk.close()
    return nc


def _reorder_wq(wq):
    w = wq.reshape(E, HD, QH)
    nope = w[:, :, :NOPE].reshape(E, HD * NOPE)
    pe = w[:, :, NOPE:].reshape(E, HD, 2, 32)
    pe = pe.reshape(E, 8, 2, 2, 32)
    pe = pe.transpose(0, 1, 3, 2, 4).reshape(E, 8 * 128)
    return np.ascontiguousarray(np.concatenate([nope, pe], 1))


def _shard_inputs(inputs):
    x = np.ascontiguousarray(inputs["L0_attn_x"])
    consts = make_consts()
    gamma = np.ascontiguousarray(
        np.asarray(inputs["L0_attn_kv_norm_scale"], dtype=np.float32).reshape(4, 128).T)
    common = {
        "wq": _reorder_wq(np.asarray(inputs["L0_attn_wq"], dtype=np.float32)),
        "wkva": np.ascontiguousarray(inputs["L0_attn_wkv_a"], dtype=np.float32),
        "wkvb": np.ascontiguousarray(inputs["L0_attn_wkv_b"], dtype=np.float32),
        "wo": np.ascontiguousarray(inputs["L0_attn_wo"], dtype=np.float32),
        "gamma": gamma,
        "lut": np.asarray(inputs["L0_ffn_lut"], dtype=np.float32).reshape(1, 16),
        "gidx": np.ascontiguousarray(inputs["L0_ffn_gate_idx"], dtype=np.int32),
        "uidx": np.ascontiguousarray(inputs["L0_ffn_up_idx"], dtype=np.int32),
        "didx": np.ascontiguousarray(inputs["L0_ffn_down_idx"], dtype=np.int32),
        **consts,
    }
    in_maps = []
    for i in range(B):
        m = dict(common)
        m["x"] = np.ascontiguousarray(x[i], dtype=np.float32)
        in_maps.append(m)
    return in_maps


def kernel(**inputs):
    in_maps = _shard_inputs(inputs)
    nc = build_program(debug=DEBUG, phases=PHASES)
    res = run_bass_kernel_spmd(nc, in_maps, core_ids=list(range(B)))
    out = np.stack([np.asarray(r["out"], dtype=np.float32) for r in res.results], 0)
    return out
